# version 07
# baseline (speedup 1.0000x reference)
import math
import numpy as np
import ml_dtypes

import concourse.bass as bass
import concourse.mybir as mybir
from concourse.bass import IndirectOffsetOnAxis
from concourse.bass_utils import run_bass_kernel_spmd

F32 = mybir.dt.float32
BF16 = mybir.dt.bfloat16
U8 = mybir.dt.uint8
U32 = mybir.dt.uint32
ALU = mybir.AluOpType
AF = mybir.ActivationFunctionType

IDX_ZERO = False
DBG_H = False
GLIMIT = 10**9
PLAIN_GATHER = False

B, S, D, NH, HD = 4, 2048, 4096, 32, 128
INTER, SPR, NB, RANK = 14336, 256, 56, 16
SCALE = HD ** -0.5
NTOK = 1024
NKT = 16
NFO = 32
NTT = 8
LN_EPS = 1e-5
MASK_NEG = -50000.0
KILL = -1e9

_start = 2.0 ** (-(2.0 ** -(math.log2(NH) - 3)))
SLOPES = [float(np.float32(_start ** (i + 1))) for i in range(NH)]

SLOTSETS = {0: [0, 1, 2, 3, 8, 9, 10, 11], 1: list(range(16))}
MASKED = {0: {0: 0, 1: 1, 2: 2, 3: 3}, 1: {4: 0, 5: 1, 6: 2, 7: 3}}


def n_groups(h):
    sh = SLOPES[h]
    if sh * 256.0 <= 66.0:
        return 1
    if sh * 128.0 <= 66.0:
        return 2
    return 4


def _core_geom(c):
    half = c % 2
    if half == 0:
        perm = np.concatenate([np.arange(0, 512), np.arange(1536, 2048),
                               np.arange(512, 1536)])
        q0s = (0, 1536)
    else:
        perm = np.concatenate([np.arange(512, 1536), np.arange(0, 512),
                               np.arange(1536, 2048)])
        q0s = (512, 1024)
    tile_orig = [int(perm[s * 128]) for s in range(NKT)]
    return perm, q0s, tile_orig


_BIAS_COLS = []
COLOF = {}
for _h in range(NH):
    for _qc in (0, 1):
        for _si in range(len(SLOTSETS[_qc])):
            for _g in range(n_groups(_h)):
                COLOF[(_h, _qc, _si, _g)] = len(_BIAS_COLS)
                _BIAS_COLS.append((_h, _qc, _si, _g))
NCOL = len(_BIAS_COLS)


def _host_biasmat(c):
    _, q0s, torig = _core_geom(c)
    bm = np.zeros((128, NCOL), dtype=np.float64)
    p = np.arange(128, dtype=np.float64)
    for col, (h, qc, si, g) in enumerate(_BIAS_COLS):
        s = SLOTSETS[qc][si]
        k0, q0 = torig[s], q0s[qc]
        if k0 > q0 + 511:
            bm[:, col] = KILL
        else:
            w = 512 // n_groups(h)
            qt = q0 + w * g + w // 2
            bm[:, col] = SLOPES[h] * (k0 + p - qt)
    return bm.astype(np.float32)


def _host_masks():
    m = np.zeros((4, 128, 512), dtype=np.float32)
    c = np.arange(128)[:, None]
    j = np.arange(512)[None, :]
    for d in range(4):
        m[d] = np.where(c > j - 128 * d, MASK_NEG, 0.0)
    return m.astype(ml_dtypes.bfloat16)


class Ctx:
    SEMS = ("LD", "LA", "LB", "LC", "SD", "GA", "PE", "VE", "SC")

    def __init__(self):
        self.emit = False
        self.cnt = {s: 0 for s in self.SEMS}
        self.ev = {}
        self.sems = {}

    def inc(self, inst, sem, name, n):
        if self.emit:
            assert inst is not None, (sem, name)
            inst.then_inc(self.sems[sem], n)
        else:
            self.cnt[sem] += n
            self.ev[(sem, name)] = self.cnt[sem]

    def wait(self, eng, sem, name):
        if self.emit:
            eng.wait_ge(self.sems[sem], self.ev[(sem, name)])

    def E(self, thunk):
        return thunk() if self.emit else None

    def mark(self, sem, name):
        if not self.emit:
            self.ev[(sem, name)] = self.cnt[sem]


def build_program(debug=False, stage=99, fsub=99):
    nc = bass.Bass("TRN2", target_bir_lowering=False)

    d_xfmt = nc.dram_tensor("xfmt", [NFO, 128, S], BF16, kind="ExternalInput").ap()
    d_xf32 = nc.dram_tensor("xf32", [NFO, 128, NTOK], F32, kind="ExternalInput").ap()
    d_wq = nc.dram_tensor("wq_t", [NFO, 128, D], BF16, kind="ExternalInput").ap()
    d_wo = nc.dram_tensor("wo_t", [NFO, 128, D], BF16, kind="ExternalInput").ap()
    d_wkv = nc.dram_tensor("wkv_t", [128, 8192], BF16, kind="ExternalInput").ap()
    d_c1 = nc.dram_tensor("c1t", [128, 512], BF16, kind="ExternalInput").ap()
    d_c2 = nc.dram_tensor("c2t", [16, INTER], BF16, kind="ExternalInput").ap()
    d_w12 = nc.dram_tensor("w1w2", [INTER, 8192], BF16, kind="ExternalInput").ap()
    d_bias = nc.dram_tensor("biasm", [128, NCOL], F32, kind="ExternalInput").ap()
    d_mask = nc.dram_tensor("mask4", [4, 128, 512], BF16, kind="ExternalInput").ap()
    d_u1 = nc.dram_tensor("u1", [128, 128], BF16, kind="ExternalInput").ap()
    d_id = nc.dram_tensor("ident", [128, 128], BF16, kind="ExternalInput").ap()
    d_ones = nc.dram_tensor("onesb", [128, 128], BF16, kind="ExternalInput").ap()
    d_iota = nc.dram_tensor("iota", [128, 256], BF16, kind="ExternalInput").ap()
    d_x1s = nc.dram_tensor("x1s", [NFO, 128, NTOK], F32, kind="Internal").ap()
    d_y = nc.dram_tensor("y", [NTT, 128, D], F32, kind="ExternalOutput").ap()
    dbg = {}
    if debug:
        for nm, shp, dt in [("dbg_y1", [128, NFO * NTOK], BF16),
                            ("dbg_q", [128, NH * NTOK], BF16),
                            ("dbg_kf", [128, NKT * 128], BF16),
                            ("dbg_vt", [128, NKT * 128], BF16),
                            ("dbg_ao", [128, NH * NTOK], BF16),
                            ("dbg_x1", [128, NFO * NTOK], BF16),
                            ("dbg_u", [16, NTOK], BF16),
                            ("dbg_y2", [128, NTT * D], BF16),
                            ("dbg_h", [NTT, 128, 56], F32)]:
            dbg[nm] = nc.dram_tensor(nm, shp, dt, kind="ExternalOutput").ap()

    ctx = Ctx()

    from contextlib import ExitStack
    with ExitStack() as _st:
        def _sb(name, shape, dt):
            return _st.enter_context(nc.sbuf_tensor(name, shape, dt))

        def _ps(name, shape, dt):
            return _st.enter_context(nc.psum_tensor(name, shape, dt))

        arA = _sb("arA", [128, 65536], U8)
        arB = _sb("arB", [128, 32768], BF16)
        arC = _sb("arC", [128, 49152], U8)
        sb_bias = _sb("sb_bias", [128, NCOL], F32)
        sb_u1 = _sb("sb_u1", [128, 128], BF16)
        sb_id = _sb("sb_id", [128, 128], BF16)
        sb_ones = _sb("sb_ones", [128, 128], BF16)
        sb_kf = _sb("sb_kf", [128, NKT * 128], BF16)
        sb_vt = _sb("sb_vt", [128, NKT * 128], BF16)
        sb_c1 = _sb("sb_c1", [128, 512], BF16)
        sb_pt0 = _sb("sb_pt0", [128, 512], BF16)
        sb_pt1 = _sb("sb_pt1", [128, 512], BF16)
        sb_mbc = _sb("sb_mbc", [128, 2048], BF16)
        sb_sq = _sb("sb_sq", [128, 1024], BF16)
        sb_rrow = _sb("sb_rrow", [1, 512], F32)
        sb_rbf = _sb("sb_rbf", [1, 2048], BF16)
        sb_u = _sb("sb_u", [16, 1024], BF16)
        sb_c2b0 = _sb("sb_c2b0", [16, 256], BF16)
        sb_c2b1 = _sb("sb_c2b1", [16, 256], BF16)
        sb_m8 = _sb("sb_m8", [128, 8], F32)
        sb_i8 = _sb("sb_i8", [128, 16], U32)
        sb_h = _sb("sb_h", [128, 64], F32)
        sb_hr = _sb("sb_hr", [128, 64], F32)
        pb0 = _ps("pb0", [128, 512], F32)
        pb1 = _ps("pb1", [128, 512], F32)
        pb2 = _ps("pb2", [128, 512], F32)
        pb3 = _ps("pb3", [128, 512], F32)
        pb4 = _ps("pb4", [128, 512], F32)
        pb5 = _ps("pb5", [128, 512], F32)
        pt0 = _ps("pt0", [128, 128], BF16)
        pt1 = _ps("pt1", [128, 128], BF16)
        sLD = _st.enter_context(nc.semaphore("sLD"))
        sLA = _st.enter_context(nc.semaphore("sLA"))
        sLB = _st.enter_context(nc.semaphore("sLB"))
        sLC = _st.enter_context(nc.semaphore("sLC"))
        sSD = _st.enter_context(nc.semaphore("sSD"))
        sGA = _st.enter_context(nc.semaphore("sGA"))
        sPE = _st.enter_context(nc.semaphore("sPE"))
        sVE = _st.enter_context(nc.semaphore("sVE"))
        sSC = _st.enter_context(nc.semaphore("sSC"))
        block = _st.enter_context(nc.Block())
        ctx.sems = {"LD": sLD, "LA": sLA, "LB": sLB, "LC": sLC,
                    "SD": sSD, "GA": sGA,
                    "PE": sPE, "VE": sVE, "SC": sSC}
        pb = [pb0, pb1, pb2, pb3, pb4, pb5]
        pt = [pt0, pt1]

        vA = arA.ap().bitcast(BF16)
        y1fm = vA
        attnout = vA
        y2tm = vA
        vB = arB.ap()
        qT = vB
        x1bf = vB
        wbuf = [vB[:, 0:8192], vB[:, 8192:16384]]
        x1c = arB.ap().bitcast(F32)[:, 8192:12288]
        otile = arB.ap().bitcast(F32)[:, 12288:16384]
        vC16 = arC.ap().bitcast(BF16)
        xchunk = vC16[:, 0:16384]
        wkv_sb = vC16[:, 16384:24576]
        wqbuf = [vC16[:, 0:4096], vC16[:, 4096:8192]]
        wobuf = wqbuf
        acc_bf = [vC16[:, 0:4096], vC16[:, 4096:8192]]
        w2sc = [vC16[:, 16384:20480], vC16[:, 12288:16384]]
        prod = vC16[:, 20480:24576]
        sb_ykv = vB[:, 0:1024]
        sb_nrm_kv = vB[:, 1024:2048]
        sb_vfm = vB[:, 2048:2560]
        vBf32 = arB.ap().bitcast(F32)
        rows_kv = vBf32[0:1, 1280:3328]
        vCf32 = arC.ap().bitcast(F32)
        sb_x1o = vCf32[:, 4096:5120]
        sb_x1f = vCf32[:, 5120:6144]
        rows_ln2 = vCf32[0:1, 6144:8192]
        y2r = [vC16[:, 8192:9216], vC16[:, 9216:10240]]
        sb_nrm_t = vC16[:, 10240:11264]
        sb_mask = vC16[:, 14336:16384]
        sb_rbc = vC16[:, 6144:6656]

        P_tiles = [sb_pt0, sb_pt1]
        c2b = [sb_c2b0, sb_c2b1]

        def q_off(h, qc):
            return h * 1024 + qc * 512

        def stream_sync(eng):
            E, inc, wait = ctx.E, ctx.inc, ctx.wait
            for nm, dst, src in [("c_bias", sb_bias.ap(), d_bias),
                                 ("c_u1", sb_u1.ap(), d_u1),
                                 ("c_id", sb_id.ap(), d_id),
                                 ("c_ones", sb_ones.ap(), d_ones),
                                 ("c_c1", sb_c1.ap(), d_c1),
                                 ("c_wkv", wkv_sb, d_wkv)]:
                inc(E(lambda dst=dst, src=src: eng.dma_start(out=dst, in_=src)),
                    "LC", nm, 16)
            ctx.mark("LC", "b_pre")
            wait(eng, "LC", "b_pre")
            for tc in range(4):
                for kt in range(NFO):
                    if tc >= 1:
                        wait(eng, "VE", f"y_{tc-1}_{kt}")
                    inc(E(lambda tc=tc, kt=kt: eng.dma_start(
                        out=xchunk[:, kt * 512:(kt + 1) * 512],
                        in_=d_xfmt[kt][:, tc * 512:(tc + 1) * 512])),
                        "LC", f"xc_{tc}_{kt}", 16)
                ctx.mark("LC", f"b_xc{tc}")
                wait(eng, "LC", f"b_xc{tc}")
            if stage < 1:
                return
            wait(eng, "VE", "y_3_31")
            for dd in range(4):
                inc(E(lambda dd=dd: eng.dma_start(
                    out=sb_mask[:, dd * 512:(dd + 1) * 512],
                    in_=d_mask[dd])), "LC", f"c_mask{dd}", 16)
            ctx.mark("LC", "b_mask")
            wait(eng, "LC", "b_mask")
            for m in range(NFO):
                if m >= 2:
                    wait(eng, "PE", f"q_{m-2}")
                inc(E(lambda m=m: eng.dma_start(out=wqbuf[m % 2],
                                                in_=d_wq[m])),
                    "LA" if m % 2 == 0 else "LB", f"wq_{m}", 16)
            if stage < 3:
                return
            for fo in range(NFO):
                if fo == 0:
                    wait(eng, "VE", "ao_31_1")
                if fo >= 2:
                    wait(eng, "PE", f"wo_{fo-2}")
                inc(E(lambda fo=fo: eng.dma_start(out=wobuf[fo % 2],
                                                  in_=d_wo[fo])),
                    "LA" if fo % 2 == 0 else "LB", f"wwo_{fo}", 16)
                if fo >= 1:
                    wait(eng, "VE", f"x1_{fo-1}")
                inc(E(lambda fo=fo: eng.dma_start(out=sb_x1f,
                                                  in_=d_xf32[fo])),
                    "LA" if fo % 2 == 0 else "LB", f"xf_{fo}", 16)
            if stage < 5:
                return
            wait(eng, "PE", f"av_31_1_{len(SLOTSETS[1])-1}")
            inc(E(lambda: eng.dma_start(out=sb_pt0.ap()[:, 256:512],
                                        in_=d_iota)), "LC", "iota", 16)
            ctx.mark("LC", "b_iota")
            wait(eng, "LC", "b_iota")
            def sy_c2(g2):
                t2, j2 = g2 // NB, g2 % NB
                if g2 >= 2:
                    p2, q2 = (g2 - 2) // NB, (g2 - 2) % NB
                    wait(eng, "PE", f"int_{p2}_{q2}")
                inc(E(lambda t2=t2, j2=j2: eng.dma_start(
                    out=c2b[j2 % 2].ap(),
                    in_=d_c2[:, j2 * 256:(j2 + 1) * 256])),
                    "LA" if j2 % 2 == 0 else "LB", f"c2_{t2}_{j2}", 16)

            def sy_x1c(tt):
                if tt == 0:
                    wait(eng, "VE", "ue")
                    wait(eng, "SD", f"x1s_{NFO-1}")
                else:
                    wait(eng, "VE", f"fa_{tt-1}_31")
                for fo in range(NFO):
                    inc(E(lambda tt=tt, fo=fo: eng.dma_start(
                        out=x1c[:, fo * 128:(fo + 1) * 128],
                        in_=d_x1s[fo][:, tt * 128:(tt + 1) * 128])),
                        "LC", f"x1c_{tt}_{fo}", 16)
                ctx.mark("LC", f"b_x1c{tt}")
                wait(eng, "LC", f"b_x1c{tt}")

            for g in range(NTT * NB):
                sy_c2(g)
                if g % NB == 2:
                    sy_x1c(g // NB)
            if NB <= 2:
                raise AssertionError

        def stream_gpsimd(eng):
            E, inc, wait = ctx.E, ctx.inc, ctx.wait
            if debug:
                wait(eng, "VE", "y_1_31")
                inc(E(lambda: eng.dma_start(out=dbg["dbg_y1"], in_=y1fm)),
                    "GA", "dbg_y1", 16)
                wait(eng, "VE", "kve_3")
                wait(eng, "SC", "vt_3_3")
                inc(E(lambda: eng.dma_start(out=dbg["dbg_kf"], in_=sb_kf.ap())),
                    "GA", "dbg_kf", 16)
                inc(E(lambda: eng.dma_start(out=dbg["dbg_vt"], in_=sb_vt.ap())),
                    "GA", "dbg_vt", 16)
                wait(eng, "VE", "qe_31")
                inc(E(lambda: eng.dma_start(out=dbg["dbg_q"], in_=qT)),
                    "GA", "dbg_q", 16)
                wait(eng, "VE", "ao_31_1")
                inc(E(lambda: eng.dma_start(out=dbg["dbg_ao"], in_=attnout)),
                    "GA", "dbg_ao", 16)
                wait(eng, "VE", "x1_31")
                inc(E(lambda: eng.dma_start(out=dbg["dbg_x1"], in_=x1bf)),
                    "GA", "dbg_x1", 16)
                wait(eng, "VE", "ue")
                inc(E(lambda: eng.dma_start(out=dbg["dbg_u"], in_=sb_u.ap())),
                    "GA", "dbg_u", 16)
                wait(eng, "SC", "tre_31_7")
                inc(E(lambda: eng.dma_start(out=dbg["dbg_y2"], in_=y2tm)),
                    "GA", "dbg_y2", 16)
            if stage < 5:
                return
            NG_ALL = NTT * NB
            for g in range(NG_ALL):
                tt, j = g // NB, g % NB
                wait(eng, "VE", f"am_{tt}_{j}")
                if g >= 2:
                    pt_, pj = (g - 2) // NB, (g - 2) % NB
                    wait(eng, "VE", f"o_{pt_}_{pj}")
                inc(E(lambda tt=tt, j=j: eng.indirect_dma_start(
                    wbuf[j % 2],
                    None,
                    d_w12,
                    IndirectOffsetOnAxis(
                        ap=sb_i8.ap()[:, g % 4 if False else (tt * NB + j) % 4:
                                      (tt * NB + j) % 4 + 1],
                        axis=0),
                    element_offset=j * 256 * 8192)),
                    "GA", f"g_{tt}_{j}", 16)

        def stream_tensor(eng):
            E, inc, wait = ctx.E, ctx.inc, ctx.wait
            wait(eng, "LC", "b_pre")
            for tc in range(4):
                wait(eng, "LC", f"b_xc{tc}")
                for kt in range(NFO):
                    E(lambda tc=tc, kt=kt: eng.matmul(
                        pb[0][0:1, :], sb_ones.ap()[:, 0:1],
                        xchunk[:, kt * 512:(kt + 1) * 512],
                        start=(kt == 0), stop=(kt == NFO - 1)))
                    wait(eng, "SC", f"sq_{tc}_{kt}")
                    inc(E(lambda kt=kt: eng.matmul(
                        pb[1][0:1, :], sb_ones.ap()[:, 0:1],
                        sb_sq.ap()[:, (kt % 2) * 512:(kt % 2) * 512 + 512],
                        start=(kt == 0), stop=(kt == NFO - 1))),
                        "PE", f"sx_{tc}_{kt}", 1)
                wait(eng, "VE", f"rows_{tc}")
                E(lambda: eng.matmul(pb[2][:, :], sb_ones.ap()[0:1, 0:128],
                                     sb_rbf.ap()[0:1, 0:512],
                                     start=True, stop=True))
                inc(E(lambda: eng.matmul(pb[3][:, :], sb_ones.ap()[0:1, 0:128],
                                         sb_rbf.ap()[0:1, 512:1024],
                                         start=True, stop=True)),
                    "PE", f"bc_{tc}", 1)
                for kt in range(NFO):
                    wait(eng, "VE", f"y_{tc}_{kt}")
                    if tc < 2:
                        dst = y1fm[:, kt * 1024 + tc * 512:
                                   kt * 1024 + tc * 512 + 512]
                    else:
                        dst = sb_ykv[:, (kt % 2) * 512:(kt % 2) * 512 + 512]
                    E(lambda kt=kt, dst=dst: eng.matmul(
                        pb[4][:, :], wkv_sb[:, kt * 256:kt * 256 + 128], dst,
                        start=(kt == 0), stop=(kt == NFO - 1)))
                    inc(E(lambda kt=kt, dst=dst: eng.matmul(
                        pb[5][:, :], wkv_sb[:, kt * 256 + 128:kt * 256 + 256],
                        dst, start=(kt == 0), stop=(kt == NFO - 1))),
                        "PE", f"kv_{tc}_{kt}", 1)
                wait(eng, "VE", f"kve_{tc}")
                for i in range(4):
                    if i >= 2:
                        wait(eng, "SC", f"vt_{tc}_{i-2}")
                    elif tc >= 1:
                        wait(eng, "SC", f"vt_{tc-1}_{2+i}")
                    inc(E(lambda i=i: eng.transpose(
                        pt[i % 2][:, :], sb_vfm[:, i * 128:(i + 1) * 128],
                        sb_id.ap())), "PE", f"vt_{tc}_{i}", 1)
            if stage < 1:
                return
            for m in range(NFO):
                wait(eng, "LA" if m % 2 == 0 else "LB", f"wq_{m}")
                if m >= 2:
                    wait(eng, "VE", f"qe_{m-2}")
                for kt in range(NFO):
                    E(lambda m=m, kt=kt: eng.matmul(
                        pb[(m % 2) * 2][:, :],
                        wqbuf[m % 2][:, kt * 128:(kt + 1) * 128],
                        y1fm[:, kt * 1024:kt * 1024 + 512],
                        start=(kt == 0), stop=(kt == NFO - 1)))
                    i2 = E(lambda m=m, kt=kt: eng.matmul(
                        pb[(m % 2) * 2 + 1][:, :],
                        wqbuf[m % 2][:, kt * 128:(kt + 1) * 128],
                        y1fm[:, kt * 1024 + 512:kt * 1024 + 1024],
                        start=(kt == 0), stop=(kt == NFO - 1)))
                    if kt == NFO - 1:
                        inc(i2, "PE", f"q_{m}", 1)
            if stage < 2:
                return
            wait(eng, "LC", "b_mask")
            for qc in (0, 1):
                slots = SLOTSETS[qc]
                nsl = len(slots)
                for h in range(NH):
                    wait(eng, "VE", f"qe_{NFO-1}" if (qc == 0 and h == 0)
                         else f"qe_{h}")
                    if not (qc == 0 and h == 0):
                        ph, pqc = (h - 1, qc) if h > 0 else (NH - 1, 0)
                        wait(eng, "VE", f"ao_{ph}_{pqc}")
                    for si in range(nsl + 1):
                        if si < nsl:
                            s = slots[si]
                            if si >= 2:
                                wait(eng, "SC", f"exp_{h}_{qc}_{si-2}")
                            mk = MASKED[qc].get(s)
                            psA = pb[si % 2]
                            i1 = E(lambda s=s, h=h, qc=qc, psA=psA, mk=mk:
                                   eng.matmul(
                                       psA[:, :],
                                       sb_kf.ap()[:, s * 128:(s + 1) * 128],
                                       qT[:, q_off(h, qc):q_off(h, qc) + 512],
                                       start=True, stop=(mk is None)))
                            if mk is not None:
                                i1 = E(lambda mk=mk, psA=psA: eng.matmul(
                                    psA[:, :], sb_u1.ap(),
                                    sb_mask[:, mk * 512:(mk + 1) * 512],
                                    start=False, stop=True))
                            inc(i1, "PE", f"sc_{h}_{qc}_{si}", 1)
                        if si >= 1:
                            pi = si - 1
                            s_prev = slots[pi]
                            wait(eng, "SC", f"exp_{h}_{qc}_{pi}")
                            E(lambda s_prev=s_prev, pi=pi, nsl=nsl: eng.matmul(
                                pb[2][:, :],
                                sb_vt.ap()[:, s_prev * 128:(s_prev + 1) * 128],
                                P_tiles[pi % 2].ap()[:, :],
                                start=(pi == 0), stop=(pi == nsl - 1)))
                            inc(E(lambda pi=pi, nsl=nsl: eng.matmul(
                                pb[3][0:1, :], sb_ones.ap()[:, 0:1],
                                P_tiles[pi % 2].ap()[:, :],
                                start=(pi == 0), stop=(pi == nsl - 1))),
                                "PE", f"av_{h}_{qc}_{pi}", 1)
                    wait(eng, "VE", f"r_{h}_{qc}")
                    if not (qc == 0 and h == 0):
                        ph, pqc = (h - 1, qc) if h > 0 else (NH - 1, 0)
                        wait(eng, "SC", f"rbe_{ph}_{pqc}")
                    inc(E(lambda: eng.matmul(pb[4][:, :],
                                             sb_ones.ap()[0:1, 0:128],
                                             sb_rbf.ap()[0:1, 0:512],
                                             start=True, stop=True)),
                        "PE", f"rb_{h}_{qc}", 1)
            if stage < 3:
                return
            for fo in range(NFO):
                wait(eng, "LA" if fo % 2 == 0 else "LB", f"xf_{fo}")
                if fo == 0:
                    wait(eng, "SC", f"exp_31_1_{len(SLOTSETS[1])-1}")
                    wait(eng, "SC", "rbe_31_1")
                if fo == 1:
                    wait(eng, "VE", "ao_31_1")
                if fo >= 2:
                    wait(eng, "VE", f"x1_{fo-2}")
                for kt in range(NFO):
                    E(lambda fo=fo, kt=kt: eng.matmul(
                        pb[(fo % 2) * 2][:, :],
                        wobuf[fo % 2][:, kt * 128:(kt + 1) * 128],
                        attnout[:, kt * 1024:kt * 1024 + 512],
                        start=(kt == 0), stop=(kt == NFO - 1)))
                    i2 = E(lambda fo=fo, kt=kt: eng.matmul(
                        pb[(fo % 2) * 2 + 1][:, :],
                        wobuf[fo % 2][:, kt * 128:(kt + 1) * 128],
                        attnout[:, kt * 1024 + 512:kt * 1024 + 1024],
                        start=(kt == 0), stop=(kt == NFO - 1)))
                    if kt == NFO - 1:
                        inc(i2, "PE", f"wo_{fo}", 1)
                wait(eng, "VE", f"x1_{fo}")
                wait(eng, "SC", f"sq2_{fo}")
                E(lambda fo=fo: eng.matmul(
                    pb[4][0:1, :], sb_ones.ap()[:, 0:1],
                    x1bf[:, fo * 1024:fo * 1024 + 512],
                    start=(fo == 0), stop=False, skip_group_check=True))
                E(lambda fo=fo: eng.matmul(
                    pb[4][32:33, :], sb_ones.ap()[:, 0:1],
                    x1bf[:, fo * 1024 + 512:fo * 1024 + 1024],
                    start=False, stop=False, skip_group_check=True))
                E(lambda fo=fo: eng.matmul(
                    pb[4][64:65, :], sb_ones.ap()[:, 0:1],
                    sb_sq.ap()[:, 0:512],
                    start=False, stop=(fo == NFO - 1), skip_group_check=True))
                inc(E(lambda fo=fo: eng.matmul(
                    pb[5][0:1, :], sb_ones.ap()[:, 0:1],
                    sb_sq.ap()[:, 512:1024],
                    start=(fo == 0), stop=(fo == NFO - 1))),
                    "PE", f"st2_{fo}", 1)
            wait(eng, "VE", "rows2")
            E(lambda: eng.matmul(pb[0][:, :], sb_ones.ap()[0:1, 0:128],
                                 sb_rbf.ap()[0:1, 0:512], start=True, stop=True))
            E(lambda: eng.matmul(pb[1][:, :], sb_ones.ap()[0:1, 0:128],
                                 sb_rbf.ap()[0:1, 512:1024],
                                 start=True, stop=True))
            E(lambda: eng.matmul(pb[2][:, :], sb_ones.ap()[0:1, 0:128],
                                 sb_rbf.ap()[0:1, 1024:1536],
                                 start=True, stop=True))
            inc(E(lambda: eng.matmul(pb[3][:, :], sb_ones.ap()[0:1, 0:128],
                                     sb_rbf.ap()[0:1, 1536:2048],
                                     start=True, stop=True)),
                "PE", "bc2", 1)
            if stage < 4:
                return
            for fo in range(NFO):
                wait(eng, "VE", f"y2_{fo}")
                E(lambda fo=fo: eng.matmul(
                    pb[4][0:16, :], sb_c1.ap()[:, fo * 16:(fo + 1) * 16],
                    y2r[fo % 2][:, 0:512],
                    start=(fo == 0), stop=(fo == NFO - 1)))
                i2 = E(lambda fo=fo: eng.matmul(
                    pb[5][0:16, :], sb_c1.ap()[:, fo * 16:(fo + 1) * 16],
                    y2r[fo % 2][:, 512:1024],
                    start=(fo == 0), stop=(fo == NFO - 1)))
                if fo == NFO - 1:
                    inc(i2, "PE", "u", 1)
                for i in range(8):
                    gidx = fo * 8 + i
                    if gidx >= 2:
                        pp = gidx - 2
                        wait(eng, "SC", f"tre_{pp // 8}_{pp % 8}")
                    inc(E(lambda fo=fo, i=i: eng.transpose(
                        pt[i % 2][:, :],
                        y2r[fo % 2][:, i * 128:(i + 1) * 128],
                        sb_id.ap())), "PE", f"tr_{fo}_{i}", 1)
            if stage < 5:
                return
            def pe_int(g2):
                t2, j2 = g2 // NB, g2 % NB
                wait(eng, "LA" if j2 % 2 == 0 else "LB", f"c2_{t2}_{j2}")
                if g2 == 0:
                    wait(eng, "VE", "ue")
                if g2 >= 2:
                    p2, q2 = (g2 - 2) // NB, (g2 - 2) % NB
                    wait(eng, "SC", f"ie_{p2}_{q2}")
                inc(E(lambda t2=t2, j2=j2: eng.matmul(
                    pb[2 + j2 % 2][:, 0:256],
                    sb_u.ap()[0:16, t2 * 128:(t2 + 1) * 128],
                    c2b[j2 % 2].ap()[:, :],
                    start=True, stop=True)),
                    "PE", f"int_{t2}_{j2}", 1)

            def pe_ft(tt):
                wait(eng, "VE", f"o_{tt}_{NB-1}")
                for fo in range(NFO):
                    gidx = tt * NFO + fo
                    if gidx >= 2:
                        pp = gidx - 2
                        wait(eng, "VE", f"fa_{pp // NFO}_{pp % NFO}")
                    inc(E(lambda fo=fo: eng.transpose(
                        pt[fo % 2][:, :], acc_bf[0][:, fo * 128:(fo + 1) * 128],
                        sb_id.ap())), "PE", f"ft_{tt}_{fo}", 1)

            for g in range(NTT * NB):
                pe_int(g)
                if g % NB == 2 and g >= NB:
                    pe_ft(g // NB - 1)
            pe_ft(NTT - 1)

        def stream_scalar(eng):
            E, inc, wait = ctx.E, ctx.inc, ctx.wait
            pass
            for tc in range(4):
                wait(eng, "LC", f"b_xc{tc}")
                for kt in range(NFO):
                    if kt >= 2:
                        wait(eng, "PE", f"sx_{tc}_{kt-2}")
                    elif tc >= 1:
                        wait(eng, "PE", f"sx_{tc-1}_{NFO-2+kt}")
                    inc(E(lambda kt=kt, tc=tc: eng.activation(
                        sb_sq.ap()[:, (kt % 2) * 512:(kt % 2) * 512 + 512],
                        xchunk[:, kt * 512:(kt + 1) * 512], AF.Square)),
                        "SC", f"sq_{tc}_{kt}", 1)
                wait(eng, "VE", f"var_{tc}")
                inc(E(lambda: eng.activation(
                    rows_kv[0:1, 1024:1536], rows_kv[0:1, 1536:2048],
                    AF.Sqrt)), "SC", f"srt_{tc}", 1)
                for i in range(4):
                    wait(eng, "PE", f"vt_{tc}_{i}")
                    inc(E(lambda tc=tc, i=i: eng.copy(
                        sb_vt.ap()[:, (tc * 4 + i) * 128:(tc * 4 + i + 1) * 128],
                        pt[i % 2][:, :])), "SC", f"vt_{tc}_{i}", 1)
            if stage < 2:
                if ctx.emit:
                    eng.wait_ge(ctx.sems["SD"], 0)
                return
            wait(eng, "LC", "b_pre")
            for qc in (0, 1):
                slots = SLOTSETS[qc]
                for h in range(NH):
                    ng = n_groups(h)
                    w = 512 // ng
                    for si in range(len(slots)):
                        wait(eng, "PE", f"sc_{h}_{qc}_{si}")
                        if si >= 2:
                            wait(eng, "PE", f"av_{h}_{qc}_{si-2}")
                        psA = pb[si % 2]
                        for g in range(ng):
                            col = COLOF[(h, qc, si, g)]
                            i1 = E(lambda si=si, g=g, w=w, col=col, psA=psA:
                                   eng.activation(
                                       P_tiles[si % 2].ap()[:, g * w:(g + 1) * w],
                                       psA[:, g * w:(g + 1) * w],
                                       AF.Exp,
                                       bias=sb_bias.ap()[:, col:col + 1],
                                       scale=SCALE))
                            if g == ng - 1:
                                inc(i1, "SC", f"exp_{h}_{qc}_{si}", 1)
                    wait(eng, "PE", f"rb_{h}_{qc}")
                    inc(E(lambda: eng.copy(sb_rbc[:, :], pb[4][:, :])),
                        "SC", f"rbe_{h}_{qc}", 1)
            if stage < 3:
                return
            for fo in range(NFO):
                wait(eng, "VE", f"x1_{fo}")
                inc(E(lambda fo=fo: eng.dma_start(out=d_x1s[fo],
                                                  in_=sb_x1o)),
                    "SD", f"x1s_{fo}", 16)
                if fo >= 1:
                    wait(eng, "PE", f"st2_{fo-1}")
                inc(E(lambda fo=fo: eng.activation(
                    sb_sq.ap()[:, :], x1bf[:, fo * 1024:(fo + 1) * 1024],
                    AF.Square)), "SC", f"sq2_{fo}", 1)
            wait(eng, "VE", "var2")
            E(lambda: eng.activation(
                sb_x1o[0:1, 0:512], rows_ln2[0:1, 1024:1536],
                AF.Sqrt))
            inc(E(lambda: eng.activation(
                sb_x1o[0:1, 512:1024], rows_ln2[0:1, 1536:2048],
                AF.Sqrt)), "SC", "srt2", 1)
            wait(eng, "PE", "bc2")
            E(lambda: eng.copy(sb_mbc.ap()[:, 0:512], pb[0][:, :]))
            E(lambda: eng.copy(sb_mbc.ap()[:, 512:1024], pb[1][:, :]))
            E(lambda: eng.copy(sb_mbc.ap()[:, 1024:1536], pb[2][:, :]))
            inc(E(lambda: eng.copy(sb_mbc.ap()[:, 1536:2048], pb[3][:, :])),
                "SC", "bc2e", 1)
            if stage < 4:
                return
            for fo in range(NFO):
                for i in range(8):
                    wait(eng, "PE", f"tr_{fo}_{i}")
                    inc(E(lambda fo=fo, i=i: eng.copy(
                        y2tm[:, i * 4096 + fo * 128: i * 4096 + (fo + 1) * 128],
                        pt[i % 2][:, :])), "SC", f"tre_{fo}_{i}", 1)
            if stage >= 5:
                NG_ALL = NTT * NB

                def sc_ie(g2):
                    t2, j2 = g2 // NB, g2 % NB
                    wait(eng, "PE", f"int_{t2}_{j2}")
                    if g2 >= 2:
                        p2, q2 = (g2 - 2) // NB, (g2 - 2) % NB
                        wait(eng, "VE", f"am_{p2}_{q2}")
                    inc(E(lambda t2=t2, j2=j2: eng.copy(
                        P_tiles[j2 % 2].ap()[:, 0:256],
                        pb[2 + j2 % 2][:, 0:256])),
                        "SC", f"ie_{t2}_{j2}", 1)

                sc_ie(0)
                sc_ie(1)
                for g in range(NG_ALL):
                    tt, j = g // NB, g % NB
                    if g + 2 < NG_ALL:
                        sc_ie(g + 2)
                    wait(eng, "VE", f"hs_{tt}_{j}")
                    inc(E(lambda j=j: eng.activation(
                        sb_hr.ap()[:, j:j + 1], sb_h.ap()[:, j:j + 1],
                        AF.Relu)), "SC", f"hrl_{tt}_{j}", 1)
                    wait(eng, "SC", f"hrl_{tt}_{j}")
                    if g >= 2:
                        p2, q2 = (g - 2) // NB, (g - 2) % NB
                        wait(eng, "VE", f"o_{p2}_{q2}")
                    inc(E(lambda tt=tt, j=j: eng.activation(
                        w2sc[j % 2][:, :], wbuf[j % 2][:, 4096:8192],
                        AF.Copy, scale=sb_hr.ap()[:, j:j + 1])),
                        "SC", f"os_{tt}_{j}", 1)
                    if j == 2 and tt >= 1:
                        wait(eng, "VE", f"fa_{tt-1}_31")
                        inc(E(lambda tt=tt: eng.dma_start(out=d_y[tt - 1],
                                                          in_=otile)),
                            "SD", f"y_{tt-1}", 16)
                wait(eng, "VE", f"fa_{NTT-1}_31")
                inc(E(lambda: eng.dma_start(out=d_y[NTT - 1], in_=otile)),
                    "SD", f"y_{NTT-1}", 16)
                if ctx.emit:
                    eng.wait_ge(ctx.sems["SD"], ctx.ev[("SD", f"y_{NTT-1}")])
            else:
                inc(E(lambda: eng.dma_start(out=d_y[0][:, 0:NCOL],
                                            in_=sb_bias.ap())),
                    "SD", "y_stub", 16)
                if ctx.emit:
                    eng.wait_ge(ctx.sems["SD"], ctx.ev[("SD", "y_stub")])

        def stream_vector(eng):
            E, inc, wait = ctx.E, ctx.inc, ctx.wait
            pass
            for tc in range(4):
                wait(eng, "PE", f"sx_{tc}_{NFO-1}")
                E(lambda: eng.tensor_scalar(rows_kv[0:1, 0:512], pb[0][0:1, :],
                                            1.0 / D, None, ALU.mult))
                E(lambda: eng.tensor_scalar(rows_kv[0:1, 512:1024], pb[1][0:1, :],
                                            1.0 / D, None, ALU.mult))
                E(lambda: eng.tensor_tensor(rows_kv[0:1, 1536:2048],
                                            rows_kv[0:1, 0:512], rows_kv[0:1, 0:512],
                                            ALU.mult))
                E(lambda: eng.tensor_tensor(rows_kv[0:1, 1024:1536],
                                            rows_kv[0:1, 512:1024],
                                            rows_kv[0:1, 1536:2048],
                                            ALU.subtract))
                inc(E(lambda: eng.tensor_scalar(rows_kv[0:1, 1536:2048],
                                                rows_kv[0:1, 1024:1536],
                                                LN_EPS, None, ALU.add)),
                    "VE", f"var_{tc}", 1)
                wait(eng, "SC", f"srt_{tc}")
                E(lambda: eng.tensor_copy(sb_rbf.ap()[0:1, 0:512],
                                          rows_kv[0:1, 0:512]))
                E(lambda: eng.reciprocal(rows_kv[0:1, 0:512],
                                         rows_kv[0:1, 1024:1536]))
                inc(E(lambda: eng.tensor_copy(sb_rbf.ap()[0:1, 512:1024],
                                              rows_kv[0:1, 0:512])),
                    "VE", f"rows_{tc}", 1)
                wait(eng, "PE", f"bc_{tc}")
                E(lambda: eng.tensor_copy(sb_mbc.ap()[:, 0:512], pb[2][:, :]))
                inc(E(lambda: eng.tensor_copy(sb_mbc.ap()[:, 512:1024],
                                              pb[3][:, :])),
                    "VE", f"mbc_{tc}", 1)
                for kt in range(NFO):
                    if tc < 2:
                        dst = y1fm[:, kt * 1024 + tc * 512:
                                   kt * 1024 + tc * 512 + 512]
                    else:
                        dst = sb_ykv[:, (kt % 2) * 512:(kt % 2) * 512 + 512]
                        if kt >= 2:
                            wait(eng, "PE", f"kv_{tc}_{kt-2}")
                    E(lambda kt=kt: eng.tensor_tensor(
                        sb_nrm_kv[:, (kt % 2) * 512:(kt % 2) * 512 + 512],
                        xchunk[:, kt * 512:(kt + 1) * 512],
                        sb_mbc.ap()[:, 0:512], ALU.subtract))
                    inc(E(lambda kt=kt, dst=dst: eng.tensor_tensor(
                        dst,
                        sb_nrm_kv[:, (kt % 2) * 512:(kt % 2) * 512 + 512],
                        sb_mbc.ap()[:, 512:1024], ALU.mult)),
                        "VE", f"y_{tc}_{kt}", 1)
                wait(eng, "PE", f"kv_{tc}_{NFO-1}")
                E(lambda tc=tc: eng.tensor_copy(
                    sb_kf.ap()[:, tc * 512:(tc + 1) * 512], pb[4][:, :]))
                inc(E(lambda: eng.tensor_copy(sb_vfm[:, :], pb[5][:, :])),
                    "VE", f"kve_{tc}", 1)
            if stage < 1:
                return
            for m in range(NFO):
                wait(eng, "PE", f"q_{m}")
                E(lambda m=m: eng.tensor_copy(
                    qT[:, m * 1024:m * 1024 + 512], pb[(m % 2) * 2][:, :]))
                inc(E(lambda m=m: eng.tensor_copy(
                    qT[:, m * 1024 + 512:m * 1024 + 1024],
                    pb[(m % 2) * 2 + 1][:, :])), "VE", f"qe_{m}", 1)
            if stage < 2:
                return
            for qc in (0, 1):
                nsl = len(SLOTSETS[qc])
                for h in range(NH):
                    wait(eng, "PE", f"av_{h}_{qc}_{nsl-1}")
                    E(lambda: eng.reciprocal(sb_rrow.ap()[0:1, :],
                                             pb[3][0:1, :]))
                    inc(E(lambda: eng.tensor_copy(sb_rbf.ap()[0:1, 0:512],
                                                  sb_rrow.ap()[0:1, :])),
                        "VE", f"r_{h}_{qc}", 1)
                    wait(eng, "SC", f"rbe_{h}_{qc}")
                    inc(E(lambda h=h, qc=qc: eng.tensor_tensor(
                        attnout[:, q_off(h, qc):q_off(h, qc) + 512],
                        pb[2][:, :], sb_rbc[:, :], ALU.mult)),
                        "VE", f"ao_{h}_{qc}", 1)
            if stage < 3:
                return
            for fo in range(NFO):
                wait(eng, "PE", f"wo_{fo}")
                wait(eng, "LA" if fo % 2 == 0 else "LB", f"xf_{fo}")
                if fo >= 1:
                    wait(eng, "SD", f"x1s_{fo-1}")
                E(lambda fo=fo: eng.tensor_tensor(
                    sb_x1o[:, 0:512], pb[(fo % 2) * 2][:, :],
                    sb_x1f[:, 0:512], ALU.add))
                E(lambda fo=fo: eng.tensor_tensor(
                    sb_x1o[:, 512:1024], pb[(fo % 2) * 2 + 1][:, :],
                    sb_x1f[:, 512:1024], ALU.add))
                inc(E(lambda fo=fo: eng.tensor_copy(
                    x1bf[:, fo * 1024:(fo + 1) * 1024], sb_x1o[:, :])),
                    "VE", f"x1_{fo}", 1)
            wait(eng, "PE", f"st2_{NFO-1}")
            wait(eng, "SD", f"x1s_{NFO-1}")
            x1fr = sb_x1f[0:1, :]
            x1or = sb_x1o[0:1, :]
            E(lambda: eng.tensor_scalar(rows_ln2[0:1, 0:512], pb[4][0:1, :],
                                        1.0 / D, None, ALU.mult))
            E(lambda: eng.tensor_scalar(rows_ln2[0:1, 512:1024], pb[4][32:33, :],
                                        1.0 / D, None, ALU.mult))
            E(lambda: eng.tensor_scalar(rows_ln2[0:1, 1024:1536], pb[4][64:65, :],
                                        1.0 / D, None, ALU.mult))
            E(lambda: eng.tensor_scalar(rows_ln2[0:1, 1536:2048], pb[5][0:1, :],
                                        1.0 / D, None, ALU.mult))
            E(lambda: eng.tensor_tensor(x1fr[:, 0:512],
                                        rows_ln2[0:1, 0:512], rows_ln2[0:1, 0:512],
                                        ALU.mult))
            E(lambda: eng.tensor_tensor(x1fr[:, 512:1024],
                                        rows_ln2[0:1, 512:1024],
                                        rows_ln2[0:1, 512:1024], ALU.mult))
            E(lambda: eng.tensor_tensor(x1or[:, 0:512],
                                        rows_ln2[0:1, 1024:1536],
                                        x1fr[:, 0:512], ALU.subtract))
            E(lambda: eng.tensor_tensor(x1or[:, 512:1024],
                                        rows_ln2[0:1, 1536:2048],
                                        x1fr[:, 512:1024],
                                        ALU.subtract))
            E(lambda: eng.tensor_scalar(rows_ln2[0:1, 1024:1536],
                                        x1or[:, 0:512],
                                        LN_EPS, None, ALU.add))
            inc(E(lambda: eng.tensor_scalar(rows_ln2[0:1, 1536:2048],
                                            x1or[:, 512:1024],
                                            LN_EPS, None, ALU.add)),
                "VE", "var2", 1)
            wait(eng, "SC", "srt2")
            E(lambda: eng.reciprocal(x1fr[:, 0:512], x1or[:, 0:512]))
            E(lambda: eng.reciprocal(x1fr[:, 512:1024], x1or[:, 512:1024]))
            E(lambda: eng.tensor_copy(sb_rbf.ap()[0:1, 0:1024],
                                      rows_ln2[0:1, 0:1024]))
            inc(E(lambda: eng.tensor_copy(sb_rbf.ap()[0:1, 1024:2048],
                                          x1fr[:, 0:1024])),
                "VE", "rows2", 1)
            if stage < 4:
                return
            wait(eng, "SC", "bc2e")
            for fo in range(NFO):
                if fo >= 2:
                    wait(eng, "PE", f"tr_{fo-2}_7")
                E(lambda fo=fo: eng.tensor_tensor(
                    sb_nrm_t[:, :], x1bf[:, fo * 1024:(fo + 1) * 1024],
                    sb_mbc.ap()[:, 0:1024], ALU.subtract))
                inc(E(lambda fo=fo: eng.tensor_tensor(
                    y2r[fo % 2][:, :], sb_nrm_t[:, :],
                    sb_mbc.ap()[:, 1024:2048], ALU.mult)),
                    "VE", f"y2_{fo}", 1)
            wait(eng, "PE", "u")
            E(lambda: eng.tensor_copy(sb_u.ap()[0:16, 0:512], pb[4][0:16, :]))
            inc(E(lambda: eng.tensor_copy(sb_u.ap()[0:16, 512:1024],
                                          pb[5][0:16, :])), "VE", "ue", 1)
            if stage < 5:
                return
            NG_ALL = NTT * NB

            def ve_argmax(g2):
                t2, j2 = g2 // NB, g2 % NB
                wait(eng, "SC", f"ie_{t2}_{j2}")
                if g2 == 0:
                    wait(eng, "LC", "b_iota")
                inc(E(lambda j2=j2: eng.reduce_max(
                    sb_m8.ap()[:, 0:1], P_tiles[j2 % 2].ap()[:, 0:256],
                    axis=mybir.AxisListType.X)),
                    "VE", f"amx_{t2}_{j2}", 1)
                wait(eng, "VE", f"amx_{t2}_{j2}")
                E(lambda j2=j2: eng.tensor_scalar(
                    sb_pt1.ap()[:, 256:512], P_tiles[j2 % 2].ap()[:, 0:256],
                    sb_m8.ap()[:, 0:1], None, ALU.is_ge))
                E(lambda: eng.tensor_tensor(
                    sb_sq.ap()[:, 0:256], sb_pt1.ap()[:, 256:512],
                    sb_pt0.ap()[:, 256:512], ALU.mult))
                inc(E(lambda: eng.reduce_max(
                    sb_m8.ap()[:, 1:2], sb_sq.ap()[:, 0:256],
                    axis=mybir.AxisListType.X)),
                    "VE", f"amr_{t2}_{j2}", 1)
                wait(eng, "VE", f"amr_{t2}_{j2}")
                inc(E(lambda g2=g2: eng.tensor_copy(
                    sb_i8.ap()[:, g2 % 4:g2 % 4 + 1],
                    sb_m8.ap()[:, 1:2])),
                    "VE", f"am_{t2}_{j2}", 1)

            def ve_add(p):
                pt_, pj = p // NB, p % NB
                wait(eng, "SC", f"os_{pt_}_{pj}")
                if pj == 0:
                    if pt_ >= 1:
                        wait(eng, "PE", f"ft_{pt_-1}_{NFO-1}")
                    inc(E(lambda: eng.tensor_copy(
                        acc_bf[1][:, :], w2sc[0][:, :])),
                        "VE", f"o_{pt_}_0", 1)
                else:
                    if pj == 1 and pt_ >= 1:
                        wait(eng, "PE", f"ft_{pt_-1}_{NFO-1}")
                    inc(E(lambda pj=pj: eng.tensor_tensor(
                        acc_bf[(pj + 1) % 2][:, :], w2sc[pj % 2][:, :],
                        acc_bf[pj % 2][:, :], ALU.add)),
                        "VE", f"o_{pt_}_{pj}", 1)
                if pj == NB - 1:
                    for fo in range(NFO):
                        wait(eng, "PE", f"ft_{pt_}_{fo}")
                        wait(eng, "LC", f"b_x1c{pt_}")
                        if pt_ >= 1 and fo == 0:
                            wait(eng, "SD", f"y_{pt_-1}")
                        inc(E(lambda pt_=pt_, fo=fo: eng.tensor_tensor(
                            otile[:, fo * 128:(fo + 1) * 128],
                            pt[fo % 2][:, :],
                            x1c[:, fo * 128:(fo + 1) * 128], ALU.add)),
                            "VE", f"fa_{pt_}_{fo}", 1)

            ve_argmax(0)
            ve_argmax(1)
            for g in range(NG_ALL):
                tt, j = g // NB, g % NB
                la = g + 2 < NG_ALL
                if la:
                    t2, j2 = (g + 2) // NB, (g + 2) % NB
                    wait(eng, "SC", f"ie_{t2}_{j2}")
                    inc(E(lambda j2=j2: eng.reduce_max(
                        sb_m8.ap()[:, 0:1], P_tiles[j2 % 2].ap()[:, 0:256],
                        axis=mybir.AxisListType.X)),
                        "VE", f"amx_{t2}_{j2}", 1)
                wait(eng, "GA", f"g_{tt}_{j}")
                inc(E(lambda tt=tt, j=j: eng.scalar_tensor_tensor(
                    out=prod[:, :], in0=wbuf[j % 2][:, 0:4096], scalar=1.0,
                    in1=y2tm[:, tt * 4096:(tt + 1) * 4096],
                    op0=ALU.mult, op1=ALU.mult,
                    accum_out=sb_h.ap()[:, j:j + 1])),
                    "VE", f"hs_{tt}_{j}", 1)
                if la:
                    wait(eng, "VE", f"amx_{t2}_{j2}")
                    E(lambda j2=j2: eng.tensor_scalar(
                        sb_pt1.ap()[:, 256:512], P_tiles[j2 % 2].ap()[:, 0:256],
                        sb_m8.ap()[:, 0:1], None, ALU.is_ge))
                    E(lambda: eng.tensor_tensor(
                        sb_sq.ap()[:, 0:256], sb_pt1.ap()[:, 256:512],
                        sb_pt0.ap()[:, 256:512], ALU.mult))
                    inc(E(lambda: eng.reduce_max(
                        sb_m8.ap()[:, 1:2], sb_sq.ap()[:, 0:256],
                        axis=mybir.AxisListType.X)),
                        "VE", f"amr_{t2}_{j2}", 1)
                if g >= 1:
                    ve_add(g - 1)
                if la:
                    wait(eng, "VE", f"amr_{t2}_{j2}")
                    inc(E(lambda g=g: eng.tensor_copy(
                        sb_i8.ap()[:, (g + 2) % 4:(g + 2) % 4 + 1],
                        sb_m8.ap()[:, 1:2])),
                        "VE", f"am_{t2}_{j2}", 1)
            ve_add(NG_ALL - 1)

        ctx.emit = False
        stream_sync(None)
        stream_gpsimd(None)
        stream_tensor(None)
        stream_scalar(None)
        stream_vector(None)

        ctx.emit = True

        @block.sync
        def _(eng):
            stream_sync(eng)

        @block.gpsimd
        def _(eng):
            stream_gpsimd(eng)

        @block.tensor
        def _(eng):
            stream_tensor(eng)

        @block.scalar
        def _(eng):
            stream_scalar(eng)

        @block.vector
        def _(eng):
            stream_vector(eng)

    return nc


def _bf(x):
    return np.ascontiguousarray(x).astype(ml_dtypes.bfloat16)


def kernel(L0_attn_hidden_states, L0_attn_Wq, L0_attn_Wkv, L0_attn_Wo,
           L0_ffn_c1_w, L0_ffn_c2_w, L0_ffn_w1, L0_ffn_w2t, _debug=False,
           _stage=99, _fsub=99):
    X = np.asarray(L0_attn_hidden_states, dtype=np.float32)
    Wq = np.asarray(L0_attn_Wq, dtype=np.float32)
    Wkv = np.asarray(L0_attn_Wkv, dtype=np.float32)
    Wo = np.asarray(L0_attn_Wo, dtype=np.float32)
    c1 = np.asarray(L0_ffn_c1_w, dtype=np.float32)
    c2 = np.asarray(L0_ffn_c2_w, dtype=np.float32)
    w1 = np.asarray(L0_ffn_w1, dtype=np.float32)
    w2t = np.asarray(L0_ffn_w2t, dtype=np.float32)

    wq_t = _bf(Wq.reshape(NFO, 128, NFO, 128).transpose(2, 1, 0, 3)
               .reshape(NFO, 128, D))
    wo_t = _bf(Wo.reshape(NFO, 128, NFO, 128).transpose(2, 1, 0, 3)
               .reshape(NFO, 128, D))
    wkv_t = _bf(Wkv.reshape(NFO, 128, 256).transpose(1, 0, 2).reshape(128, 8192))
    c1t = _bf(c1.T.reshape(NFO, 128, 16).transpose(1, 0, 2).reshape(128, 512))
    c2t = _bf(c2.T)
    w1w2 = np.concatenate([_bf(w1), _bf(w2t)], axis=1)
    mask4 = _host_masks()
    u1 = np.triu(np.ones((128, 128))).astype(ml_dtypes.bfloat16)
    ident = np.eye(128).astype(ml_dtypes.bfloat16)
    ones = np.ones((128, 128)).astype(ml_dtypes.bfloat16)
    iota = np.broadcast_to(np.arange(256, dtype=np.float32), (128, 256)) \
        .astype(ml_dtypes.bfloat16)

    nc = build_program(debug=_debug, stage=_stage, fsub=_fsub)
    in_maps = []
    for c in range(8):
        perm, _, _ = _core_geom(c)
        b = c // 2
        xp = X[b][perm]
        in_maps.append({
            "xfmt": _bf(xp.T.reshape(NFO, 128, S)),
            "xf32": np.ascontiguousarray(
                xp[:NTOK].T.reshape(NFO, 128, NTOK)).astype(np.float32),
            "wq_t": wq_t, "wo_t": wo_t, "wkv_t": wkv_t,
            "c1t": c1t, "c2t": c2t, "w1w2": w1w2,
            "biasm": _host_biasmat(c), "mask4": mask4,
            "u1": u1, "ident": ident, "onesb": ones, "iota": iota,
        })

    res = run_bass_kernel_spmd(nc, in_maps, core_ids=list(range(8)))

    out = np.empty((B, S, D), dtype=np.float32)
    for c in range(8):
        perm, _, _ = _core_geom(c)
        b = c // 2
        yc = np.asarray(res.results[c]["y"])
        toks = yc.reshape(NTT, 128, NFO, 128).transpose(0, 3, 2, 1) \
                 .reshape(NTOK, D)
        out[b][perm[:NTOK]] = toks
    if _debug:
        return out, res
    return out


# version 08
# speedup vs baseline: 1.4530x; 1.4530x over previous
import math
import numpy as np
import ml_dtypes

import concourse.bass as bass
import concourse.mybir as mybir
from concourse.bass import IndirectOffsetOnAxis
from concourse.bass_utils import run_bass_kernel_spmd

F32 = mybir.dt.float32
BF16 = mybir.dt.bfloat16
U8 = mybir.dt.uint8
U32 = mybir.dt.uint32
ALU = mybir.AluOpType
AF = mybir.ActivationFunctionType

IDX_ZERO = False
DBG_H = False
GLIMIT = 10**9
PLAIN_GATHER = False

B, S, D, NH, HD = 4, 2048, 4096, 32, 128
INTER, SPR, NB, RANK = 14336, 256, 56, 16
SCALE = HD ** -0.5
NTOK = 1024
NKT = 16
NFO = 32
NTT = 8
LN_EPS = 1e-5
MASK_NEG = -50000.0
KILL = -1e9

_start = 2.0 ** (-(2.0 ** -(math.log2(NH) - 3)))
SLOPES = [float(np.float32(_start ** (i + 1))) for i in range(NH)]

SLOTSETS = {0: [0, 1, 2, 3, 8, 9, 10, 11], 1: list(range(16))}
MASKED = {0: {0: 0, 1: 1, 2: 2, 3: 3}, 1: {4: 0, 5: 1, 6: 2, 7: 3}}


def n_groups(h):
    sh = SLOPES[h]
    if sh * 256.0 <= 66.0:
        return 1
    if sh * 128.0 <= 66.0:
        return 2
    return 4


def _core_geom(c):
    half = c % 2
    if half == 0:
        perm = np.concatenate([np.arange(0, 512), np.arange(1536, 2048),
                               np.arange(512, 1536)])
        q0s = (0, 1536)
    else:
        perm = np.concatenate([np.arange(512, 1536), np.arange(0, 512),
                               np.arange(1536, 2048)])
        q0s = (512, 1024)
    tile_orig = [int(perm[s * 128]) for s in range(NKT)]
    return perm, q0s, tile_orig


_BIAS_COLS = []
COLOF = {}
for _h in range(NH):
    for _qc in (0, 1):
        for _si in range(len(SLOTSETS[_qc])):
            for _g in range(n_groups(_h)):
                COLOF[(_h, _qc, _si, _g)] = len(_BIAS_COLS)
                _BIAS_COLS.append((_h, _qc, _si, _g))
NCOL = len(_BIAS_COLS)


def _host_biasmat(c):
    _, q0s, torig = _core_geom(c)
    bm = np.zeros((128, NCOL), dtype=np.float64)
    p = np.arange(128, dtype=np.float64)
    for col, (h, qc, si, g) in enumerate(_BIAS_COLS):
        s = SLOTSETS[qc][si]
        k0, q0 = torig[s], q0s[qc]
        if k0 > q0 + 511:
            bm[:, col] = KILL
        else:
            w = 512 // n_groups(h)
            qt = q0 + w * g + w // 2
            bm[:, col] = SLOPES[h] * (k0 + p - qt)
    return bm.astype(np.float32)


def _host_masks():
    m = np.zeros((4, 128, 512), dtype=np.float32)
    c = np.arange(128)[:, None]
    j = np.arange(512)[None, :]
    for d in range(4):
        m[d] = np.where(c > j - 128 * d, MASK_NEG, 0.0)
    return m.astype(ml_dtypes.bfloat16)


class Ctx:
    SEMS = ("LD", "LA", "LB", "LC", "SD", "GA", "PE", "VE", "SC")

    def __init__(self):
        self.emit = False
        self.cnt = {s: 0 for s in self.SEMS}
        self.ev = {}
        self.sems = {}

    def inc(self, inst, sem, name, n):
        if self.emit:
            assert inst is not None, (sem, name)
            inst.then_inc(self.sems[sem], n)
        else:
            self.cnt[sem] += n
            self.ev[(sem, name)] = self.cnt[sem]

    def wait(self, eng, sem, name):
        if self.emit:
            eng.wait_ge(self.sems[sem], self.ev[(sem, name)])

    def E(self, thunk):
        return thunk() if self.emit else None

    def mark(self, sem, name):
        if not self.emit:
            self.ev[(sem, name)] = self.cnt[sem]


def build_program(debug=False, stage=99, fsub=99):
    nc = bass.Bass("TRN2", target_bir_lowering=False)

    d_xfmt = nc.dram_tensor("xfmt", [NFO, 128, S], BF16, kind="ExternalInput").ap()
    d_xf32 = nc.dram_tensor("xf32", [NFO, 128, NTOK], F32, kind="ExternalInput").ap()
    d_wq = nc.dram_tensor("wq_t", [NFO, 128, D], BF16, kind="ExternalInput").ap()
    d_wo = nc.dram_tensor("wo_t", [NFO, 128, D], BF16, kind="ExternalInput").ap()
    d_wkv = nc.dram_tensor("wkv_t", [128, 8192], BF16, kind="ExternalInput").ap()
    d_c1 = nc.dram_tensor("c1t", [128, 512], BF16, kind="ExternalInput").ap()
    d_c2 = nc.dram_tensor("c2t", [16, INTER], BF16, kind="ExternalInput").ap()
    d_w12 = nc.dram_tensor("w1w2", [INTER, 8192], BF16, kind="ExternalInput").ap()
    d_bias = nc.dram_tensor("biasm", [128, NCOL], F32, kind="ExternalInput").ap()
    d_mask = nc.dram_tensor("mask4", [4, 128, 512], BF16, kind="ExternalInput").ap()
    d_u1 = nc.dram_tensor("u1", [128, 128], BF16, kind="ExternalInput").ap()
    d_id = nc.dram_tensor("ident", [128, 128], BF16, kind="ExternalInput").ap()
    d_ones = nc.dram_tensor("onesb", [128, 128], BF16, kind="ExternalInput").ap()
    d_iota = nc.dram_tensor("iota", [128, 256], BF16, kind="ExternalInput").ap()
    d_x1s = nc.dram_tensor("x1s", [NFO, 128, NTOK], F32, kind="Internal").ap()
    d_y = nc.dram_tensor("y", [NTT, 128, D], F32, kind="ExternalOutput").ap()
    dbg = {}
    if debug:
        for nm, shp, dt in [("dbg_y1", [128, NFO * NTOK], BF16),
                            ("dbg_q", [128, NH * NTOK], BF16),
                            ("dbg_kf", [128, NKT * 128], BF16),
                            ("dbg_vt", [128, NKT * 128], BF16),
                            ("dbg_ao", [128, NH * NTOK], BF16),
                            ("dbg_x1", [128, NFO * NTOK], BF16),
                            ("dbg_u", [16, NTOK], BF16),
                            ("dbg_y2", [128, NTT * D], BF16),
                            ("dbg_h", [NTT, 128, 56], F32)]:
            dbg[nm] = nc.dram_tensor(nm, shp, dt, kind="ExternalOutput").ap()

    ctx = Ctx()

    from contextlib import ExitStack
    with ExitStack() as _st:
        def _sb(name, shape, dt):
            return _st.enter_context(nc.sbuf_tensor(name, shape, dt))

        def _ps(name, shape, dt):
            return _st.enter_context(nc.psum_tensor(name, shape, dt))

        arA = _sb("arA", [128, 65536], U8)
        arB = _sb("arB", [128, 32768], BF16)
        arC = _sb("arC", [128, 49152], U8)
        sb_bias = _sb("sb_bias", [128, NCOL], F32)
        sb_u1 = _sb("sb_u1", [128, 128], BF16)
        sb_id = _sb("sb_id", [128, 128], BF16)
        sb_ones = _sb("sb_ones", [128, 128], BF16)
        sb_kf = _sb("sb_kf", [128, NKT * 128], BF16)
        sb_vt = _sb("sb_vt", [128, NKT * 128], BF16)
        sb_c1 = _sb("sb_c1", [128, 512], BF16)
        sb_pt0 = _sb("sb_pt0", [128, 512], BF16)
        sb_pt1 = _sb("sb_pt1", [128, 512], BF16)
        sb_mbc = _sb("sb_mbc", [128, 2048], BF16)
        sb_sq = _sb("sb_sq", [128, 1024], BF16)
        sb_rrow = _sb("sb_rrow", [1, 512], F32)
        sb_rbf = _sb("sb_rbf", [1, 2048], BF16)
        sb_u = _sb("sb_u", [16, 1024], BF16)
        sb_c2b0 = _sb("sb_c2b0", [16, 256], BF16)
        sb_c2b1 = _sb("sb_c2b1", [16, 256], BF16)
        sb_m8 = _sb("sb_m8", [128, 8], F32)
        sb_i8 = _sb("sb_i8", [128, 16], U32)
        sb_h = _sb("sb_h", [128, 64], F32)
        sb_hr = _sb("sb_hr", [128, 64], F32)
        pb0 = _ps("pb0", [128, 512], F32)
        pb1 = _ps("pb1", [128, 512], F32)
        pb2 = _ps("pb2", [128, 512], F32)
        pb3 = _ps("pb3", [128, 512], F32)
        pb4 = _ps("pb4", [128, 512], F32)
        pb5 = _ps("pb5", [128, 512], F32)
        pt0 = _ps("pt0", [128, 128], BF16)
        pt1 = _ps("pt1", [128, 128], BF16)
        sLD = _st.enter_context(nc.semaphore("sLD"))
        sLA = _st.enter_context(nc.semaphore("sLA"))
        sLB = _st.enter_context(nc.semaphore("sLB"))
        sLC = _st.enter_context(nc.semaphore("sLC"))
        sSD = _st.enter_context(nc.semaphore("sSD"))
        sGA = _st.enter_context(nc.semaphore("sGA"))
        sPE = _st.enter_context(nc.semaphore("sPE"))
        sVE = _st.enter_context(nc.semaphore("sVE"))
        sSC = _st.enter_context(nc.semaphore("sSC"))
        block = _st.enter_context(nc.Block())
        ctx.sems = {"LD": sLD, "LA": sLA, "LB": sLB, "LC": sLC,
                    "SD": sSD, "GA": sGA,
                    "PE": sPE, "VE": sVE, "SC": sSC}
        pb = [pb0, pb1, pb2, pb3, pb4, pb5]
        pt = [pt0, pt1]

        vA = arA.ap().bitcast(BF16)
        y1fm = vA
        attnout = vA
        y2tm = vA
        vB = arB.ap()
        qT = vB
        x1bf = vB
        wbuf = [vB[:, 0:8192], vB[:, 8192:16384]]
        x1c = arB.ap().bitcast(F32)[:, 8192:12288]
        otile = arB.ap().bitcast(F32)[:, 12288:16384]
        vC16 = arC.ap().bitcast(BF16)
        xchunk = vC16[:, 0:16384]
        wkv_sb = vC16[:, 16384:24576]
        wqbuf = [vC16[:, 0:4096], vC16[:, 4096:8192]]
        wobuf = wqbuf
        acc_bf = [vC16[:, 0:4096], vC16[:, 4096:8192]]
        w2sc = [vC16[:, 16384:20480], vC16[:, 12288:16384]]
        prod = vC16[:, 20480:24576]
        sb_ykv = vB[:, 0:1024]
        sb_nrm_kv = vB[:, 1024:2048]
        sb_vfm = vB[:, 2048:2560]
        vBf32 = arB.ap().bitcast(F32)
        rows_kv = vBf32[0:1, 1280:3328]
        vCf32 = arC.ap().bitcast(F32)
        sb_x1o = vCf32[:, 4096:5120]
        sb_x1f = vCf32[:, 5120:6144]
        rows_ln2 = vCf32[0:1, 6144:8192]
        y2r = [vC16[:, 8192:9216], vC16[:, 9216:10240]]
        sb_nrm_t = vC16[:, 10240:11264]
        sb_mask = vC16[:, 14336:16384]
        sb_rbc = vC16[:, 6144:6656]

        P_tiles = [sb_pt0, sb_pt1]
        c2b = [sb_c2b0, sb_c2b1]

        def q_off(h, qc):
            return h * 1024 + qc * 512

        def stream_sync(eng):
            E, inc, wait = ctx.E, ctx.inc, ctx.wait
            for nm, dst, src in [("c_bias", sb_bias.ap(), d_bias),
                                 ("c_u1", sb_u1.ap(), d_u1),
                                 ("c_id", sb_id.ap(), d_id),
                                 ("c_ones", sb_ones.ap(), d_ones),
                                 ("c_c1", sb_c1.ap(), d_c1),
                                 ("c_wkv", wkv_sb, d_wkv)]:
                inc(E(lambda dst=dst, src=src: eng.dma_start(out=dst, in_=src)),
                    "LC", nm, 16)
            ctx.mark("LC", "b_pre")
            wait(eng, "LC", "b_pre")
            for tc in range(4):
                for kt in range(NFO):
                    if tc >= 1:
                        wait(eng, "VE", f"y_{tc-1}_{kt}")
                    inc(E(lambda tc=tc, kt=kt: eng.dma_start(
                        out=xchunk[:, kt * 512:(kt + 1) * 512],
                        in_=d_xfmt[kt][:, tc * 512:(tc + 1) * 512])),
                        "LC", f"xc_{tc}_{kt}", 16)
                ctx.mark("LC", f"b_xc{tc}")
                wait(eng, "LC", f"b_xc{tc}")
            if stage < 1:
                return
            wait(eng, "VE", "y_3_31")
            for dd in range(4):
                inc(E(lambda dd=dd: eng.dma_start(
                    out=sb_mask[:, dd * 512:(dd + 1) * 512],
                    in_=d_mask[dd])), "LC", f"c_mask{dd}", 16)
            ctx.mark("LC", "b_mask")
            wait(eng, "LC", "b_mask")
            for m in range(NFO):
                if m >= 2:
                    wait(eng, "PE", f"q_{m-2}")
                inc(E(lambda m=m: eng.dma_start(out=wqbuf[m % 2],
                                                in_=d_wq[m])),
                    "LA" if m % 2 == 0 else "LB", f"wq_{m}", 16)
            if stage < 3:
                return
            for fo in range(NFO):
                if fo == 0:
                    wait(eng, "VE", "ao_31_1")
                if fo >= 2:
                    wait(eng, "PE", f"wo_{fo-2}")
                inc(E(lambda fo=fo: eng.dma_start(out=wobuf[fo % 2],
                                                  in_=d_wo[fo])),
                    "LA" if fo % 2 == 0 else "LB", f"wwo_{fo}", 16)
                if fo >= 1:
                    wait(eng, "VE", f"x1_{fo-1}")
                inc(E(lambda fo=fo: eng.dma_start(out=sb_x1f,
                                                  in_=d_xf32[fo])),
                    "LA" if fo % 2 == 0 else "LB", f"xf_{fo}", 16)
            if stage < 5:
                return
            wait(eng, "PE", f"av_31_1_{len(SLOTSETS[1])-1}")
            inc(E(lambda: eng.dma_start(out=sb_pt0.ap()[:, 256:512],
                                        in_=d_iota)), "LC", "iota", 16)
            ctx.mark("LC", "b_iota")
            wait(eng, "LC", "b_iota")
            def sy_c2(g2):
                t2, j2 = g2 // NB, g2 % NB
                if g2 >= 2:
                    p2, q2 = (g2 - 2) // NB, (g2 - 2) % NB
                    wait(eng, "PE", f"int_{p2}_{q2}")
                inc(E(lambda t2=t2, j2=j2: eng.dma_start(
                    out=c2b[j2 % 2].ap(),
                    in_=d_c2[:, j2 * 256:(j2 + 1) * 256])),
                    "LA" if j2 % 2 == 0 else "LB", f"c2_{t2}_{j2}", 16)

            def sy_x1c(tt):
                if tt == 0:
                    wait(eng, "VE", "ue")
                    wait(eng, "SD", f"x1s_{NFO-1}")
                else:
                    wait(eng, "VE", f"fa_{tt-1}_31")
                for fo in range(NFO):
                    inc(E(lambda tt=tt, fo=fo: eng.dma_start(
                        out=x1c[:, fo * 128:(fo + 1) * 128],
                        in_=d_x1s[fo][:, tt * 128:(tt + 1) * 128])),
                        "LC", f"x1c_{tt}_{fo}", 16)
                ctx.mark("LC", f"b_x1c{tt}")
                wait(eng, "LC", f"b_x1c{tt}")

            for g in range(NTT * NB):
                sy_c2(g)
                if g % NB == 2:
                    sy_x1c(g // NB)
            if NB <= 2:
                raise AssertionError

        def stream_gpsimd(eng):
            E, inc, wait = ctx.E, ctx.inc, ctx.wait
            if debug:
                wait(eng, "VE", "y_1_31")
                inc(E(lambda: eng.dma_start(out=dbg["dbg_y1"], in_=y1fm)),
                    "GA", "dbg_y1", 16)
                wait(eng, "VE", "kve_3")
                wait(eng, "SC", "vt_3_3")
                inc(E(lambda: eng.dma_start(out=dbg["dbg_kf"], in_=sb_kf.ap())),
                    "GA", "dbg_kf", 16)
                inc(E(lambda: eng.dma_start(out=dbg["dbg_vt"], in_=sb_vt.ap())),
                    "GA", "dbg_vt", 16)
                wait(eng, "VE", "qe_31")
                inc(E(lambda: eng.dma_start(out=dbg["dbg_q"], in_=qT)),
                    "GA", "dbg_q", 16)
                wait(eng, "VE", "ao_31_1")
                inc(E(lambda: eng.dma_start(out=dbg["dbg_ao"], in_=attnout)),
                    "GA", "dbg_ao", 16)
                wait(eng, "VE", "x1_31")
                inc(E(lambda: eng.dma_start(out=dbg["dbg_x1"], in_=x1bf)),
                    "GA", "dbg_x1", 16)
                wait(eng, "VE", "ue")
                inc(E(lambda: eng.dma_start(out=dbg["dbg_u"], in_=sb_u.ap())),
                    "GA", "dbg_u", 16)
                wait(eng, "SC", "tre_31_7")
                inc(E(lambda: eng.dma_start(out=dbg["dbg_y2"], in_=y2tm)),
                    "GA", "dbg_y2", 16)
            if stage < 5:
                return
            NG_ALL = NTT * NB
            for g in range(NG_ALL):
                tt, j = g // NB, g % NB
                wait(eng, "VE", f"am_{tt}_{j}")
                if g >= 2:
                    pt_, pj = (g - 2) // NB, (g - 2) % NB
                    wait(eng, "VE", f"hs_{pt_}_{pj}")
                    wait(eng, "SC", f"os_{pt_}_{pj}")
                inc(E(lambda tt=tt, j=j: eng.indirect_dma_start(
                    wbuf[j % 2],
                    None,
                    d_w12,
                    IndirectOffsetOnAxis(
                        ap=sb_i8.ap()[:, g % 4 if False else (tt * NB + j) % 4:
                                      (tt * NB + j) % 4 + 1],
                        axis=0),
                    element_offset=j * 256 * 8192)),
                    "GA", f"g_{tt}_{j}", 16)

        def stream_tensor(eng):
            E, inc, wait = ctx.E, ctx.inc, ctx.wait
            wait(eng, "LC", "b_pre")
            for tc in range(4):
                wait(eng, "LC", f"b_xc{tc}")
                for kt in range(NFO):
                    E(lambda tc=tc, kt=kt: eng.matmul(
                        pb[0][0:1, :], sb_ones.ap()[:, 0:1],
                        xchunk[:, kt * 512:(kt + 1) * 512],
                        start=(kt == 0), stop=(kt == NFO - 1)))
                    wait(eng, "SC", f"sq_{tc}_{kt}")
                    inc(E(lambda kt=kt: eng.matmul(
                        pb[1][0:1, :], sb_ones.ap()[:, 0:1],
                        sb_sq.ap()[:, (kt % 2) * 512:(kt % 2) * 512 + 512],
                        start=(kt == 0), stop=(kt == NFO - 1))),
                        "PE", f"sx_{tc}_{kt}", 1)
                wait(eng, "VE", f"rows_{tc}")
                E(lambda: eng.matmul(pb[2][:, :], sb_ones.ap()[0:1, 0:128],
                                     sb_rbf.ap()[0:1, 0:512],
                                     start=True, stop=True))
                inc(E(lambda: eng.matmul(pb[3][:, :], sb_ones.ap()[0:1, 0:128],
                                         sb_rbf.ap()[0:1, 512:1024],
                                         start=True, stop=True)),
                    "PE", f"bc_{tc}", 1)
                for kt in range(NFO):
                    wait(eng, "VE", f"y_{tc}_{kt}")
                    if tc < 2:
                        dst = y1fm[:, kt * 1024 + tc * 512:
                                   kt * 1024 + tc * 512 + 512]
                    else:
                        dst = sb_ykv[:, (kt % 2) * 512:(kt % 2) * 512 + 512]
                    E(lambda kt=kt, dst=dst: eng.matmul(
                        pb[4][:, :], wkv_sb[:, kt * 256:kt * 256 + 128], dst,
                        start=(kt == 0), stop=(kt == NFO - 1)))
                    inc(E(lambda kt=kt, dst=dst: eng.matmul(
                        pb[5][:, :], wkv_sb[:, kt * 256 + 128:kt * 256 + 256],
                        dst, start=(kt == 0), stop=(kt == NFO - 1))),
                        "PE", f"kv_{tc}_{kt}", 1)
                wait(eng, "VE", f"kve_{tc}")
                for i in range(4):
                    if i >= 2:
                        wait(eng, "SC", f"vt_{tc}_{i-2}")
                    elif tc >= 1:
                        wait(eng, "SC", f"vt_{tc-1}_{2+i}")
                    inc(E(lambda i=i: eng.transpose(
                        pt[i % 2][:, :], sb_vfm[:, i * 128:(i + 1) * 128],
                        sb_id.ap())), "PE", f"vt_{tc}_{i}", 1)
            if stage < 1:
                return
            for m in range(NFO):
                wait(eng, "LA" if m % 2 == 0 else "LB", f"wq_{m}")
                if m >= 2:
                    wait(eng, "VE", f"qe_{m-2}")
                for kt in range(NFO):
                    E(lambda m=m, kt=kt: eng.matmul(
                        pb[(m % 2) * 2][:, :],
                        wqbuf[m % 2][:, kt * 128:(kt + 1) * 128],
                        y1fm[:, kt * 1024:kt * 1024 + 512],
                        start=(kt == 0), stop=(kt == NFO - 1)))
                    i2 = E(lambda m=m, kt=kt: eng.matmul(
                        pb[(m % 2) * 2 + 1][:, :],
                        wqbuf[m % 2][:, kt * 128:(kt + 1) * 128],
                        y1fm[:, kt * 1024 + 512:kt * 1024 + 1024],
                        start=(kt == 0), stop=(kt == NFO - 1)))
                    if kt == NFO - 1:
                        inc(i2, "PE", f"q_{m}", 1)
            if stage < 2:
                return
            wait(eng, "LC", "b_mask")
            for qc in (0, 1):
                slots = SLOTSETS[qc]
                nsl = len(slots)
                for h in range(NH):
                    wait(eng, "VE", f"qe_{NFO-1}" if (qc == 0 and h == 0)
                         else f"qe_{h}")
                    if not (qc == 0 and h == 0):
                        ph, pqc = (h - 1, qc) if h > 0 else (NH - 1, 0)
                        wait(eng, "VE", f"ao_{ph}_{pqc}")
                    for si in range(nsl + 1):
                        if si < nsl:
                            s = slots[si]
                            if si >= 2:
                                wait(eng, "SC", f"exp_{h}_{qc}_{si-2}")
                            mk = MASKED[qc].get(s)
                            psA = pb[si % 2]
                            i1 = E(lambda s=s, h=h, qc=qc, psA=psA, mk=mk:
                                   eng.matmul(
                                       psA[:, :],
                                       sb_kf.ap()[:, s * 128:(s + 1) * 128],
                                       qT[:, q_off(h, qc):q_off(h, qc) + 512],
                                       start=True, stop=(mk is None)))
                            if mk is not None:
                                i1 = E(lambda mk=mk, psA=psA: eng.matmul(
                                    psA[:, :], sb_u1.ap(),
                                    sb_mask[:, mk * 512:(mk + 1) * 512],
                                    start=False, stop=True))
                            inc(i1, "PE", f"sc_{h}_{qc}_{si}", 1)
                        if si >= 1:
                            pi = si - 1
                            s_prev = slots[pi]
                            wait(eng, "SC", f"exp_{h}_{qc}_{pi}")
                            E(lambda s_prev=s_prev, pi=pi, nsl=nsl: eng.matmul(
                                pb[2][:, :],
                                sb_vt.ap()[:, s_prev * 128:(s_prev + 1) * 128],
                                P_tiles[pi % 2].ap()[:, :],
                                start=(pi == 0), stop=(pi == nsl - 1)))
                            inc(E(lambda pi=pi, nsl=nsl: eng.matmul(
                                pb[3][0:1, :], sb_ones.ap()[:, 0:1],
                                P_tiles[pi % 2].ap()[:, :],
                                start=(pi == 0), stop=(pi == nsl - 1))),
                                "PE", f"av_{h}_{qc}_{pi}", 1)
                    wait(eng, "VE", f"r_{h}_{qc}")
                    if not (qc == 0 and h == 0):
                        ph, pqc = (h - 1, qc) if h > 0 else (NH - 1, 0)
                        wait(eng, "SC", f"rbe_{ph}_{pqc}")
                    inc(E(lambda: eng.matmul(pb[4][:, :],
                                             sb_ones.ap()[0:1, 0:128],
                                             sb_rbf.ap()[0:1, 0:512],
                                             start=True, stop=True)),
                        "PE", f"rb_{h}_{qc}", 1)
            if stage < 3:
                return
            for fo in range(NFO):
                wait(eng, "LA" if fo % 2 == 0 else "LB", f"xf_{fo}")
                if fo == 0:
                    wait(eng, "SC", f"exp_31_1_{len(SLOTSETS[1])-1}")
                    wait(eng, "SC", "rbe_31_1")
                if fo == 1:
                    wait(eng, "VE", "ao_31_1")
                if fo >= 2:
                    wait(eng, "VE", f"x1_{fo-2}")
                for kt in range(NFO):
                    E(lambda fo=fo, kt=kt: eng.matmul(
                        pb[(fo % 2) * 2][:, :],
                        wobuf[fo % 2][:, kt * 128:(kt + 1) * 128],
                        attnout[:, kt * 1024:kt * 1024 + 512],
                        start=(kt == 0), stop=(kt == NFO - 1)))
                    i2 = E(lambda fo=fo, kt=kt: eng.matmul(
                        pb[(fo % 2) * 2 + 1][:, :],
                        wobuf[fo % 2][:, kt * 128:(kt + 1) * 128],
                        attnout[:, kt * 1024 + 512:kt * 1024 + 1024],
                        start=(kt == 0), stop=(kt == NFO - 1)))
                    if kt == NFO - 1:
                        inc(i2, "PE", f"wo_{fo}", 1)
                wait(eng, "VE", f"x1_{fo}")
                wait(eng, "SC", f"sq2_{fo}")
                E(lambda fo=fo: eng.matmul(
                    pb[4][0:1, :], sb_ones.ap()[:, 0:1],
                    x1bf[:, fo * 1024:fo * 1024 + 512],
                    start=(fo == 0), stop=False, skip_group_check=True))
                E(lambda fo=fo: eng.matmul(
                    pb[4][32:33, :], sb_ones.ap()[:, 0:1],
                    x1bf[:, fo * 1024 + 512:fo * 1024 + 1024],
                    start=False, stop=False, skip_group_check=True))
                E(lambda fo=fo: eng.matmul(
                    pb[4][64:65, :], sb_ones.ap()[:, 0:1],
                    sb_sq.ap()[:, 0:512],
                    start=False, stop=(fo == NFO - 1), skip_group_check=True))
                inc(E(lambda fo=fo: eng.matmul(
                    pb[5][0:1, :], sb_ones.ap()[:, 0:1],
                    sb_sq.ap()[:, 512:1024],
                    start=(fo == 0), stop=(fo == NFO - 1))),
                    "PE", f"st2_{fo}", 1)
            wait(eng, "VE", "rows2")
            E(lambda: eng.matmul(pb[0][:, :], sb_ones.ap()[0:1, 0:128],
                                 sb_rbf.ap()[0:1, 0:512], start=True, stop=True))
            E(lambda: eng.matmul(pb[1][:, :], sb_ones.ap()[0:1, 0:128],
                                 sb_rbf.ap()[0:1, 512:1024],
                                 start=True, stop=True))
            E(lambda: eng.matmul(pb[2][:, :], sb_ones.ap()[0:1, 0:128],
                                 sb_rbf.ap()[0:1, 1024:1536],
                                 start=True, stop=True))
            inc(E(lambda: eng.matmul(pb[3][:, :], sb_ones.ap()[0:1, 0:128],
                                     sb_rbf.ap()[0:1, 1536:2048],
                                     start=True, stop=True)),
                "PE", "bc2", 1)
            if stage < 4:
                return
            for fo in range(NFO):
                wait(eng, "VE", f"y2_{fo}")
                E(lambda fo=fo: eng.matmul(
                    pb[4][0:16, :], sb_c1.ap()[:, fo * 16:(fo + 1) * 16],
                    y2r[fo % 2][:, 0:512],
                    start=(fo == 0), stop=(fo == NFO - 1)))
                i2 = E(lambda fo=fo: eng.matmul(
                    pb[5][0:16, :], sb_c1.ap()[:, fo * 16:(fo + 1) * 16],
                    y2r[fo % 2][:, 512:1024],
                    start=(fo == 0), stop=(fo == NFO - 1)))
                if fo == NFO - 1:
                    inc(i2, "PE", "u", 1)
                for i in range(8):
                    gidx = fo * 8 + i
                    if gidx >= 2:
                        pp = gidx - 2
                        wait(eng, "SC", f"tre_{pp // 8}_{pp % 8}")
                    inc(E(lambda fo=fo, i=i: eng.transpose(
                        pt[i % 2][:, :],
                        y2r[fo % 2][:, i * 128:(i + 1) * 128],
                        sb_id.ap())), "PE", f"tr_{fo}_{i}", 1)
            if stage < 5:
                return
            def pe_int(g2):
                t2, j2 = g2 // NB, g2 % NB
                wait(eng, "LA" if j2 % 2 == 0 else "LB", f"c2_{t2}_{j2}")
                if g2 == 0:
                    wait(eng, "VE", "ue")
                if g2 >= 2:
                    p2, q2 = (g2 - 2) // NB, (g2 - 2) % NB
                    wait(eng, "SC", f"ie_{p2}_{q2}")
                inc(E(lambda t2=t2, j2=j2: eng.matmul(
                    pb[2 + j2 % 2][:, 0:256],
                    sb_u.ap()[0:16, t2 * 128:(t2 + 1) * 128],
                    c2b[j2 % 2].ap()[:, :],
                    start=True, stop=True)),
                    "PE", f"int_{t2}_{j2}", 1)

            def pe_ft(tt):
                wait(eng, "VE", f"o_{tt}_{NB-1}")
                for fo in range(NFO):
                    gidx = tt * NFO + fo
                    if gidx >= 2:
                        pp = gidx - 2
                        wait(eng, "VE", f"fa_{pp // NFO}_{pp % NFO}")
                    inc(E(lambda fo=fo: eng.transpose(
                        pt[fo % 2][:, :], acc_bf[0][:, fo * 128:(fo + 1) * 128],
                        sb_id.ap())), "PE", f"ft_{tt}_{fo}", 1)

            for g in range(NTT * NB):
                pe_int(g)
                if g % NB == 2 and g >= NB:
                    pe_ft(g // NB - 1)
            pe_ft(NTT - 1)

        def stream_scalar(eng):
            E, inc, wait = ctx.E, ctx.inc, ctx.wait
            pass
            for tc in range(4):
                wait(eng, "LC", f"b_xc{tc}")
                for kt in range(NFO):
                    if kt >= 2:
                        wait(eng, "PE", f"sx_{tc}_{kt-2}")
                    elif tc >= 1:
                        wait(eng, "PE", f"sx_{tc-1}_{NFO-2+kt}")
                    inc(E(lambda kt=kt, tc=tc: eng.activation(
                        sb_sq.ap()[:, (kt % 2) * 512:(kt % 2) * 512 + 512],
                        xchunk[:, kt * 512:(kt + 1) * 512], AF.Square)),
                        "SC", f"sq_{tc}_{kt}", 1)
                wait(eng, "VE", f"var_{tc}")
                inc(E(lambda: eng.activation(
                    rows_kv[0:1, 1024:1536], rows_kv[0:1, 1536:2048],
                    AF.Sqrt)), "SC", f"srt_{tc}", 1)
                for i in range(4):
                    wait(eng, "PE", f"vt_{tc}_{i}")
                    inc(E(lambda tc=tc, i=i: eng.copy(
                        sb_vt.ap()[:, (tc * 4 + i) * 128:(tc * 4 + i + 1) * 128],
                        pt[i % 2][:, :])), "SC", f"vt_{tc}_{i}", 1)
            if stage < 2:
                if ctx.emit:
                    eng.wait_ge(ctx.sems["SD"], 0)
                return
            wait(eng, "LC", "b_pre")
            for qc in (0, 1):
                slots = SLOTSETS[qc]
                for h in range(NH):
                    ng = n_groups(h)
                    w = 512 // ng
                    for si in range(len(slots)):
                        wait(eng, "PE", f"sc_{h}_{qc}_{si}")
                        if si >= 2:
                            wait(eng, "PE", f"av_{h}_{qc}_{si-2}")
                        psA = pb[si % 2]
                        for g in range(ng):
                            col = COLOF[(h, qc, si, g)]
                            i1 = E(lambda si=si, g=g, w=w, col=col, psA=psA:
                                   eng.activation(
                                       P_tiles[si % 2].ap()[:, g * w:(g + 1) * w],
                                       psA[:, g * w:(g + 1) * w],
                                       AF.Exp,
                                       bias=sb_bias.ap()[:, col:col + 1],
                                       scale=SCALE))
                            if g == ng - 1:
                                inc(i1, "SC", f"exp_{h}_{qc}_{si}", 1)
                    wait(eng, "PE", f"rb_{h}_{qc}")
                    inc(E(lambda: eng.copy(sb_rbc[:, :], pb[4][:, :])),
                        "SC", f"rbe_{h}_{qc}", 1)
            if stage < 3:
                return
            for fo in range(NFO):
                wait(eng, "VE", f"x1_{fo}")
                inc(E(lambda fo=fo: eng.dma_start(out=d_x1s[fo],
                                                  in_=sb_x1o)),
                    "SD", f"x1s_{fo}", 16)
                if fo >= 1:
                    wait(eng, "PE", f"st2_{fo-1}")
                inc(E(lambda fo=fo: eng.activation(
                    sb_sq.ap()[:, :], x1bf[:, fo * 1024:(fo + 1) * 1024],
                    AF.Square)), "SC", f"sq2_{fo}", 1)
            wait(eng, "VE", "var2")
            E(lambda: eng.activation(
                sb_x1o[0:1, 0:512], rows_ln2[0:1, 1024:1536],
                AF.Sqrt))
            inc(E(lambda: eng.activation(
                sb_x1o[0:1, 512:1024], rows_ln2[0:1, 1536:2048],
                AF.Sqrt)), "SC", "srt2", 1)
            wait(eng, "PE", "bc2")
            E(lambda: eng.copy(sb_mbc.ap()[:, 0:512], pb[0][:, :]))
            E(lambda: eng.copy(sb_mbc.ap()[:, 512:1024], pb[1][:, :]))
            E(lambda: eng.copy(sb_mbc.ap()[:, 1024:1536], pb[2][:, :]))
            inc(E(lambda: eng.copy(sb_mbc.ap()[:, 1536:2048], pb[3][:, :])),
                "SC", "bc2e", 1)
            if stage < 4:
                return
            for fo in range(NFO):
                for i in range(8):
                    wait(eng, "PE", f"tr_{fo}_{i}")
                    inc(E(lambda fo=fo, i=i: eng.copy(
                        y2tm[:, i * 4096 + fo * 128: i * 4096 + (fo + 1) * 128],
                        pt[i % 2][:, :])), "SC", f"tre_{fo}_{i}", 1)
            if stage >= 5:
                NG_ALL = NTT * NB

                def sc_ie(g2):
                    t2, j2 = g2 // NB, g2 % NB
                    wait(eng, "PE", f"int_{t2}_{j2}")
                    if g2 >= 2:
                        p2, q2 = (g2 - 2) // NB, (g2 - 2) % NB
                        wait(eng, "VE", f"am_{p2}_{q2}")
                    inc(E(lambda t2=t2, j2=j2: eng.copy(
                        P_tiles[j2 % 2].ap()[:, 0:256],
                        pb[2 + j2 % 2][:, 0:256])),
                        "SC", f"ie_{t2}_{j2}", 1)

                sc_ie(0)
                sc_ie(1)
                for g in range(NG_ALL):
                    tt, j = g // NB, g % NB
                    if g + 2 < NG_ALL:
                        sc_ie(g + 2)
                    wait(eng, "VE", f"hs_{tt}_{j}")
                    inc(E(lambda j=j: eng.activation(
                        sb_hr.ap()[:, j:j + 1], sb_h.ap()[:, j:j + 1],
                        AF.Relu)), "SC", f"hrl_{tt}_{j}", 1)
                    wait(eng, "SC", f"hrl_{tt}_{j}")
                    if g >= 2:
                        p2, q2 = (g - 2) // NB, (g - 2) % NB
                        wait(eng, "VE", f"o_{p2}_{q2}")
                    inc(E(lambda tt=tt, j=j: eng.activation(
                        w2sc[j % 2][:, :], wbuf[j % 2][:, 4096:8192],
                        AF.Copy, scale=sb_hr.ap()[:, j:j + 1])),
                        "SC", f"os_{tt}_{j}", 1)
                    if j == 2 and tt >= 1:
                        wait(eng, "VE", f"fa_{tt-1}_31")
                        inc(E(lambda tt=tt: eng.dma_start(out=d_y[tt - 1],
                                                          in_=otile)),
                            "SD", f"y_{tt-1}", 16)
                wait(eng, "VE", f"fa_{NTT-1}_31")
                inc(E(lambda: eng.dma_start(out=d_y[NTT - 1], in_=otile)),
                    "SD", f"y_{NTT-1}", 16)
                if ctx.emit:
                    eng.wait_ge(ctx.sems["SD"], ctx.ev[("SD", f"y_{NTT-1}")])
            else:
                inc(E(lambda: eng.dma_start(out=d_y[0][:, 0:NCOL],
                                            in_=sb_bias.ap())),
                    "SD", "y_stub", 16)
                if ctx.emit:
                    eng.wait_ge(ctx.sems["SD"], ctx.ev[("SD", "y_stub")])

        def stream_vector(eng):
            E, inc, wait = ctx.E, ctx.inc, ctx.wait
            pass
            for tc in range(4):
                wait(eng, "PE", f"sx_{tc}_{NFO-1}")
                E(lambda: eng.tensor_scalar(rows_kv[0:1, 0:512], pb[0][0:1, :],
                                            1.0 / D, None, ALU.mult))
                E(lambda: eng.tensor_scalar(rows_kv[0:1, 512:1024], pb[1][0:1, :],
                                            1.0 / D, None, ALU.mult))
                E(lambda: eng.tensor_tensor(rows_kv[0:1, 1536:2048],
                                            rows_kv[0:1, 0:512], rows_kv[0:1, 0:512],
                                            ALU.mult))
                E(lambda: eng.tensor_tensor(rows_kv[0:1, 1024:1536],
                                            rows_kv[0:1, 512:1024],
                                            rows_kv[0:1, 1536:2048],
                                            ALU.subtract))
                inc(E(lambda: eng.tensor_scalar(rows_kv[0:1, 1536:2048],
                                                rows_kv[0:1, 1024:1536],
                                                LN_EPS, None, ALU.add)),
                    "VE", f"var_{tc}", 1)
                wait(eng, "SC", f"srt_{tc}")
                E(lambda: eng.tensor_copy(sb_rbf.ap()[0:1, 0:512],
                                          rows_kv[0:1, 0:512]))
                E(lambda: eng.reciprocal(rows_kv[0:1, 0:512],
                                         rows_kv[0:1, 1024:1536]))
                inc(E(lambda: eng.tensor_copy(sb_rbf.ap()[0:1, 512:1024],
                                              rows_kv[0:1, 0:512])),
                    "VE", f"rows_{tc}", 1)
                wait(eng, "PE", f"bc_{tc}")
                E(lambda: eng.tensor_copy(sb_mbc.ap()[:, 0:512], pb[2][:, :]))
                inc(E(lambda: eng.tensor_copy(sb_mbc.ap()[:, 512:1024],
                                              pb[3][:, :])),
                    "VE", f"mbc_{tc}", 1)
                for kt in range(NFO):
                    if tc < 2:
                        dst = y1fm[:, kt * 1024 + tc * 512:
                                   kt * 1024 + tc * 512 + 512]
                    else:
                        dst = sb_ykv[:, (kt % 2) * 512:(kt % 2) * 512 + 512]
                        if kt >= 2:
                            wait(eng, "PE", f"kv_{tc}_{kt-2}")
                    E(lambda kt=kt: eng.tensor_tensor(
                        sb_nrm_kv[:, (kt % 2) * 512:(kt % 2) * 512 + 512],
                        xchunk[:, kt * 512:(kt + 1) * 512],
                        sb_mbc.ap()[:, 0:512], ALU.subtract))
                    inc(E(lambda kt=kt, dst=dst: eng.tensor_tensor(
                        dst,
                        sb_nrm_kv[:, (kt % 2) * 512:(kt % 2) * 512 + 512],
                        sb_mbc.ap()[:, 512:1024], ALU.mult)),
                        "VE", f"y_{tc}_{kt}", 1)
                wait(eng, "PE", f"kv_{tc}_{NFO-1}")
                E(lambda tc=tc: eng.tensor_copy(
                    sb_kf.ap()[:, tc * 512:(tc + 1) * 512], pb[4][:, :]))
                inc(E(lambda: eng.tensor_copy(sb_vfm[:, :], pb[5][:, :])),
                    "VE", f"kve_{tc}", 1)
            if stage < 1:
                return
            for m in range(NFO):
                wait(eng, "PE", f"q_{m}")
                E(lambda m=m: eng.tensor_copy(
                    qT[:, m * 1024:m * 1024 + 512], pb[(m % 2) * 2][:, :]))
                inc(E(lambda m=m: eng.tensor_copy(
                    qT[:, m * 1024 + 512:m * 1024 + 1024],
                    pb[(m % 2) * 2 + 1][:, :])), "VE", f"qe_{m}", 1)
            if stage < 2:
                return
            for qc in (0, 1):
                nsl = len(SLOTSETS[qc])
                for h in range(NH):
                    wait(eng, "PE", f"av_{h}_{qc}_{nsl-1}")
                    E(lambda: eng.reciprocal(sb_rrow.ap()[0:1, :],
                                             pb[3][0:1, :]))
                    inc(E(lambda: eng.tensor_copy(sb_rbf.ap()[0:1, 0:512],
                                                  sb_rrow.ap()[0:1, :])),
                        "VE", f"r_{h}_{qc}", 1)
                    wait(eng, "SC", f"rbe_{h}_{qc}")
                    inc(E(lambda h=h, qc=qc: eng.tensor_tensor(
                        attnout[:, q_off(h, qc):q_off(h, qc) + 512],
                        pb[2][:, :], sb_rbc[:, :], ALU.mult)),
                        "VE", f"ao_{h}_{qc}", 1)
            if stage < 3:
                return
            for fo in range(NFO):
                wait(eng, "PE", f"wo_{fo}")
                wait(eng, "LA" if fo % 2 == 0 else "LB", f"xf_{fo}")
                if fo >= 1:
                    wait(eng, "SD", f"x1s_{fo-1}")
                E(lambda fo=fo: eng.tensor_tensor(
                    sb_x1o[:, 0:512], pb[(fo % 2) * 2][:, :],
                    sb_x1f[:, 0:512], ALU.add))
                E(lambda fo=fo: eng.tensor_tensor(
                    sb_x1o[:, 512:1024], pb[(fo % 2) * 2 + 1][:, :],
                    sb_x1f[:, 512:1024], ALU.add))
                inc(E(lambda fo=fo: eng.tensor_copy(
                    x1bf[:, fo * 1024:(fo + 1) * 1024], sb_x1o[:, :])),
                    "VE", f"x1_{fo}", 1)
            wait(eng, "PE", f"st2_{NFO-1}")
            wait(eng, "SD", f"x1s_{NFO-1}")
            x1fr = sb_x1f[0:1, :]
            x1or = sb_x1o[0:1, :]
            E(lambda: eng.tensor_scalar(rows_ln2[0:1, 0:512], pb[4][0:1, :],
                                        1.0 / D, None, ALU.mult))
            E(lambda: eng.tensor_scalar(rows_ln2[0:1, 512:1024], pb[4][32:33, :],
                                        1.0 / D, None, ALU.mult))
            E(lambda: eng.tensor_scalar(rows_ln2[0:1, 1024:1536], pb[4][64:65, :],
                                        1.0 / D, None, ALU.mult))
            E(lambda: eng.tensor_scalar(rows_ln2[0:1, 1536:2048], pb[5][0:1, :],
                                        1.0 / D, None, ALU.mult))
            E(lambda: eng.tensor_tensor(x1fr[:, 0:512],
                                        rows_ln2[0:1, 0:512], rows_ln2[0:1, 0:512],
                                        ALU.mult))
            E(lambda: eng.tensor_tensor(x1fr[:, 512:1024],
                                        rows_ln2[0:1, 512:1024],
                                        rows_ln2[0:1, 512:1024], ALU.mult))
            E(lambda: eng.tensor_tensor(x1or[:, 0:512],
                                        rows_ln2[0:1, 1024:1536],
                                        x1fr[:, 0:512], ALU.subtract))
            E(lambda: eng.tensor_tensor(x1or[:, 512:1024],
                                        rows_ln2[0:1, 1536:2048],
                                        x1fr[:, 512:1024],
                                        ALU.subtract))
            E(lambda: eng.tensor_scalar(rows_ln2[0:1, 1024:1536],
                                        x1or[:, 0:512],
                                        LN_EPS, None, ALU.add))
            inc(E(lambda: eng.tensor_scalar(rows_ln2[0:1, 1536:2048],
                                            x1or[:, 512:1024],
                                            LN_EPS, None, ALU.add)),
                "VE", "var2", 1)
            wait(eng, "SC", "srt2")
            E(lambda: eng.reciprocal(x1fr[:, 0:512], x1or[:, 0:512]))
            E(lambda: eng.reciprocal(x1fr[:, 512:1024], x1or[:, 512:1024]))
            E(lambda: eng.tensor_copy(sb_rbf.ap()[0:1, 0:1024],
                                      rows_ln2[0:1, 0:1024]))
            inc(E(lambda: eng.tensor_copy(sb_rbf.ap()[0:1, 1024:2048],
                                          x1fr[:, 0:1024])),
                "VE", "rows2", 1)
            if stage < 4:
                return
            wait(eng, "SC", "bc2e")
            for fo in range(NFO):
                if fo >= 2:
                    wait(eng, "PE", f"tr_{fo-2}_7")
                E(lambda fo=fo: eng.tensor_tensor(
                    sb_nrm_t[:, :], x1bf[:, fo * 1024:(fo + 1) * 1024],
                    sb_mbc.ap()[:, 0:1024], ALU.subtract))
                inc(E(lambda fo=fo: eng.tensor_tensor(
                    y2r[fo % 2][:, :], sb_nrm_t[:, :],
                    sb_mbc.ap()[:, 1024:2048], ALU.mult)),
                    "VE", f"y2_{fo}", 1)
            wait(eng, "PE", "u")
            E(lambda: eng.tensor_copy(sb_u.ap()[0:16, 0:512], pb[4][0:16, :]))
            inc(E(lambda: eng.tensor_copy(sb_u.ap()[0:16, 512:1024],
                                          pb[5][0:16, :])), "VE", "ue", 1)
            if stage < 5:
                return
            NG_ALL = NTT * NB

            def ve_argmax(g2):
                t2, j2 = g2 // NB, g2 % NB
                wait(eng, "SC", f"ie_{t2}_{j2}")
                if g2 == 0:
                    wait(eng, "LC", "b_iota")
                inc(E(lambda j2=j2: eng.reduce_max(
                    sb_m8.ap()[:, 0:1], P_tiles[j2 % 2].ap()[:, 0:256],
                    axis=mybir.AxisListType.X)),
                    "VE", f"amx_{t2}_{j2}", 1)
                wait(eng, "VE", f"amx_{t2}_{j2}")
                E(lambda j2=j2: eng.tensor_scalar(
                    sb_pt1.ap()[:, 256:512], P_tiles[j2 % 2].ap()[:, 0:256],
                    sb_m8.ap()[:, 0:1], None, ALU.is_ge))
                E(lambda: eng.tensor_tensor(
                    sb_sq.ap()[:, 0:256], sb_pt1.ap()[:, 256:512],
                    sb_pt0.ap()[:, 256:512], ALU.mult))
                inc(E(lambda: eng.reduce_max(
                    sb_m8.ap()[:, 1:2], sb_sq.ap()[:, 0:256],
                    axis=mybir.AxisListType.X)),
                    "VE", f"amr_{t2}_{j2}", 1)
                wait(eng, "VE", f"amr_{t2}_{j2}")
                inc(E(lambda g2=g2: eng.tensor_copy(
                    sb_i8.ap()[:, g2 % 4:g2 % 4 + 1],
                    sb_m8.ap()[:, 1:2])),
                    "VE", f"am_{t2}_{j2}", 1)

            def ve_add(p):
                pt_, pj = p // NB, p % NB
                wait(eng, "SC", f"os_{pt_}_{pj}")
                if pj == 0:
                    if pt_ >= 1:
                        wait(eng, "PE", f"ft_{pt_-1}_{NFO-1}")
                    inc(E(lambda: eng.tensor_copy(
                        acc_bf[1][:, :], w2sc[0][:, :])),
                        "VE", f"o_{pt_}_0", 1)
                else:
                    if pj == 1 and pt_ >= 1:
                        wait(eng, "PE", f"ft_{pt_-1}_{NFO-1}")
                    inc(E(lambda pj=pj: eng.tensor_tensor(
                        acc_bf[(pj + 1) % 2][:, :], w2sc[pj % 2][:, :],
                        acc_bf[pj % 2][:, :], ALU.add)),
                        "VE", f"o_{pt_}_{pj}", 1)
                if pj == NB - 1:
                    for fo in range(NFO):
                        wait(eng, "PE", f"ft_{pt_}_{fo}")
                        wait(eng, "LC", f"b_x1c{pt_}")
                        if pt_ >= 1 and fo == 0:
                            wait(eng, "SD", f"y_{pt_-1}")
                        inc(E(lambda pt_=pt_, fo=fo: eng.tensor_tensor(
                            otile[:, fo * 128:(fo + 1) * 128],
                            pt[fo % 2][:, :],
                            x1c[:, fo * 128:(fo + 1) * 128], ALU.add)),
                            "VE", f"fa_{pt_}_{fo}", 1)

            ve_argmax(0)
            ve_argmax(1)
            for g in range(NG_ALL):
                tt, j = g // NB, g % NB
                la = g + 2 < NG_ALL
                if la:
                    t2, j2 = (g + 2) // NB, (g + 2) % NB
                    wait(eng, "SC", f"ie_{t2}_{j2}")
                    inc(E(lambda j2=j2: eng.reduce_max(
                        sb_m8.ap()[:, 0:1], P_tiles[j2 % 2].ap()[:, 0:256],
                        axis=mybir.AxisListType.X)),
                        "VE", f"amx_{t2}_{j2}", 1)
                wait(eng, "GA", f"g_{tt}_{j}")
                inc(E(lambda tt=tt, j=j: eng.scalar_tensor_tensor(
                    out=prod[:, :], in0=wbuf[j % 2][:, 0:4096], scalar=1.0,
                    in1=y2tm[:, tt * 4096:(tt + 1) * 4096],
                    op0=ALU.mult, op1=ALU.mult,
                    accum_out=sb_h.ap()[:, j:j + 1])),
                    "VE", f"hs_{tt}_{j}", 1)
                if la:
                    wait(eng, "VE", f"amx_{t2}_{j2}")
                    E(lambda j2=j2: eng.tensor_scalar(
                        sb_pt1.ap()[:, 256:512], P_tiles[j2 % 2].ap()[:, 0:256],
                        sb_m8.ap()[:, 0:1], None, ALU.is_ge))
                    E(lambda: eng.tensor_tensor(
                        sb_sq.ap()[:, 0:256], sb_pt1.ap()[:, 256:512],
                        sb_pt0.ap()[:, 256:512], ALU.mult))
                    inc(E(lambda: eng.reduce_max(
                        sb_m8.ap()[:, 1:2], sb_sq.ap()[:, 0:256],
                        axis=mybir.AxisListType.X)),
                        "VE", f"amr_{t2}_{j2}", 1)
                if g >= 1:
                    ve_add(g - 1)
                if la:
                    wait(eng, "VE", f"amr_{t2}_{j2}")
                    inc(E(lambda g=g: eng.tensor_copy(
                        sb_i8.ap()[:, (g + 2) % 4:(g + 2) % 4 + 1],
                        sb_m8.ap()[:, 1:2])),
                        "VE", f"am_{t2}_{j2}", 1)
            ve_add(NG_ALL - 1)

        ctx.emit = False
        stream_sync(None)
        stream_gpsimd(None)
        stream_tensor(None)
        stream_scalar(None)
        stream_vector(None)

        ctx.emit = True

        @block.sync
        def _(eng):
            stream_sync(eng)

        @block.gpsimd
        def _(eng):
            stream_gpsimd(eng)

        @block.tensor
        def _(eng):
            stream_tensor(eng)

        @block.scalar
        def _(eng):
            stream_scalar(eng)

        @block.vector
        def _(eng):
            stream_vector(eng)

    return nc


def _bf(x):
    return np.ascontiguousarray(x).astype(ml_dtypes.bfloat16)


def kernel(L0_attn_hidden_states, L0_attn_Wq, L0_attn_Wkv, L0_attn_Wo,
           L0_ffn_c1_w, L0_ffn_c2_w, L0_ffn_w1, L0_ffn_w2t, _debug=False,
           _stage=99, _fsub=99):
    X = np.asarray(L0_attn_hidden_states, dtype=np.float32)
    Wq = np.asarray(L0_attn_Wq, dtype=np.float32)
    Wkv = np.asarray(L0_attn_Wkv, dtype=np.float32)
    Wo = np.asarray(L0_attn_Wo, dtype=np.float32)
    c1 = np.asarray(L0_ffn_c1_w, dtype=np.float32)
    c2 = np.asarray(L0_ffn_c2_w, dtype=np.float32)
    w1 = np.asarray(L0_ffn_w1, dtype=np.float32)
    w2t = np.asarray(L0_ffn_w2t, dtype=np.float32)

    wq_t = _bf(Wq.reshape(NFO, 128, NFO, 128).transpose(2, 1, 0, 3)
               .reshape(NFO, 128, D))
    wo_t = _bf(Wo.reshape(NFO, 128, NFO, 128).transpose(2, 1, 0, 3)
               .reshape(NFO, 128, D))
    wkv_t = _bf(Wkv.reshape(NFO, 128, 256).transpose(1, 0, 2).reshape(128, 8192))
    c1t = _bf(c1.T.reshape(NFO, 128, 16).transpose(1, 0, 2).reshape(128, 512))
    c2t = _bf(c2.T)
    w1w2 = np.concatenate([_bf(w1), _bf(w2t)], axis=1)
    mask4 = _host_masks()
    u1 = np.triu(np.ones((128, 128))).astype(ml_dtypes.bfloat16)
    ident = np.eye(128).astype(ml_dtypes.bfloat16)
    ones = np.ones((128, 128)).astype(ml_dtypes.bfloat16)
    iota = np.broadcast_to(np.arange(256, dtype=np.float32), (128, 256)) \
        .astype(ml_dtypes.bfloat16)

    nc = build_program(debug=_debug, stage=_stage, fsub=_fsub)
    in_maps = []
    for c in range(8):
        perm, _, _ = _core_geom(c)
        b = c // 2
        xp = X[b][perm]
        in_maps.append({
            "xfmt": _bf(xp.T.reshape(NFO, 128, S)),
            "xf32": np.ascontiguousarray(
                xp[:NTOK].T.reshape(NFO, 128, NTOK)).astype(np.float32),
            "wq_t": wq_t, "wo_t": wo_t, "wkv_t": wkv_t,
            "c1t": c1t, "c2t": c2t, "w1w2": w1w2,
            "biasm": _host_biasmat(c), "mask4": mask4,
            "u1": u1, "ident": ident, "onesb": ones, "iota": iota,
        })

    res = run_bass_kernel_spmd(nc, in_maps, core_ids=list(range(8)))

    out = np.empty((B, S, D), dtype=np.float32)
    for c in range(8):
        perm, _, _ = _core_geom(c)
        b = c // 2
        yc = np.asarray(res.results[c]["y"])
        toks = yc.reshape(NTT, 128, NFO, 128).transpose(0, 3, 2, 1) \
                 .reshape(NTOK, D)
        out[b][perm[:NTOK]] = toks
    if _debug:
        return out, res
    return out


# version 09
# speedup vs baseline: 1.7213x; 1.1847x over previous
import math
import numpy as np
import ml_dtypes

import concourse.bass as bass
import concourse.mybir as mybir
from concourse.bass import IndirectOffsetOnAxis
from concourse.bass_utils import run_bass_kernel_spmd

F32 = mybir.dt.float32
BF16 = mybir.dt.bfloat16
U8 = mybir.dt.uint8
U32 = mybir.dt.uint32
ALU = mybir.AluOpType
AF = mybir.ActivationFunctionType

IDX_ZERO = False
DBG_H = False
GLIMIT = 10**9
PLAIN_GATHER = False

B, S, D, NH, HD = 4, 2048, 4096, 32, 128
INTER, SPR, NB, RANK = 14336, 256, 56, 16
SCALE = HD ** -0.5
NTOK = 1024
NKT = 16
NFO = 32
NTT = 8
LN_EPS = 1e-5
MASK_NEG = -50000.0
KILL = -1e9

_start = 2.0 ** (-(2.0 ** -(math.log2(NH) - 3)))
SLOPES = [float(np.float32(_start ** (i + 1))) for i in range(NH)]

SLOTSETS = {0: [0, 1, 2, 3, 8, 9, 10, 11], 1: list(range(16))}
MASKED = {0: {0: 0, 1: 1, 2: 2, 3: 3}, 1: {4: 0, 5: 1, 6: 2, 7: 3}}


def n_groups(h):
    sh = SLOPES[h]
    if sh * 256.0 <= 66.0:
        return 1
    if sh * 128.0 <= 66.0:
        return 2
    return 4


def _core_geom(c):
    half = c % 2
    if half == 0:
        perm = np.concatenate([np.arange(0, 512), np.arange(1536, 2048),
                               np.arange(512, 1536)])
        q0s = (0, 1536)
    else:
        perm = np.concatenate([np.arange(512, 1536), np.arange(0, 512),
                               np.arange(1536, 2048)])
        q0s = (512, 1024)
    tile_orig = [int(perm[s * 128]) for s in range(NKT)]
    return perm, q0s, tile_orig


_BIAS_COLS = []
COLOF = {}
for _h in range(NH):
    for _qc in (0, 1):
        for _si in range(len(SLOTSETS[_qc])):
            for _g in range(n_groups(_h)):
                COLOF[(_h, _qc, _si, _g)] = len(_BIAS_COLS)
                _BIAS_COLS.append((_h, _qc, _si, _g))
NCOL = len(_BIAS_COLS)


def _host_biasmat(c):
    _, q0s, torig = _core_geom(c)
    bm = np.zeros((128, NCOL), dtype=np.float64)
    p = np.arange(128, dtype=np.float64)
    for col, (h, qc, si, g) in enumerate(_BIAS_COLS):
        s = SLOTSETS[qc][si]
        k0, q0 = torig[s], q0s[qc]
        if k0 > q0 + 511:
            bm[:, col] = KILL
        else:
            w = 512 // n_groups(h)
            qt = q0 + w * g + w // 2
            bm[:, col] = SLOPES[h] * (k0 + p - qt)
    return bm.astype(np.float32)


def _host_masks():
    m = np.zeros((4, 128, 512), dtype=np.float32)
    c = np.arange(128)[:, None]
    j = np.arange(512)[None, :]
    for d in range(4):
        m[d] = np.where(c > j - 128 * d, MASK_NEG, 0.0)
    return m.astype(ml_dtypes.bfloat16)


class Ctx:
    SEMS = ("LD", "LA", "LB", "LC", "SD", "GA", "PE", "VE", "SC")

    def __init__(self):
        self.emit = False
        self.cnt = {s: 0 for s in self.SEMS}
        self.ev = {}
        self.sems = {}

    def inc(self, inst, sem, name, n):
        if self.emit:
            assert inst is not None, (sem, name)
            inst.then_inc(self.sems[sem], n)
        else:
            self.cnt[sem] += n
            self.ev[(sem, name)] = self.cnt[sem]

    def wait(self, eng, sem, name):
        if self.emit:
            eng.wait_ge(self.sems[sem], self.ev[(sem, name)])

    def E(self, thunk):
        return thunk() if self.emit else None

    def mark(self, sem, name):
        if not self.emit:
            self.ev[(sem, name)] = self.cnt[sem]


def build_program(debug=False, stage=99, fsub=99):
    nc = bass.Bass("TRN2", target_bir_lowering=False)

    d_xfmt = nc.dram_tensor("xfmt", [NFO, 128, S], BF16, kind="ExternalInput").ap()
    d_xf32 = nc.dram_tensor("xf32", [NFO, 128, NTOK], F32, kind="ExternalInput").ap()
    d_wq = nc.dram_tensor("wq_t", [NFO, 128, D], BF16, kind="ExternalInput").ap()
    d_wo = nc.dram_tensor("wo_t", [NFO, 128, D], BF16, kind="ExternalInput").ap()
    d_wkv = nc.dram_tensor("wkv_t", [128, 8192], BF16, kind="ExternalInput").ap()
    d_c1 = nc.dram_tensor("c1t", [128, 512], BF16, kind="ExternalInput").ap()
    d_c2 = nc.dram_tensor("c2t", [16, INTER], BF16, kind="ExternalInput").ap()
    d_w12 = nc.dram_tensor("w1w2", [INTER, 8192], BF16, kind="ExternalInput").ap()
    d_bias = nc.dram_tensor("biasm", [128, NCOL], F32, kind="ExternalInput").ap()
    d_mask = nc.dram_tensor("mask4", [4, 128, 512], BF16, kind="ExternalInput").ap()
    d_u1 = nc.dram_tensor("u1", [128, 128], BF16, kind="ExternalInput").ap()
    d_id = nc.dram_tensor("ident", [128, 128], BF16, kind="ExternalInput").ap()
    d_ones = nc.dram_tensor("onesb", [128, 128], BF16, kind="ExternalInput").ap()
    d_iota = nc.dram_tensor("iota", [128, 256], BF16, kind="ExternalInput").ap()
    d_x1s = nc.dram_tensor("x1s", [NFO, 128, NTOK], F32, kind="Internal").ap()
    d_y = nc.dram_tensor("y", [NTT, 128, D], F32, kind="ExternalOutput").ap()
    dbg = {}
    if debug:
        for nm, shp, dt in [("dbg_y1", [128, NFO * NTOK], BF16),
                            ("dbg_q", [128, NH * NTOK], BF16),
                            ("dbg_kf", [128, NKT * 128], BF16),
                            ("dbg_vt", [128, NKT * 128], BF16),
                            ("dbg_ao", [128, NH * NTOK], BF16),
                            ("dbg_x1", [128, NFO * NTOK], BF16),
                            ("dbg_u", [16, NTOK], BF16),
                            ("dbg_y2", [128, NTT * D], BF16),
                            ("dbg_h", [NTT, 128, 56], F32)]:
            dbg[nm] = nc.dram_tensor(nm, shp, dt, kind="ExternalOutput").ap()

    ctx = Ctx()

    from contextlib import ExitStack
    with ExitStack() as _st:
        def _sb(name, shape, dt):
            return _st.enter_context(nc.sbuf_tensor(name, shape, dt))

        def _ps(name, shape, dt):
            return _st.enter_context(nc.psum_tensor(name, shape, dt))

        arA = _sb("arA", [128, 65536], U8)
        arB = _sb("arB", [128, 32768], BF16)
        arC = _sb("arC", [128, 49152], U8)
        sb_bias = _sb("sb_bias", [128, NCOL], F32)
        sb_u1 = _sb("sb_u1", [128, 128], BF16)
        sb_id = _sb("sb_id", [128, 128], BF16)
        sb_ones = _sb("sb_ones", [128, 128], BF16)
        sb_kf = _sb("sb_kf", [128, NKT * 128], BF16)
        sb_vt = _sb("sb_vt", [128, NKT * 128], BF16)
        sb_c1 = _sb("sb_c1", [128, 512], BF16)
        sb_pt0 = _sb("sb_pt0", [128, 512], BF16)
        sb_pt1 = _sb("sb_pt1", [128, 512], BF16)
        sb_mbc = _sb("sb_mbc", [128, 2048], BF16)
        sb_sq = _sb("sb_sq", [128, 1024], BF16)
        sb_rrow = _sb("sb_rrow", [1, 512], F32)
        sb_rbf = _sb("sb_rbf", [1, 2048], BF16)
        sb_u = _sb("sb_u", [16, 1024], BF16)
        sb_c2b0 = _sb("sb_c2b0", [16, 256], BF16)
        sb_c2b1 = _sb("sb_c2b1", [16, 256], BF16)
        sb_m8 = _sb("sb_m8", [128, 8], F32)
        sb_i8 = _sb("sb_i8", [128, 16], U32)
        sb_h = _sb("sb_h", [128, 64], F32)
        sb_hr = _sb("sb_hr", [128, 64], F32)
        pb0 = _ps("pb0", [128, 512], F32)
        pb1 = _ps("pb1", [128, 512], F32)
        pb2 = _ps("pb2", [128, 512], F32)
        pb3 = _ps("pb3", [128, 512], F32)
        pb4 = _ps("pb4", [128, 512], F32)
        pb5 = _ps("pb5", [128, 512], F32)
        pt0 = _ps("pt0", [128, 128], BF16)
        pt1 = _ps("pt1", [128, 128], BF16)
        sLD = _st.enter_context(nc.semaphore("sLD"))
        sLA = _st.enter_context(nc.semaphore("sLA"))
        sLB = _st.enter_context(nc.semaphore("sLB"))
        sLC = _st.enter_context(nc.semaphore("sLC"))
        sSD = _st.enter_context(nc.semaphore("sSD"))
        sGA = _st.enter_context(nc.semaphore("sGA"))
        sPE = _st.enter_context(nc.semaphore("sPE"))
        sVE = _st.enter_context(nc.semaphore("sVE"))
        sSC = _st.enter_context(nc.semaphore("sSC"))
        block = _st.enter_context(nc.Block())
        ctx.sems = {"LD": sLD, "LA": sLA, "LB": sLB, "LC": sLC,
                    "SD": sSD, "GA": sGA,
                    "PE": sPE, "VE": sVE, "SC": sSC}
        pb = [pb0, pb1, pb2, pb3, pb4, pb5]
        pt = [pt0, pt1]

        vA = arA.ap().bitcast(BF16)
        y1fm = vA
        attnout = vA
        y2tm = vA
        vB = arB.ap()
        qT = vB
        x1bf = vB
        wbuf = [vB[:, 0:8192], vB[:, 8192:16384]]
        x1c = arB.ap().bitcast(F32)[:, 8192:12288]
        otile = arB.ap().bitcast(F32)[:, 12288:16384]
        vC16 = arC.ap().bitcast(BF16)
        xchunk = vC16[:, 0:16384]
        wkv_sb = vC16[:, 16384:24576]
        wqbuf = [vC16[:, 0:4096], vC16[:, 4096:8192]]
        wobuf = wqbuf
        acc_bf = [vC16[:, 0:4096], vC16[:, 4096:8192]]
        w1buf = [vB[:, 0:4096], vB[:, 4096:8192]]
        w2buf = [vB[:, 8192:12288], vB[:, 12288:16384]]
        w2sc = vC16[:, 16384:20480]
        sc_scr = vC16[:, 12288:16384]
        prod = [vC16[:, 20480:24576], vC16[:, 8192:12288]]
        sb_ykv = vB[:, 0:1024]
        sb_nrm_kv = vB[:, 1024:2048]
        sb_vfm = vB[:, 2048:2560]
        vBf32 = arB.ap().bitcast(F32)
        rows_kv = vBf32[0:1, 1280:3328]
        vCf32 = arC.ap().bitcast(F32)
        sb_x1o = vCf32[:, 4096:5120]
        sb_x1f = vCf32[:, 5120:6144]
        rows_ln2 = vCf32[0:1, 6144:8192]
        y2r = [vC16[:, 8192:9216], vC16[:, 9216:10240]]
        sb_nrm_t = vC16[:, 10240:11264]
        sb_mask = vC16[:, 14336:16384]
        sb_rbc = vC16[:, 6144:6656]

        P_tiles = [sb_pt0, sb_pt1]
        c2b = [sb_c2b0, sb_c2b1]

        def q_off(h, qc):
            return h * 1024 + qc * 512

        def stream_sync(eng):
            E, inc, wait = ctx.E, ctx.inc, ctx.wait
            for nm, dst, src in [("c_bias", sb_bias.ap(), d_bias),
                                 ("c_u1", sb_u1.ap(), d_u1),
                                 ("c_id", sb_id.ap(), d_id),
                                 ("c_ones", sb_ones.ap(), d_ones),
                                 ("c_c1", sb_c1.ap(), d_c1),
                                 ("c_wkv", wkv_sb, d_wkv)]:
                inc(E(lambda dst=dst, src=src: eng.dma_start(out=dst, in_=src)),
                    "LC", nm, 16)
            ctx.mark("LC", "b_pre")
            wait(eng, "LC", "b_pre")
            for tc in range(4):
                for kt in range(NFO):
                    if tc >= 1:
                        wait(eng, "VE", f"y_{tc-1}_{kt}")
                    inc(E(lambda tc=tc, kt=kt: eng.dma_start(
                        out=xchunk[:, kt * 512:(kt + 1) * 512],
                        in_=d_xfmt[kt][:, tc * 512:(tc + 1) * 512])),
                        "LC", f"xc_{tc}_{kt}", 16)
                ctx.mark("LC", f"b_xc{tc}")
                wait(eng, "LC", f"b_xc{tc}")
            if stage < 1:
                return
            wait(eng, "VE", "y_3_31")
            for dd in range(4):
                inc(E(lambda dd=dd: eng.dma_start(
                    out=sb_mask[:, dd * 512:(dd + 1) * 512],
                    in_=d_mask[dd])), "LC", f"c_mask{dd}", 16)
            ctx.mark("LC", "b_mask")
            wait(eng, "LC", "b_mask")
            for m in range(NFO):
                if m >= 2:
                    wait(eng, "PE", f"q_{m-2}")
                inc(E(lambda m=m: eng.dma_start(out=wqbuf[m % 2],
                                                in_=d_wq[m])),
                    "LA" if m % 2 == 0 else "LB", f"wq_{m}", 16)
            if stage < 3:
                return
            for fo in range(NFO):
                if fo == 0:
                    wait(eng, "VE", "ao_31_1")
                if fo >= 2:
                    wait(eng, "PE", f"wo_{fo-2}")
                inc(E(lambda fo=fo: eng.dma_start(out=wobuf[fo % 2],
                                                  in_=d_wo[fo])),
                    "LA" if fo % 2 == 0 else "LB", f"wwo_{fo}", 16)
                if fo >= 1:
                    wait(eng, "VE", f"x1_{fo-1}")
                inc(E(lambda fo=fo: eng.dma_start(out=sb_x1f,
                                                  in_=d_xf32[fo])),
                    "LA" if fo % 2 == 0 else "LB", f"xf_{fo}", 16)
            if stage < 5:
                return
            wait(eng, "PE", f"av_31_1_{len(SLOTSETS[1])-1}")
            inc(E(lambda: eng.dma_start(out=sb_pt0.ap()[:, 256:512],
                                        in_=d_iota)), "LC", "iota", 16)
            ctx.mark("LC", "b_iota")
            wait(eng, "LC", "b_iota")
            def sy_c2(g2):
                t2, j2 = g2 // NB, g2 % NB
                if g2 >= 2:
                    p2, q2 = (g2 - 2) // NB, (g2 - 2) % NB
                    wait(eng, "PE", f"int_{p2}_{q2}")
                inc(E(lambda t2=t2, j2=j2: eng.dma_start(
                    out=c2b[j2 % 2].ap(),
                    in_=d_c2[:, j2 * 256:(j2 + 1) * 256])),
                    "LA" if j2 % 2 == 0 else "LB", f"c2_{t2}_{j2}", 16)

            def sy_x1c(tt):
                if tt == 0:
                    wait(eng, "VE", "ue")
                    wait(eng, "SD", f"x1s_{NFO-1}")
                else:
                    wait(eng, "VE", f"fa_{tt-1}_31")
                for fo in range(NFO):
                    inc(E(lambda tt=tt, fo=fo: eng.dma_start(
                        out=x1c[:, fo * 128:(fo + 1) * 128],
                        in_=d_x1s[fo][:, tt * 128:(tt + 1) * 128])),
                        "LC", f"x1c_{tt}_{fo}", 16)
                ctx.mark("LC", f"b_x1c{tt}")
                wait(eng, "LC", f"b_x1c{tt}")

            for g in range(NTT * NB):
                sy_c2(g)
                if g % NB == 2:
                    sy_x1c(g // NB)
            if NB <= 2:
                raise AssertionError

        def stream_gpsimd(eng):
            E, inc, wait = ctx.E, ctx.inc, ctx.wait
            if debug:
                wait(eng, "VE", "y_1_31")
                inc(E(lambda: eng.dma_start(out=dbg["dbg_y1"], in_=y1fm)),
                    "GA", "dbg_y1", 16)
                wait(eng, "VE", "kve_3")
                wait(eng, "SC", "vt_3_3")
                inc(E(lambda: eng.dma_start(out=dbg["dbg_kf"], in_=sb_kf.ap())),
                    "GA", "dbg_kf", 16)
                inc(E(lambda: eng.dma_start(out=dbg["dbg_vt"], in_=sb_vt.ap())),
                    "GA", "dbg_vt", 16)
                wait(eng, "VE", "qe_31")
                inc(E(lambda: eng.dma_start(out=dbg["dbg_q"], in_=qT)),
                    "GA", "dbg_q", 16)
                wait(eng, "VE", "ao_31_1")
                inc(E(lambda: eng.dma_start(out=dbg["dbg_ao"], in_=attnout)),
                    "GA", "dbg_ao", 16)
                wait(eng, "VE", "x1_31")
                inc(E(lambda: eng.dma_start(out=dbg["dbg_x1"], in_=x1bf)),
                    "GA", "dbg_x1", 16)
                wait(eng, "VE", "ue")
                inc(E(lambda: eng.dma_start(out=dbg["dbg_u"], in_=sb_u.ap())),
                    "GA", "dbg_u", 16)
                wait(eng, "SC", "tre_31_7")
                inc(E(lambda: eng.dma_start(out=dbg["dbg_y2"], in_=y2tm)),
                    "GA", "dbg_y2", 16)
            if stage < 5:
                return
            NG_ALL = NTT * NB
            for g in range(NG_ALL):
                tt, j = g // NB, g % NB
                wait(eng, "VE", f"am_{tt}_{j}")
                if g >= 2:
                    pt_, pj = (g - 2) // NB, (g - 2) % NB
                    wait(eng, "VE", f"pr_{pt_}_{pj}")
                inc(E(lambda tt=tt, j=j, g=g: eng.indirect_dma_start(
                    w1buf[j % 2],
                    None,
                    d_w12,
                    IndirectOffsetOnAxis(
                        ap=sb_i8.ap()[:, g % 4:g % 4 + 1], axis=0),
                    element_offset=j * 256 * 8192)),
                    "GA", f"g_{tt}_{j}", 16)
                if g >= 2:
                    wait(eng, "VE", f"ws_{pt_}_{pj}")
                inc(E(lambda tt=tt, j=j, g=g: eng.indirect_dma_start(
                    w2buf[j % 2],
                    None,
                    d_w12,
                    IndirectOffsetOnAxis(
                        ap=sb_i8.ap()[:, g % 4:g % 4 + 1], axis=0),
                    element_offset=j * 256 * 8192 + 4096)),
                    "GA", f"g2_{tt}_{j}", 16)

        def stream_tensor(eng):
            E, inc, wait = ctx.E, ctx.inc, ctx.wait
            wait(eng, "LC", "b_pre")
            for tc in range(4):
                wait(eng, "LC", f"b_xc{tc}")
                for kt in range(NFO):
                    E(lambda tc=tc, kt=kt: eng.matmul(
                        pb[0][0:1, :], sb_ones.ap()[:, 0:1],
                        xchunk[:, kt * 512:(kt + 1) * 512],
                        start=(kt == 0), stop=(kt == NFO - 1)))
                    wait(eng, "SC", f"sq_{tc}_{kt}")
                    inc(E(lambda kt=kt: eng.matmul(
                        pb[1][0:1, :], sb_ones.ap()[:, 0:1],
                        sb_sq.ap()[:, (kt % 2) * 512:(kt % 2) * 512 + 512],
                        start=(kt == 0), stop=(kt == NFO - 1))),
                        "PE", f"sx_{tc}_{kt}", 1)
                wait(eng, "VE", f"rows_{tc}")
                E(lambda: eng.matmul(pb[2][:, :], sb_ones.ap()[0:1, 0:128],
                                     sb_rbf.ap()[0:1, 0:512],
                                     start=True, stop=True))
                inc(E(lambda: eng.matmul(pb[3][:, :], sb_ones.ap()[0:1, 0:128],
                                         sb_rbf.ap()[0:1, 512:1024],
                                         start=True, stop=True)),
                    "PE", f"bc_{tc}", 1)
                for kt in range(NFO):
                    wait(eng, "VE", f"y_{tc}_{kt}")
                    if tc < 2:
                        dst = y1fm[:, kt * 1024 + tc * 512:
                                   kt * 1024 + tc * 512 + 512]
                    else:
                        dst = sb_ykv[:, (kt % 2) * 512:(kt % 2) * 512 + 512]
                    E(lambda kt=kt, dst=dst: eng.matmul(
                        pb[4][:, :], wkv_sb[:, kt * 256:kt * 256 + 128], dst,
                        start=(kt == 0), stop=(kt == NFO - 1)))
                    inc(E(lambda kt=kt, dst=dst: eng.matmul(
                        pb[5][:, :], wkv_sb[:, kt * 256 + 128:kt * 256 + 256],
                        dst, start=(kt == 0), stop=(kt == NFO - 1))),
                        "PE", f"kv_{tc}_{kt}", 1)
                wait(eng, "VE", f"kve_{tc}")
                for i in range(4):
                    if i >= 2:
                        wait(eng, "SC", f"vt_{tc}_{i-2}")
                    elif tc >= 1:
                        wait(eng, "SC", f"vt_{tc-1}_{2+i}")
                    inc(E(lambda i=i: eng.transpose(
                        pt[i % 2][:, :], sb_vfm[:, i * 128:(i + 1) * 128],
                        sb_id.ap())), "PE", f"vt_{tc}_{i}", 1)
            if stage < 1:
                return
            for m in range(NFO):
                wait(eng, "LA" if m % 2 == 0 else "LB", f"wq_{m}")
                if m >= 2:
                    wait(eng, "VE", f"qe_{m-2}")
                for kt in range(NFO):
                    E(lambda m=m, kt=kt: eng.matmul(
                        pb[(m % 2) * 2][:, :],
                        wqbuf[m % 2][:, kt * 128:(kt + 1) * 128],
                        y1fm[:, kt * 1024:kt * 1024 + 512],
                        start=(kt == 0), stop=(kt == NFO - 1)))
                    i2 = E(lambda m=m, kt=kt: eng.matmul(
                        pb[(m % 2) * 2 + 1][:, :],
                        wqbuf[m % 2][:, kt * 128:(kt + 1) * 128],
                        y1fm[:, kt * 1024 + 512:kt * 1024 + 1024],
                        start=(kt == 0), stop=(kt == NFO - 1)))
                    if kt == NFO - 1:
                        inc(i2, "PE", f"q_{m}", 1)
            if stage < 2:
                return
            wait(eng, "LC", "b_mask")
            for qc in (0, 1):
                slots = SLOTSETS[qc]
                nsl = len(slots)
                for h in range(NH):
                    wait(eng, "VE", f"qe_{NFO-1}" if (qc == 0 and h == 0)
                         else f"qe_{h}")
                    if not (qc == 0 and h == 0):
                        ph, pqc = (h - 1, qc) if h > 0 else (NH - 1, 0)
                        wait(eng, "VE", f"ao_{ph}_{pqc}")
                    for si in range(nsl + 1):
                        if si < nsl:
                            s = slots[si]
                            if si >= 2:
                                wait(eng, "SC", f"exp_{h}_{qc}_{si-2}")
                            mk = MASKED[qc].get(s)
                            psA = pb[si % 2]
                            i1 = E(lambda s=s, h=h, qc=qc, psA=psA, mk=mk:
                                   eng.matmul(
                                       psA[:, :],
                                       sb_kf.ap()[:, s * 128:(s + 1) * 128],
                                       qT[:, q_off(h, qc):q_off(h, qc) + 512],
                                       start=True, stop=(mk is None)))
                            if mk is not None:
                                i1 = E(lambda mk=mk, psA=psA: eng.matmul(
                                    psA[:, :], sb_u1.ap(),
                                    sb_mask[:, mk * 512:(mk + 1) * 512],
                                    start=False, stop=True))
                            inc(i1, "PE", f"sc_{h}_{qc}_{si}", 1)
                        if si >= 1:
                            pi = si - 1
                            s_prev = slots[pi]
                            wait(eng, "SC", f"exp_{h}_{qc}_{pi}")
                            E(lambda s_prev=s_prev, pi=pi, nsl=nsl: eng.matmul(
                                pb[2][:, :],
                                sb_vt.ap()[:, s_prev * 128:(s_prev + 1) * 128],
                                P_tiles[pi % 2].ap()[:, :],
                                start=(pi == 0), stop=(pi == nsl - 1)))
                            inc(E(lambda pi=pi, nsl=nsl: eng.matmul(
                                pb[3][0:1, :], sb_ones.ap()[:, 0:1],
                                P_tiles[pi % 2].ap()[:, :],
                                start=(pi == 0), stop=(pi == nsl - 1))),
                                "PE", f"av_{h}_{qc}_{pi}", 1)
                    wait(eng, "VE", f"r_{h}_{qc}")
                    if not (qc == 0 and h == 0):
                        ph, pqc = (h - 1, qc) if h > 0 else (NH - 1, 0)
                        wait(eng, "SC", f"rbe_{ph}_{pqc}")
                    inc(E(lambda: eng.matmul(pb[4][:, :],
                                             sb_ones.ap()[0:1, 0:128],
                                             sb_rbf.ap()[0:1, 0:512],
                                             start=True, stop=True)),
                        "PE", f"rb_{h}_{qc}", 1)
            if stage < 3:
                return
            for fo in range(NFO):
                wait(eng, "LA" if fo % 2 == 0 else "LB", f"xf_{fo}")
                if fo == 0:
                    wait(eng, "SC", f"exp_31_1_{len(SLOTSETS[1])-1}")
                    wait(eng, "SC", "rbe_31_1")
                if fo == 1:
                    wait(eng, "VE", "ao_31_1")
                if fo >= 2:
                    wait(eng, "VE", f"x1_{fo-2}")
                for kt in range(NFO):
                    E(lambda fo=fo, kt=kt: eng.matmul(
                        pb[(fo % 2) * 2][:, :],
                        wobuf[fo % 2][:, kt * 128:(kt + 1) * 128],
                        attnout[:, kt * 1024:kt * 1024 + 512],
                        start=(kt == 0), stop=(kt == NFO - 1)))
                    i2 = E(lambda fo=fo, kt=kt: eng.matmul(
                        pb[(fo % 2) * 2 + 1][:, :],
                        wobuf[fo % 2][:, kt * 128:(kt + 1) * 128],
                        attnout[:, kt * 1024 + 512:kt * 1024 + 1024],
                        start=(kt == 0), stop=(kt == NFO - 1)))
                    if kt == NFO - 1:
                        inc(i2, "PE", f"wo_{fo}", 1)
                wait(eng, "VE", f"x1_{fo}")
                wait(eng, "SC", f"sq2_{fo}")
                E(lambda fo=fo: eng.matmul(
                    pb[4][0:1, :], sb_ones.ap()[:, 0:1],
                    x1bf[:, fo * 1024:fo * 1024 + 512],
                    start=(fo == 0), stop=False, skip_group_check=True))
                E(lambda fo=fo: eng.matmul(
                    pb[4][32:33, :], sb_ones.ap()[:, 0:1],
                    x1bf[:, fo * 1024 + 512:fo * 1024 + 1024],
                    start=False, stop=False, skip_group_check=True))
                E(lambda fo=fo: eng.matmul(
                    pb[4][64:65, :], sb_ones.ap()[:, 0:1],
                    sb_sq.ap()[:, 0:512],
                    start=False, stop=(fo == NFO - 1), skip_group_check=True))
                inc(E(lambda fo=fo: eng.matmul(
                    pb[5][0:1, :], sb_ones.ap()[:, 0:1],
                    sb_sq.ap()[:, 512:1024],
                    start=(fo == 0), stop=(fo == NFO - 1))),
                    "PE", f"st2_{fo}", 1)
            wait(eng, "VE", "rows2")
            E(lambda: eng.matmul(pb[0][:, :], sb_ones.ap()[0:1, 0:128],
                                 sb_rbf.ap()[0:1, 0:512], start=True, stop=True))
            E(lambda: eng.matmul(pb[1][:, :], sb_ones.ap()[0:1, 0:128],
                                 sb_rbf.ap()[0:1, 512:1024],
                                 start=True, stop=True))
            E(lambda: eng.matmul(pb[2][:, :], sb_ones.ap()[0:1, 0:128],
                                 sb_rbf.ap()[0:1, 1024:1536],
                                 start=True, stop=True))
            inc(E(lambda: eng.matmul(pb[3][:, :], sb_ones.ap()[0:1, 0:128],
                                     sb_rbf.ap()[0:1, 1536:2048],
                                     start=True, stop=True)),
                "PE", "bc2", 1)
            if stage < 4:
                return
            for fo in range(NFO):
                wait(eng, "VE", f"y2_{fo}")
                E(lambda fo=fo: eng.matmul(
                    pb[4][0:16, :], sb_c1.ap()[:, fo * 16:(fo + 1) * 16],
                    y2r[fo % 2][:, 0:512],
                    start=(fo == 0), stop=(fo == NFO - 1)))
                i2 = E(lambda fo=fo: eng.matmul(
                    pb[5][0:16, :], sb_c1.ap()[:, fo * 16:(fo + 1) * 16],
                    y2r[fo % 2][:, 512:1024],
                    start=(fo == 0), stop=(fo == NFO - 1)))
                if fo == NFO - 1:
                    inc(i2, "PE", "u", 1)
                for i in range(8):
                    gidx = fo * 8 + i
                    if gidx >= 2:
                        pp = gidx - 2
                        wait(eng, "SC", f"tre_{pp // 8}_{pp % 8}")
                    inc(E(lambda fo=fo, i=i: eng.transpose(
                        pt[i % 2][:, :],
                        y2r[fo % 2][:, i * 128:(i + 1) * 128],
                        sb_id.ap())), "PE", f"tr_{fo}_{i}", 1)
            if stage < 5:
                return
            def pe_int(g2):
                t2, j2 = g2 // NB, g2 % NB
                wait(eng, "LA" if j2 % 2 == 0 else "LB", f"c2_{t2}_{j2}")
                if g2 == 0:
                    wait(eng, "VE", "ue")
                if g2 >= 2:
                    p2, q2 = (g2 - 2) // NB, (g2 - 2) % NB
                    wait(eng, "SC", f"ie_{p2}_{q2}")
                inc(E(lambda t2=t2, j2=j2: eng.matmul(
                    pb[2 + j2 % 2][:, 0:256],
                    sb_u.ap()[0:16, t2 * 128:(t2 + 1) * 128],
                    c2b[j2 % 2].ap()[:, :],
                    start=True, stop=True)),
                    "PE", f"int_{t2}_{j2}", 1)

            def pe_ft(tt):
                wait(eng, "VE", f"o_{tt}_{NB-1}")
                for fo in range(NFO):
                    gidx = tt * NFO + fo
                    if gidx >= 2:
                        pp = gidx - 2
                        wait(eng, "VE", f"fa_{pp // NFO}_{pp % NFO}")
                    inc(E(lambda fo=fo: eng.transpose(
                        pt[fo % 2][:, :], acc_bf[0][:, fo * 128:(fo + 1) * 128],
                        sb_id.ap())), "PE", f"ft_{tt}_{fo}", 1)

            for g in range(NTT * NB):
                pe_int(g)
                if g % NB == 2 and g >= NB:
                    pe_ft(g // NB - 1)
            pe_ft(NTT - 1)

        def stream_scalar(eng):
            E, inc, wait = ctx.E, ctx.inc, ctx.wait
            pass
            for tc in range(4):
                wait(eng, "LC", f"b_xc{tc}")
                for kt in range(NFO):
                    if kt >= 2:
                        wait(eng, "PE", f"sx_{tc}_{kt-2}")
                    elif tc >= 1:
                        wait(eng, "PE", f"sx_{tc-1}_{NFO-2+kt}")
                    inc(E(lambda kt=kt, tc=tc: eng.activation(
                        sb_sq.ap()[:, (kt % 2) * 512:(kt % 2) * 512 + 512],
                        xchunk[:, kt * 512:(kt + 1) * 512], AF.Square)),
                        "SC", f"sq_{tc}_{kt}", 1)
                wait(eng, "VE", f"var_{tc}")
                inc(E(lambda: eng.activation(
                    rows_kv[0:1, 1024:1536], rows_kv[0:1, 1536:2048],
                    AF.Sqrt)), "SC", f"srt_{tc}", 1)
                for i in range(4):
                    wait(eng, "PE", f"vt_{tc}_{i}")
                    inc(E(lambda tc=tc, i=i: eng.copy(
                        sb_vt.ap()[:, (tc * 4 + i) * 128:(tc * 4 + i + 1) * 128],
                        pt[i % 2][:, :])), "SC", f"vt_{tc}_{i}", 1)
            if stage < 2:
                if ctx.emit:
                    eng.wait_ge(ctx.sems["SD"], 0)
                return
            wait(eng, "LC", "b_pre")
            for qc in (0, 1):
                slots = SLOTSETS[qc]
                for h in range(NH):
                    ng = n_groups(h)
                    w = 512 // ng
                    for si in range(len(slots)):
                        wait(eng, "PE", f"sc_{h}_{qc}_{si}")
                        if si >= 2:
                            wait(eng, "PE", f"av_{h}_{qc}_{si-2}")
                        psA = pb[si % 2]
                        for g in range(ng):
                            col = COLOF[(h, qc, si, g)]
                            i1 = E(lambda si=si, g=g, w=w, col=col, psA=psA:
                                   eng.activation(
                                       P_tiles[si % 2].ap()[:, g * w:(g + 1) * w],
                                       psA[:, g * w:(g + 1) * w],
                                       AF.Exp,
                                       bias=sb_bias.ap()[:, col:col + 1],
                                       scale=SCALE))
                            if g == ng - 1:
                                inc(i1, "SC", f"exp_{h}_{qc}_{si}", 1)
                    wait(eng, "PE", f"rb_{h}_{qc}")
                    inc(E(lambda: eng.copy(sb_rbc[:, :], pb[4][:, :])),
                        "SC", f"rbe_{h}_{qc}", 1)
            if stage < 3:
                return
            for fo in range(NFO):
                wait(eng, "VE", f"x1_{fo}")
                inc(E(lambda fo=fo: eng.dma_start(out=d_x1s[fo],
                                                  in_=sb_x1o)),
                    "SD", f"x1s_{fo}", 16)
                if fo >= 1:
                    wait(eng, "PE", f"st2_{fo-1}")
                inc(E(lambda fo=fo: eng.activation(
                    sb_sq.ap()[:, :], x1bf[:, fo * 1024:(fo + 1) * 1024],
                    AF.Square)), "SC", f"sq2_{fo}", 1)
            wait(eng, "VE", "var2")
            E(lambda: eng.activation(
                sb_x1o[0:1, 0:512], rows_ln2[0:1, 1024:1536],
                AF.Sqrt))
            inc(E(lambda: eng.activation(
                sb_x1o[0:1, 512:1024], rows_ln2[0:1, 1536:2048],
                AF.Sqrt)), "SC", "srt2", 1)
            wait(eng, "PE", "bc2")
            E(lambda: eng.copy(sb_mbc.ap()[:, 0:512], pb[0][:, :]))
            E(lambda: eng.copy(sb_mbc.ap()[:, 512:1024], pb[1][:, :]))
            E(lambda: eng.copy(sb_mbc.ap()[:, 1024:1536], pb[2][:, :]))
            inc(E(lambda: eng.copy(sb_mbc.ap()[:, 1536:2048], pb[3][:, :])),
                "SC", "bc2e", 1)
            if stage < 4:
                return
            for fo in range(NFO):
                for i in range(8):
                    wait(eng, "PE", f"tr_{fo}_{i}")
                    inc(E(lambda fo=fo, i=i: eng.copy(
                        y2tm[:, i * 4096 + fo * 128: i * 4096 + (fo + 1) * 128],
                        pt[i % 2][:, :])), "SC", f"tre_{fo}_{i}", 1)
            if stage >= 5:
                NG_ALL = NTT * NB

                def sc_ie(g2):
                    t2, j2 = g2 // NB, g2 % NB
                    wait(eng, "PE", f"int_{t2}_{j2}")
                    if g2 >= 2:
                        p2, q2 = (g2 - 2) // NB, (g2 - 2) % NB
                        wait(eng, "VE", f"am_{p2}_{q2}")
                    inc(E(lambda t2=t2, j2=j2: eng.copy(
                        P_tiles[j2 % 2].ap()[:, 0:256],
                        pb[2 + j2 % 2][:, 0:256])),
                        "SC", f"ie_{t2}_{j2}", 1)

                sc_ie(0)
                sc_ie(1)
                for g in range(NG_ALL):
                    tt, j = g // NB, g % NB
                    if g + 2 < NG_ALL:
                        sc_ie(g + 2)
                    wait(eng, "VE", f"pr_{tt}_{j}")
                    inc(E(lambda j=j, g=g: eng.activation(
                        sc_scr[:, :], prod[g % 2][:, :], AF.Copy,
                        accum_out=sb_h.ap()[:, j:j + 1])),
                        "SC", f"ha_{tt}_{j}", 1)
                    wait(eng, "SC", f"ha_{tt}_{j}")
                    inc(E(lambda j=j: eng.activation(
                        sb_hr.ap()[:, j:j + 1], sb_h.ap()[:, j:j + 1],
                        AF.Relu)), "SC", f"hrl_{tt}_{j}", 1)
                    if j == 2 and tt >= 1:
                        wait(eng, "VE", f"fa_{tt-1}_31")
                        inc(E(lambda tt=tt: eng.dma_start(out=d_y[tt - 1],
                                                          in_=otile)),
                            "SD", f"y_{tt-1}", 16)
                wait(eng, "VE", f"fa_{NTT-1}_31")
                inc(E(lambda: eng.dma_start(out=d_y[NTT - 1], in_=otile)),
                    "SD", f"y_{NTT-1}", 16)
                if ctx.emit:
                    eng.wait_ge(ctx.sems["SD"], ctx.ev[("SD", f"y_{NTT-1}")])
            else:
                inc(E(lambda: eng.dma_start(out=d_y[0][:, 0:NCOL],
                                            in_=sb_bias.ap())),
                    "SD", "y_stub", 16)
                if ctx.emit:
                    eng.wait_ge(ctx.sems["SD"], ctx.ev[("SD", "y_stub")])

        def stream_vector(eng):
            E, inc, wait = ctx.E, ctx.inc, ctx.wait
            pass
            for tc in range(4):
                wait(eng, "PE", f"sx_{tc}_{NFO-1}")
                E(lambda: eng.tensor_scalar(rows_kv[0:1, 0:512], pb[0][0:1, :],
                                            1.0 / D, None, ALU.mult))
                E(lambda: eng.tensor_scalar(rows_kv[0:1, 512:1024], pb[1][0:1, :],
                                            1.0 / D, None, ALU.mult))
                E(lambda: eng.tensor_tensor(rows_kv[0:1, 1536:2048],
                                            rows_kv[0:1, 0:512], rows_kv[0:1, 0:512],
                                            ALU.mult))
                E(lambda: eng.tensor_tensor(rows_kv[0:1, 1024:1536],
                                            rows_kv[0:1, 512:1024],
                                            rows_kv[0:1, 1536:2048],
                                            ALU.subtract))
                inc(E(lambda: eng.tensor_scalar(rows_kv[0:1, 1536:2048],
                                                rows_kv[0:1, 1024:1536],
                                                LN_EPS, None, ALU.add)),
                    "VE", f"var_{tc}", 1)
                wait(eng, "SC", f"srt_{tc}")
                E(lambda: eng.tensor_copy(sb_rbf.ap()[0:1, 0:512],
                                          rows_kv[0:1, 0:512]))
                E(lambda: eng.reciprocal(rows_kv[0:1, 0:512],
                                         rows_kv[0:1, 1024:1536]))
                inc(E(lambda: eng.tensor_copy(sb_rbf.ap()[0:1, 512:1024],
                                              rows_kv[0:1, 0:512])),
                    "VE", f"rows_{tc}", 1)
                wait(eng, "PE", f"bc_{tc}")
                E(lambda: eng.tensor_copy(sb_mbc.ap()[:, 0:512], pb[2][:, :]))
                inc(E(lambda: eng.tensor_copy(sb_mbc.ap()[:, 512:1024],
                                              pb[3][:, :])),
                    "VE", f"mbc_{tc}", 1)
                for kt in range(NFO):
                    if tc < 2:
                        dst = y1fm[:, kt * 1024 + tc * 512:
                                   kt * 1024 + tc * 512 + 512]
                    else:
                        dst = sb_ykv[:, (kt % 2) * 512:(kt % 2) * 512 + 512]
                        if kt >= 2:
                            wait(eng, "PE", f"kv_{tc}_{kt-2}")
                    E(lambda kt=kt: eng.tensor_tensor(
                        sb_nrm_kv[:, (kt % 2) * 512:(kt % 2) * 512 + 512],
                        xchunk[:, kt * 512:(kt + 1) * 512],
                        sb_mbc.ap()[:, 0:512], ALU.subtract))
                    inc(E(lambda kt=kt, dst=dst: eng.tensor_tensor(
                        dst,
                        sb_nrm_kv[:, (kt % 2) * 512:(kt % 2) * 512 + 512],
                        sb_mbc.ap()[:, 512:1024], ALU.mult)),
                        "VE", f"y_{tc}_{kt}", 1)
                wait(eng, "PE", f"kv_{tc}_{NFO-1}")
                E(lambda tc=tc: eng.tensor_copy(
                    sb_kf.ap()[:, tc * 512:(tc + 1) * 512], pb[4][:, :]))
                inc(E(lambda: eng.tensor_copy(sb_vfm[:, :], pb[5][:, :])),
                    "VE", f"kve_{tc}", 1)
            if stage < 1:
                return
            for m in range(NFO):
                wait(eng, "PE", f"q_{m}")
                E(lambda m=m: eng.tensor_copy(
                    qT[:, m * 1024:m * 1024 + 512], pb[(m % 2) * 2][:, :]))
                inc(E(lambda m=m: eng.tensor_copy(
                    qT[:, m * 1024 + 512:m * 1024 + 1024],
                    pb[(m % 2) * 2 + 1][:, :])), "VE", f"qe_{m}", 1)
            if stage < 2:
                return
            for qc in (0, 1):
                nsl = len(SLOTSETS[qc])
                for h in range(NH):
                    wait(eng, "PE", f"av_{h}_{qc}_{nsl-1}")
                    E(lambda: eng.reciprocal(sb_rrow.ap()[0:1, :],
                                             pb[3][0:1, :]))
                    inc(E(lambda: eng.tensor_copy(sb_rbf.ap()[0:1, 0:512],
                                                  sb_rrow.ap()[0:1, :])),
                        "VE", f"r_{h}_{qc}", 1)
                    wait(eng, "SC", f"rbe_{h}_{qc}")
                    inc(E(lambda h=h, qc=qc: eng.tensor_tensor(
                        attnout[:, q_off(h, qc):q_off(h, qc) + 512],
                        pb[2][:, :], sb_rbc[:, :], ALU.mult)),
                        "VE", f"ao_{h}_{qc}", 1)
            if stage < 3:
                return
            for fo in range(NFO):
                wait(eng, "PE", f"wo_{fo}")
                wait(eng, "LA" if fo % 2 == 0 else "LB", f"xf_{fo}")
                if fo >= 1:
                    wait(eng, "SD", f"x1s_{fo-1}")
                E(lambda fo=fo: eng.tensor_tensor(
                    sb_x1o[:, 0:512], pb[(fo % 2) * 2][:, :],
                    sb_x1f[:, 0:512], ALU.add))
                E(lambda fo=fo: eng.tensor_tensor(
                    sb_x1o[:, 512:1024], pb[(fo % 2) * 2 + 1][:, :],
                    sb_x1f[:, 512:1024], ALU.add))
                inc(E(lambda fo=fo: eng.tensor_copy(
                    x1bf[:, fo * 1024:(fo + 1) * 1024], sb_x1o[:, :])),
                    "VE", f"x1_{fo}", 1)
            wait(eng, "PE", f"st2_{NFO-1}")
            wait(eng, "SD", f"x1s_{NFO-1}")
            x1fr = sb_x1f[0:1, :]
            x1or = sb_x1o[0:1, :]
            E(lambda: eng.tensor_scalar(rows_ln2[0:1, 0:512], pb[4][0:1, :],
                                        1.0 / D, None, ALU.mult))
            E(lambda: eng.tensor_scalar(rows_ln2[0:1, 512:1024], pb[4][32:33, :],
                                        1.0 / D, None, ALU.mult))
            E(lambda: eng.tensor_scalar(rows_ln2[0:1, 1024:1536], pb[4][64:65, :],
                                        1.0 / D, None, ALU.mult))
            E(lambda: eng.tensor_scalar(rows_ln2[0:1, 1536:2048], pb[5][0:1, :],
                                        1.0 / D, None, ALU.mult))
            E(lambda: eng.tensor_tensor(x1fr[:, 0:512],
                                        rows_ln2[0:1, 0:512], rows_ln2[0:1, 0:512],
                                        ALU.mult))
            E(lambda: eng.tensor_tensor(x1fr[:, 512:1024],
                                        rows_ln2[0:1, 512:1024],
                                        rows_ln2[0:1, 512:1024], ALU.mult))
            E(lambda: eng.tensor_tensor(x1or[:, 0:512],
                                        rows_ln2[0:1, 1024:1536],
                                        x1fr[:, 0:512], ALU.subtract))
            E(lambda: eng.tensor_tensor(x1or[:, 512:1024],
                                        rows_ln2[0:1, 1536:2048],
                                        x1fr[:, 512:1024],
                                        ALU.subtract))
            E(lambda: eng.tensor_scalar(rows_ln2[0:1, 1024:1536],
                                        x1or[:, 0:512],
                                        LN_EPS, None, ALU.add))
            inc(E(lambda: eng.tensor_scalar(rows_ln2[0:1, 1536:2048],
                                            x1or[:, 512:1024],
                                            LN_EPS, None, ALU.add)),
                "VE", "var2", 1)
            wait(eng, "SC", "srt2")
            E(lambda: eng.reciprocal(x1fr[:, 0:512], x1or[:, 0:512]))
            E(lambda: eng.reciprocal(x1fr[:, 512:1024], x1or[:, 512:1024]))
            E(lambda: eng.tensor_copy(sb_rbf.ap()[0:1, 0:1024],
                                      rows_ln2[0:1, 0:1024]))
            inc(E(lambda: eng.tensor_copy(sb_rbf.ap()[0:1, 1024:2048],
                                          x1fr[:, 0:1024])),
                "VE", "rows2", 1)
            if stage < 4:
                return
            wait(eng, "SC", "bc2e")
            for fo in range(NFO):
                if fo >= 2:
                    wait(eng, "PE", f"tr_{fo-2}_7")
                E(lambda fo=fo: eng.tensor_tensor(
                    sb_nrm_t[:, :], x1bf[:, fo * 1024:(fo + 1) * 1024],
                    sb_mbc.ap()[:, 0:1024], ALU.subtract))
                inc(E(lambda fo=fo: eng.tensor_tensor(
                    y2r[fo % 2][:, :], sb_nrm_t[:, :],
                    sb_mbc.ap()[:, 1024:2048], ALU.mult)),
                    "VE", f"y2_{fo}", 1)
            wait(eng, "PE", "u")
            E(lambda: eng.tensor_copy(sb_u.ap()[0:16, 0:512], pb[4][0:16, :]))
            inc(E(lambda: eng.tensor_copy(sb_u.ap()[0:16, 512:1024],
                                          pb[5][0:16, :])), "VE", "ue", 1)
            if stage < 5:
                return
            NG_ALL = NTT * NB

            def ve_argmax(g2):
                t2, j2 = g2 // NB, g2 % NB
                wait(eng, "SC", f"ie_{t2}_{j2}")
                if g2 == 0:
                    wait(eng, "LC", "b_iota")
                inc(E(lambda j2=j2: eng.reduce_max(
                    sb_m8.ap()[:, 0:1], P_tiles[j2 % 2].ap()[:, 0:256],
                    axis=mybir.AxisListType.X)),
                    "VE", f"amx_{t2}_{j2}", 1)
                wait(eng, "VE", f"amx_{t2}_{j2}")
                E(lambda j2=j2: eng.tensor_scalar(
                    sb_pt1.ap()[:, 256:512], P_tiles[j2 % 2].ap()[:, 0:256],
                    sb_m8.ap()[:, 0:1], None, ALU.is_ge))
                E(lambda: eng.tensor_tensor(
                    sb_sq.ap()[:, 0:256], sb_pt1.ap()[:, 256:512],
                    sb_pt0.ap()[:, 256:512], ALU.mult))
                inc(E(lambda: eng.reduce_max(
                    sb_m8.ap()[:, 1:2], sb_sq.ap()[:, 0:256],
                    axis=mybir.AxisListType.X)),
                    "VE", f"amr_{t2}_{j2}", 1)
                wait(eng, "VE", f"amr_{t2}_{j2}")
                inc(E(lambda g2=g2: eng.tensor_copy(
                    sb_i8.ap()[:, g2 % 4:g2 % 4 + 1],
                    sb_m8.ap()[:, 1:2])),
                    "VE", f"am_{t2}_{j2}", 1)

            def ve_add(p):
                pt_, pj = p // NB, p % NB
                wait(eng, "SC", f"hrl_{pt_}_{pj}")
                wait(eng, "GA", f"g2_{pt_}_{pj}")
                inc(E(lambda pt_=pt_, pj=pj: eng.tensor_scalar(
                    w2sc[:, :], w2buf[pj % 2][:, :],
                    sb_hr.ap()[:, pj:pj + 1], None, ALU.mult)),
                    "VE", f"ws_{pt_}_{pj}", 1)
                if pj == 0:
                    if pt_ >= 1:
                        wait(eng, "PE", f"ft_{pt_-1}_{NFO-1}")
                    inc(E(lambda: eng.tensor_copy(
                        acc_bf[1][:, :], w2sc[:, :])),
                        "VE", f"o_{pt_}_0", 1)
                else:
                    if pj == 1 and pt_ >= 1:
                        wait(eng, "PE", f"ft_{pt_-1}_{NFO-1}")
                    inc(E(lambda pj=pj: eng.tensor_tensor(
                        acc_bf[(pj + 1) % 2][:, :], w2sc[:, :],
                        acc_bf[pj % 2][:, :], ALU.add)),
                        "VE", f"o_{pt_}_{pj}", 1)
                if pj == NB - 1:
                    for fo in range(NFO):
                        wait(eng, "PE", f"ft_{pt_}_{fo}")
                        wait(eng, "LC", f"b_x1c{pt_}")
                        if pt_ >= 1 and fo == 0:
                            wait(eng, "SD", f"y_{pt_-1}")
                        inc(E(lambda pt_=pt_, fo=fo: eng.tensor_tensor(
                            otile[:, fo * 128:(fo + 1) * 128],
                            pt[fo % 2][:, :],
                            x1c[:, fo * 128:(fo + 1) * 128], ALU.add)),
                            "VE", f"fa_{pt_}_{fo}", 1)

            ve_argmax(0)
            ve_argmax(1)
            for g in range(NG_ALL):
                tt, j = g // NB, g % NB
                la = g + 2 < NG_ALL
                if la:
                    t2, j2 = (g + 2) // NB, (g + 2) % NB
                    wait(eng, "SC", f"ie_{t2}_{j2}")
                    inc(E(lambda j2=j2: eng.reduce_max(
                        sb_m8.ap()[:, 0:1], P_tiles[j2 % 2].ap()[:, 0:256],
                        axis=mybir.AxisListType.X)),
                        "VE", f"amx_{t2}_{j2}", 1)
                wait(eng, "GA", f"g_{tt}_{j}")
                if g >= 2:
                    p2, q2 = (g - 2) // NB, (g - 2) % NB
                    wait(eng, "SC", f"ha_{p2}_{q2}")
                inc(E(lambda tt=tt, j=j, g=g: eng.tensor_tensor(
                    prod[g % 2][:, :], w1buf[j % 2][:, :],
                    y2tm[:, tt * 4096:(tt + 1) * 4096], ALU.mult)),
                    "VE", f"pr_{tt}_{j}", 1)
                if la:
                    wait(eng, "VE", f"amx_{t2}_{j2}")
                    E(lambda j2=j2: eng.tensor_scalar(
                        sb_pt1.ap()[:, 256:512], P_tiles[j2 % 2].ap()[:, 0:256],
                        sb_m8.ap()[:, 0:1], None, ALU.is_ge))
                    E(lambda: eng.tensor_tensor(
                        sb_sq.ap()[:, 0:256], sb_pt1.ap()[:, 256:512],
                        sb_pt0.ap()[:, 256:512], ALU.mult))
                    inc(E(lambda: eng.reduce_max(
                        sb_m8.ap()[:, 1:2], sb_sq.ap()[:, 0:256],
                        axis=mybir.AxisListType.X)),
                        "VE", f"amr_{t2}_{j2}", 1)
                if g >= 1:
                    ve_add(g - 1)
                if la:
                    wait(eng, "VE", f"amr_{t2}_{j2}")
                    inc(E(lambda g=g: eng.tensor_copy(
                        sb_i8.ap()[:, (g + 2) % 4:(g + 2) % 4 + 1],
                        sb_m8.ap()[:, 1:2])),
                        "VE", f"am_{t2}_{j2}", 1)
            ve_add(NG_ALL - 1)

        ctx.emit = False
        stream_sync(None)
        stream_gpsimd(None)
        stream_tensor(None)
        stream_scalar(None)
        stream_vector(None)

        ctx.emit = True

        @block.sync
        def _(eng):
            stream_sync(eng)

        @block.gpsimd
        def _(eng):
            stream_gpsimd(eng)

        @block.tensor
        def _(eng):
            stream_tensor(eng)

        @block.scalar
        def _(eng):
            stream_scalar(eng)

        @block.vector
        def _(eng):
            stream_vector(eng)

    return nc


def _bf(x):
    return np.ascontiguousarray(x).astype(ml_dtypes.bfloat16)


def kernel(L0_attn_hidden_states, L0_attn_Wq, L0_attn_Wkv, L0_attn_Wo,
           L0_ffn_c1_w, L0_ffn_c2_w, L0_ffn_w1, L0_ffn_w2t, _debug=False,
           _stage=99, _fsub=99):
    X = np.asarray(L0_attn_hidden_states, dtype=np.float32)
    Wq = np.asarray(L0_attn_Wq, dtype=np.float32)
    Wkv = np.asarray(L0_attn_Wkv, dtype=np.float32)
    Wo = np.asarray(L0_attn_Wo, dtype=np.float32)
    c1 = np.asarray(L0_ffn_c1_w, dtype=np.float32)
    c2 = np.asarray(L0_ffn_c2_w, dtype=np.float32)
    w1 = np.asarray(L0_ffn_w1, dtype=np.float32)
    w2t = np.asarray(L0_ffn_w2t, dtype=np.float32)

    wq_t = _bf(Wq.reshape(NFO, 128, NFO, 128).transpose(2, 1, 0, 3)
               .reshape(NFO, 128, D))
    wo_t = _bf(Wo.reshape(NFO, 128, NFO, 128).transpose(2, 1, 0, 3)
               .reshape(NFO, 128, D))
    wkv_t = _bf(Wkv.reshape(NFO, 128, 256).transpose(1, 0, 2).reshape(128, 8192))
    c1t = _bf(c1.T.reshape(NFO, 128, 16).transpose(1, 0, 2).reshape(128, 512))
    c2t = _bf(c2.T)
    w1w2 = np.concatenate([_bf(w1), _bf(w2t)], axis=1)
    mask4 = _host_masks()
    u1 = np.triu(np.ones((128, 128))).astype(ml_dtypes.bfloat16)
    ident = np.eye(128).astype(ml_dtypes.bfloat16)
    ones = np.ones((128, 128)).astype(ml_dtypes.bfloat16)
    iota = np.broadcast_to(np.arange(256, dtype=np.float32), (128, 256)) \
        .astype(ml_dtypes.bfloat16)

    nc = build_program(debug=_debug, stage=_stage, fsub=_fsub)
    in_maps = []
    for c in range(8):
        perm, _, _ = _core_geom(c)
        b = c // 2
        xp = X[b][perm]
        in_maps.append({
            "xfmt": _bf(xp.T.reshape(NFO, 128, S)),
            "xf32": np.ascontiguousarray(
                xp[:NTOK].T.reshape(NFO, 128, NTOK)).astype(np.float32),
            "wq_t": wq_t, "wo_t": wo_t, "wkv_t": wkv_t,
            "c1t": c1t, "c2t": c2t, "w1w2": w1w2,
            "biasm": _host_biasmat(c), "mask4": mask4,
            "u1": u1, "ident": ident, "onesb": ones, "iota": iota,
        })

    res = run_bass_kernel_spmd(nc, in_maps, core_ids=list(range(8)))

    out = np.empty((B, S, D), dtype=np.float32)
    for c in range(8):
        perm, _, _ = _core_geom(c)
        b = c // 2
        yc = np.asarray(res.results[c]["y"])
        toks = yc.reshape(NTT, 128, NFO, 128).transpose(0, 3, 2, 1) \
                 .reshape(NTOK, D)
        out[b][perm[:NTOK]] = toks
    if _debug:
        return out, res
    return out


# version 10
# speedup vs baseline: 1.7352x; 1.0081x over previous
import math
import numpy as np
import ml_dtypes

import concourse.bass as bass
import concourse.mybir as mybir
from concourse.bass import IndirectOffsetOnAxis
from concourse.bass_utils import run_bass_kernel_spmd

F32 = mybir.dt.float32
BF16 = mybir.dt.bfloat16
U8 = mybir.dt.uint8
U32 = mybir.dt.uint32
ALU = mybir.AluOpType
AF = mybir.ActivationFunctionType

IDX_ZERO = False
DBG_H = False
GLIMIT = 10**9
PLAIN_GATHER = False

B, S, D, NH, HD = 4, 2048, 4096, 32, 128
INTER, SPR, NB, RANK = 14336, 256, 56, 16
SCALE = HD ** -0.5
NTOK = 1024
NKT = 16
NFO = 32
NTT = 8
LN_EPS = 1e-5
MASK_NEG = -50000.0
KILL = -1e9

_start = 2.0 ** (-(2.0 ** -(math.log2(NH) - 3)))
SLOPES = [float(np.float32(_start ** (i + 1))) for i in range(NH)]

SLOTSETS = {0: [0, 1, 2, 3, 8, 9, 10, 11], 1: list(range(16))}
MASKED = {0: {0: 0, 1: 1, 2: 2, 3: 3}, 1: {4: 0, 5: 1, 6: 2, 7: 3}}


def n_groups(h):
    sh = SLOPES[h]
    if sh * 256.0 <= 66.0:
        return 1
    if sh * 128.0 <= 66.0:
        return 2
    return 4


def _core_geom(c):
    half = c % 2
    if half == 0:
        perm = np.concatenate([np.arange(0, 512), np.arange(1536, 2048),
                               np.arange(512, 1536)])
        q0s = (0, 1536)
    else:
        perm = np.concatenate([np.arange(512, 1536), np.arange(0, 512),
                               np.arange(1536, 2048)])
        q0s = (512, 1024)
    tile_orig = [int(perm[s * 128]) for s in range(NKT)]
    return perm, q0s, tile_orig


_BIAS_COLS = []
COLOF = {}
for _h in range(NH):
    for _qc in (0, 1):
        for _si in range(len(SLOTSETS[_qc])):
            for _g in range(n_groups(_h)):
                COLOF[(_h, _qc, _si, _g)] = len(_BIAS_COLS)
                _BIAS_COLS.append((_h, _qc, _si, _g))
NCOL = len(_BIAS_COLS)


def _host_biasmat(c):
    _, q0s, torig = _core_geom(c)
    bm = np.zeros((128, NCOL), dtype=np.float64)
    p = np.arange(128, dtype=np.float64)
    for col, (h, qc, si, g) in enumerate(_BIAS_COLS):
        s = SLOTSETS[qc][si]
        k0, q0 = torig[s], q0s[qc]
        if k0 > q0 + 511:
            bm[:, col] = KILL
        else:
            w = 512 // n_groups(h)
            qt = q0 + w * g + w // 2
            bm[:, col] = SLOPES[h] * (k0 + p - qt)
    return bm.astype(np.float32)


def _host_masks():
    m = np.zeros((4, 128, 512), dtype=np.float32)
    c = np.arange(128)[:, None]
    j = np.arange(512)[None, :]
    for d in range(4):
        m[d] = np.where(c > j - 128 * d, MASK_NEG, 0.0)
    return m.astype(ml_dtypes.bfloat16)


class Ctx:
    SEMS = ("LD", "LA", "LB", "LC", "SD", "GA", "PE", "VE", "SC")

    def __init__(self):
        self.emit = False
        self.cnt = {s: 0 for s in self.SEMS}
        self.ev = {}
        self.sems = {}

    def inc(self, inst, sem, name, n):
        if self.emit:
            assert inst is not None, (sem, name)
            inst.then_inc(self.sems[sem], n)
        else:
            self.cnt[sem] += n
            self.ev[(sem, name)] = self.cnt[sem]

    def wait(self, eng, sem, name):
        if self.emit:
            eng.wait_ge(self.sems[sem], self.ev[(sem, name)])

    def E(self, thunk):
        return thunk() if self.emit else None

    def mark(self, sem, name):
        if not self.emit:
            self.ev[(sem, name)] = self.cnt[sem]


def build_program(debug=False, stage=99, fsub=99):
    nc = bass.Bass("TRN2", target_bir_lowering=False)

    d_xfmt = nc.dram_tensor("xfmt", [NFO, 128, S], BF16, kind="ExternalInput").ap()
    d_xf32 = nc.dram_tensor("xf32", [NFO, 128, NTOK], F32, kind="ExternalInput").ap()
    d_wq = nc.dram_tensor("wq_t", [NFO, 128, D], BF16, kind="ExternalInput").ap()
    d_wo = nc.dram_tensor("wo_t", [NFO, 128, D], BF16, kind="ExternalInput").ap()
    d_wkv = nc.dram_tensor("wkv_t", [128, 8192], BF16, kind="ExternalInput").ap()
    d_c1 = nc.dram_tensor("c1t", [128, 512], BF16, kind="ExternalInput").ap()
    d_c2 = nc.dram_tensor("c2t", [16, INTER], BF16, kind="ExternalInput").ap()
    d_w12 = nc.dram_tensor("w1w2", [INTER, 8192], BF16, kind="ExternalInput").ap()
    d_bias = nc.dram_tensor("biasm", [128, NCOL], F32, kind="ExternalInput").ap()
    d_mask = nc.dram_tensor("mask4", [4, 128, 512], BF16, kind="ExternalInput").ap()
    d_u1 = nc.dram_tensor("u1", [128, 128], BF16, kind="ExternalInput").ap()
    d_id = nc.dram_tensor("ident", [128, 128], BF16, kind="ExternalInput").ap()
    d_ones = nc.dram_tensor("onesb", [128, 128], BF16, kind="ExternalInput").ap()
    d_iota = nc.dram_tensor("iota", [128, 256], BF16, kind="ExternalInput").ap()
    d_x1s = nc.dram_tensor("x1s", [NFO, 128, NTOK], F32, kind="Internal").ap()
    d_y = nc.dram_tensor("y", [NTT, 128, D], F32, kind="ExternalOutput").ap()
    dbg = {}
    if debug:
        for nm, shp, dt in [("dbg_y1", [128, NFO * NTOK], BF16),
                            ("dbg_q", [128, NH * NTOK], BF16),
                            ("dbg_kf", [128, NKT * 128], BF16),
                            ("dbg_vt", [128, NKT * 128], BF16),
                            ("dbg_ao", [128, NH * NTOK], BF16),
                            ("dbg_x1", [128, NFO * NTOK], BF16),
                            ("dbg_u", [16, NTOK], BF16),
                            ("dbg_y2", [128, NTT * D], BF16),
                            ("dbg_h", [NTT, 128, 56], F32)]:
            dbg[nm] = nc.dram_tensor(nm, shp, dt, kind="ExternalOutput").ap()

    ctx = Ctx()

    from contextlib import ExitStack
    with ExitStack() as _st:
        def _sb(name, shape, dt):
            return _st.enter_context(nc.sbuf_tensor(name, shape, dt))

        def _ps(name, shape, dt):
            return _st.enter_context(nc.psum_tensor(name, shape, dt))

        arA = _sb("arA", [128, 65536], U8)
        arB = _sb("arB", [128, 32768], BF16)
        arC = _sb("arC", [128, 49152], U8)
        sb_bias = _sb("sb_bias", [128, NCOL], F32)
        sb_u1 = _sb("sb_u1", [128, 128], BF16)
        sb_id = _sb("sb_id", [128, 128], BF16)
        sb_ones = _sb("sb_ones", [128, 128], BF16)
        sb_kf = _sb("sb_kf", [128, NKT * 128], BF16)
        sb_vt = _sb("sb_vt", [128, NKT * 128], BF16)
        sb_c1 = _sb("sb_c1", [128, 512], BF16)
        sb_pt0 = _sb("sb_pt0", [128, 512], BF16)
        sb_pt1 = _sb("sb_pt1", [128, 512], BF16)
        sb_mbc = _sb("sb_mbc", [128, 2048], BF16)
        sb_sq = _sb("sb_sq", [128, 1024], BF16)
        sb_rrow = _sb("sb_rrow", [1, 512], F32)
        sb_rbf = _sb("sb_rbf", [1, 2048], BF16)
        sb_u = _sb("sb_u", [16, 1024], BF16)
        sb_c2b0 = _sb("sb_c2b0", [16, 256], BF16)
        sb_c2b1 = _sb("sb_c2b1", [16, 256], BF16)
        sb_m8 = _sb("sb_m8", [128, 8], F32)
        sb_i8 = _sb("sb_i8", [128, 16], U32)
        sb_h = _sb("sb_h", [128, 64], F32)
        sb_hr = _sb("sb_hr", [128, 64], F32)
        pb0 = _ps("pb0", [128, 512], F32)
        pb1 = _ps("pb1", [128, 512], F32)
        pb2 = _ps("pb2", [128, 512], F32)
        pb3 = _ps("pb3", [128, 512], F32)
        pb4 = _ps("pb4", [128, 512], F32)
        pb5 = _ps("pb5", [128, 512], F32)
        pt0 = _ps("pt0", [128, 128], BF16)
        pt1 = _ps("pt1", [128, 128], BF16)
        sLD = _st.enter_context(nc.semaphore("sLD"))
        sLA = _st.enter_context(nc.semaphore("sLA"))
        sLB = _st.enter_context(nc.semaphore("sLB"))
        sLC = _st.enter_context(nc.semaphore("sLC"))
        sSD = _st.enter_context(nc.semaphore("sSD"))
        sGA = _st.enter_context(nc.semaphore("sGA"))
        sPE = _st.enter_context(nc.semaphore("sPE"))
        sVE = _st.enter_context(nc.semaphore("sVE"))
        sSC = _st.enter_context(nc.semaphore("sSC"))
        block = _st.enter_context(nc.Block())
        ctx.sems = {"LD": sLD, "LA": sLA, "LB": sLB, "LC": sLC,
                    "SD": sSD, "GA": sGA,
                    "PE": sPE, "VE": sVE, "SC": sSC}
        pb = [pb0, pb1, pb2, pb3, pb4, pb5]
        pt = [pt0, pt1]

        vA = arA.ap().bitcast(BF16)
        y1fm = vA
        attnout = vA
        y2tm = vA
        vB = arB.ap()
        qT = vB
        x1bf = vB
        wbuf = [vB[:, 0:8192], vB[:, 8192:16384]]
        x1c = arB.ap().bitcast(F32)[:, 8192:12288]
        otile = arB.ap().bitcast(F32)[:, 12288:16384]
        vC16 = arC.ap().bitcast(BF16)
        xchunk = vC16[:, 0:16384]
        wkv_sb = vC16[:, 16384:24576]
        wqbuf = [vC16[:, 0:4096], vC16[:, 4096:8192]]
        wobuf = wqbuf
        acc_bf = [vC16[:, 0:4096], vC16[:, 4096:8192]]
        w1buf = [vB[:, 0:4096], vB[:, 4096:8192]]
        w2buf = [vB[:, 8192:12288], vB[:, 12288:16384]]
        w2sc = vC16[:, 16384:20480]
        sc_scr = vC16[:, 12288:16384]
        prod = [vC16[:, 20480:24576], vC16[:, 8192:12288]]
        sb_ykv = vB[:, 0:1024]
        sb_nrm_kv = vB[:, 1024:2048]
        sb_vfm = vB[:, 2048:2560]
        vBf32 = arB.ap().bitcast(F32)
        rows_kv = vBf32[0:1, 1280:3328]
        vCf32 = arC.ap().bitcast(F32)
        sb_x1o = vCf32[:, 4096:5120]
        sb_x1f = vCf32[:, 5120:6144]
        rows_ln2 = vCf32[0:1, 6144:8192]
        y2r = [vC16[:, 8192:9216], vC16[:, 9216:10240]]
        sb_nrm_t = vC16[:, 10240:11264]
        sb_mask = vC16[:, 14336:16384]
        sb_rbc = vC16[:, 6144:6656]

        P_tiles = [sb_pt0, sb_pt1]
        ie_dst = [sb_pt0.ap()[:, 0:256], sb_pt1.ap()[:, 0:256],
                  sb_sq.ap()[:, 512:768]]
        c2b = [sb_c2b0.ap(), sb_c2b1.ap(),
               sb_sq.ap()[0:16, 768:1024]]

        def q_off(h, qc):
            return h * 1024 + qc * 512

        def stream_sync(eng):
            E, inc, wait = ctx.E, ctx.inc, ctx.wait
            for nm, dst, src in [("c_bias", sb_bias.ap(), d_bias),
                                 ("c_u1", sb_u1.ap(), d_u1),
                                 ("c_id", sb_id.ap(), d_id),
                                 ("c_ones", sb_ones.ap(), d_ones),
                                 ("c_c1", sb_c1.ap(), d_c1),
                                 ("c_wkv", wkv_sb, d_wkv)]:
                inc(E(lambda dst=dst, src=src: eng.dma_start(out=dst, in_=src)),
                    "LC", nm, 16)
            ctx.mark("LC", "b_pre")
            wait(eng, "LC", "b_pre")
            for tc in range(4):
                for kt in range(NFO):
                    if tc >= 1:
                        wait(eng, "VE", f"y_{tc-1}_{kt}")
                    inc(E(lambda tc=tc, kt=kt: eng.dma_start(
                        out=xchunk[:, kt * 512:(kt + 1) * 512],
                        in_=d_xfmt[kt][:, tc * 512:(tc + 1) * 512])),
                        "LC", f"xc_{tc}_{kt}", 16)
                ctx.mark("LC", f"b_xc{tc}")
                wait(eng, "LC", f"b_xc{tc}")
            if stage < 1:
                return
            wait(eng, "VE", "y_3_31")
            for dd in range(4):
                inc(E(lambda dd=dd: eng.dma_start(
                    out=sb_mask[:, dd * 512:(dd + 1) * 512],
                    in_=d_mask[dd])), "LC", f"c_mask{dd}", 16)
            ctx.mark("LC", "b_mask")
            wait(eng, "LC", "b_mask")
            for m in range(NFO):
                if m >= 2:
                    wait(eng, "PE", f"q_{m-2}")
                inc(E(lambda m=m: eng.dma_start(out=wqbuf[m % 2],
                                                in_=d_wq[m])),
                    "LA" if m % 2 == 0 else "LB", f"wq_{m}", 16)
            if stage < 3:
                return
            for fo in range(NFO):
                if fo == 0:
                    wait(eng, "VE", "ao_31_1")
                if fo >= 2:
                    wait(eng, "PE", f"wo_{fo-2}")
                inc(E(lambda fo=fo: eng.dma_start(out=wobuf[fo % 2],
                                                  in_=d_wo[fo])),
                    "LA" if fo % 2 == 0 else "LB", f"wwo_{fo}", 16)
                if fo >= 1:
                    wait(eng, "VE", f"x1_{fo-1}")
                inc(E(lambda fo=fo: eng.dma_start(out=sb_x1f,
                                                  in_=d_xf32[fo])),
                    "LA" if fo % 2 == 0 else "LB", f"xf_{fo}", 16)
            if stage < 5:
                return
            wait(eng, "PE", f"av_31_1_{len(SLOTSETS[1])-1}")
            inc(E(lambda: eng.dma_start(out=sb_pt0.ap()[:, 256:512],
                                        in_=d_iota)), "LC", "iota", 16)
            ctx.mark("LC", "b_iota")
            wait(eng, "LC", "b_iota")
            def sy_c2(g2):
                t2, j2 = g2 // NB, g2 % NB
                if g2 >= 3:
                    p2, q2 = (g2 - 3) // NB, (g2 - 3) % NB
                    wait(eng, "PE", f"int_{p2}_{q2}")
                elif g2 == 2:
                    wait(eng, "PE", f"st2_{NFO-1}")
                inc(E(lambda t2=t2, j2=j2, g2=g2: eng.dma_start(
                    out=c2b[g2 % 3],
                    in_=d_c2[:, j2 * 256:(j2 + 1) * 256])),
                    ("LA", "LB", "LD")[g2 % 3], f"c2_{t2}_{j2}", 16)

            def sy_x1c(tt):
                if tt == 0:
                    wait(eng, "VE", "ue")
                    wait(eng, "SD", f"x1s_{NFO-1}")
                else:
                    wait(eng, "VE", f"fa_{tt-1}_31")
                for fo in range(NFO):
                    inc(E(lambda tt=tt, fo=fo: eng.dma_start(
                        out=x1c[:, fo * 128:(fo + 1) * 128],
                        in_=d_x1s[fo][:, tt * 128:(tt + 1) * 128])),
                        "LC", f"x1c_{tt}_{fo}", 16)
                ctx.mark("LC", f"b_x1c{tt}")
                wait(eng, "LC", f"b_x1c{tt}")

            for g in range(NTT * NB):
                sy_c2(g)
                if g % NB == 3:
                    sy_x1c(g // NB)
            if NB <= 3:
                raise AssertionError

        def stream_gpsimd(eng):
            E, inc, wait = ctx.E, ctx.inc, ctx.wait
            if debug:
                wait(eng, "VE", "y_1_31")
                inc(E(lambda: eng.dma_start(out=dbg["dbg_y1"], in_=y1fm)),
                    "GA", "dbg_y1", 16)
                wait(eng, "VE", "kve_3")
                wait(eng, "SC", "vt_3_3")
                inc(E(lambda: eng.dma_start(out=dbg["dbg_kf"], in_=sb_kf.ap())),
                    "GA", "dbg_kf", 16)
                inc(E(lambda: eng.dma_start(out=dbg["dbg_vt"], in_=sb_vt.ap())),
                    "GA", "dbg_vt", 16)
                wait(eng, "VE", "qe_31")
                inc(E(lambda: eng.dma_start(out=dbg["dbg_q"], in_=qT)),
                    "GA", "dbg_q", 16)
                wait(eng, "VE", "ao_31_1")
                inc(E(lambda: eng.dma_start(out=dbg["dbg_ao"], in_=attnout)),
                    "GA", "dbg_ao", 16)
                wait(eng, "VE", "x1_31")
                inc(E(lambda: eng.dma_start(out=dbg["dbg_x1"], in_=x1bf)),
                    "GA", "dbg_x1", 16)
                wait(eng, "VE", "ue")
                inc(E(lambda: eng.dma_start(out=dbg["dbg_u"], in_=sb_u.ap())),
                    "GA", "dbg_u", 16)
                wait(eng, "SC", "tre_31_7")
                inc(E(lambda: eng.dma_start(out=dbg["dbg_y2"], in_=y2tm)),
                    "GA", "dbg_y2", 16)
            if stage < 5:
                return
            NG_ALL = NTT * NB
            for g in range(NG_ALL):
                tt, j = g // NB, g % NB
                wait(eng, "VE", f"am_{tt}_{j}")
                if g >= 2:
                    pt_, pj = (g - 2) // NB, (g - 2) % NB
                    wait(eng, "VE", f"pr_{pt_}_{pj}")
                inc(E(lambda tt=tt, j=j, g=g: eng.indirect_dma_start(
                    w1buf[j % 2],
                    None,
                    d_w12,
                    IndirectOffsetOnAxis(
                        ap=sb_i8.ap()[:, g % 4:g % 4 + 1], axis=0),
                    element_offset=j * 256 * 8192)),
                    "GA", f"g_{tt}_{j}", 16)
                if g >= 2:
                    wait(eng, "VE", f"ws_{pt_}_{pj}")
                inc(E(lambda tt=tt, j=j, g=g: eng.indirect_dma_start(
                    w2buf[j % 2],
                    None,
                    d_w12,
                    IndirectOffsetOnAxis(
                        ap=sb_i8.ap()[:, g % 4:g % 4 + 1], axis=0),
                    element_offset=j * 256 * 8192 + 4096)),
                    "GA", f"g2_{tt}_{j}", 16)

        def stream_tensor(eng):
            E, inc, wait = ctx.E, ctx.inc, ctx.wait
            wait(eng, "LC", "b_pre")
            for tc in range(4):
                wait(eng, "LC", f"b_xc{tc}")
                for kt in range(NFO):
                    E(lambda tc=tc, kt=kt: eng.matmul(
                        pb[0][0:1, :], sb_ones.ap()[:, 0:1],
                        xchunk[:, kt * 512:(kt + 1) * 512],
                        start=(kt == 0), stop=(kt == NFO - 1)))
                    wait(eng, "SC", f"sq_{tc}_{kt}")
                    inc(E(lambda kt=kt: eng.matmul(
                        pb[1][0:1, :], sb_ones.ap()[:, 0:1],
                        sb_sq.ap()[:, (kt % 2) * 512:(kt % 2) * 512 + 512],
                        start=(kt == 0), stop=(kt == NFO - 1))),
                        "PE", f"sx_{tc}_{kt}", 1)
                wait(eng, "VE", f"rows_{tc}")
                E(lambda: eng.matmul(pb[2][:, :], sb_ones.ap()[0:1, 0:128],
                                     sb_rbf.ap()[0:1, 0:512],
                                     start=True, stop=True))
                inc(E(lambda: eng.matmul(pb[3][:, :], sb_ones.ap()[0:1, 0:128],
                                         sb_rbf.ap()[0:1, 512:1024],
                                         start=True, stop=True)),
                    "PE", f"bc_{tc}", 1)
                for kt in range(NFO):
                    wait(eng, "VE", f"y_{tc}_{kt}")
                    if tc < 2:
                        dst = y1fm[:, kt * 1024 + tc * 512:
                                   kt * 1024 + tc * 512 + 512]
                    else:
                        dst = sb_ykv[:, (kt % 2) * 512:(kt % 2) * 512 + 512]
                    E(lambda kt=kt, dst=dst: eng.matmul(
                        pb[4][:, :], wkv_sb[:, kt * 256:kt * 256 + 128], dst,
                        start=(kt == 0), stop=(kt == NFO - 1)))
                    inc(E(lambda kt=kt, dst=dst: eng.matmul(
                        pb[5][:, :], wkv_sb[:, kt * 256 + 128:kt * 256 + 256],
                        dst, start=(kt == 0), stop=(kt == NFO - 1))),
                        "PE", f"kv_{tc}_{kt}", 1)
                wait(eng, "VE", f"kve_{tc}")
                for i in range(4):
                    if i >= 2:
                        wait(eng, "SC", f"vt_{tc}_{i-2}")
                    elif tc >= 1:
                        wait(eng, "SC", f"vt_{tc-1}_{2+i}")
                    inc(E(lambda i=i: eng.transpose(
                        pt[i % 2][:, :], sb_vfm[:, i * 128:(i + 1) * 128],
                        sb_id.ap())), "PE", f"vt_{tc}_{i}", 1)
            if stage < 1:
                return
            for m in range(NFO):
                wait(eng, "LA" if m % 2 == 0 else "LB", f"wq_{m}")
                if m >= 2:
                    wait(eng, "VE", f"qe_{m-2}")
                for kt in range(NFO):
                    E(lambda m=m, kt=kt: eng.matmul(
                        pb[(m % 2) * 2][:, :],
                        wqbuf[m % 2][:, kt * 128:(kt + 1) * 128],
                        y1fm[:, kt * 1024:kt * 1024 + 512],
                        start=(kt == 0), stop=(kt == NFO - 1)))
                    i2 = E(lambda m=m, kt=kt: eng.matmul(
                        pb[(m % 2) * 2 + 1][:, :],
                        wqbuf[m % 2][:, kt * 128:(kt + 1) * 128],
                        y1fm[:, kt * 1024 + 512:kt * 1024 + 1024],
                        start=(kt == 0), stop=(kt == NFO - 1)))
                    if kt == NFO - 1:
                        inc(i2, "PE", f"q_{m}", 1)
            if stage < 2:
                return
            wait(eng, "LC", "b_mask")
            for qc in (0, 1):
                slots = SLOTSETS[qc]
                nsl = len(slots)
                for h in range(NH):
                    wait(eng, "VE", f"qe_{NFO-1}" if (qc == 0 and h == 0)
                         else f"qe_{h}")
                    if not (qc == 0 and h == 0):
                        ph, pqc = (h - 1, qc) if h > 0 else (NH - 1, 0)
                        wait(eng, "VE", f"ao_{ph}_{pqc}")
                    for si in range(nsl + 1):
                        if si < nsl:
                            s = slots[si]
                            if si >= 2:
                                wait(eng, "SC", f"exp_{h}_{qc}_{si-2}")
                            mk = MASKED[qc].get(s)
                            psA = pb[si % 2]
                            i1 = E(lambda s=s, h=h, qc=qc, psA=psA, mk=mk:
                                   eng.matmul(
                                       psA[:, :],
                                       sb_kf.ap()[:, s * 128:(s + 1) * 128],
                                       qT[:, q_off(h, qc):q_off(h, qc) + 512],
                                       start=True, stop=(mk is None)))
                            if mk is not None:
                                i1 = E(lambda mk=mk, psA=psA: eng.matmul(
                                    psA[:, :], sb_u1.ap(),
                                    sb_mask[:, mk * 512:(mk + 1) * 512],
                                    start=False, stop=True))
                            inc(i1, "PE", f"sc_{h}_{qc}_{si}", 1)
                        if si >= 1:
                            pi = si - 1
                            s_prev = slots[pi]
                            wait(eng, "SC", f"exp_{h}_{qc}_{pi}")
                            E(lambda s_prev=s_prev, pi=pi, nsl=nsl: eng.matmul(
                                pb[2][:, :],
                                sb_vt.ap()[:, s_prev * 128:(s_prev + 1) * 128],
                                P_tiles[pi % 2].ap()[:, :],
                                start=(pi == 0), stop=(pi == nsl - 1)))
                            inc(E(lambda pi=pi, nsl=nsl: eng.matmul(
                                pb[3][0:1, :], sb_ones.ap()[:, 0:1],
                                P_tiles[pi % 2].ap()[:, :],
                                start=(pi == 0), stop=(pi == nsl - 1))),
                                "PE", f"av_{h}_{qc}_{pi}", 1)
                    wait(eng, "VE", f"r_{h}_{qc}")
                    if not (qc == 0 and h == 0):
                        ph, pqc = (h - 1, qc) if h > 0 else (NH - 1, 0)
                        wait(eng, "SC", f"rbe_{ph}_{pqc}")
                    inc(E(lambda: eng.matmul(pb[4][:, :],
                                             sb_ones.ap()[0:1, 0:128],
                                             sb_rbf.ap()[0:1, 0:512],
                                             start=True, stop=True)),
                        "PE", f"rb_{h}_{qc}", 1)
            if stage < 3:
                return
            for fo in range(NFO):
                wait(eng, "LA" if fo % 2 == 0 else "LB", f"xf_{fo}")
                if fo == 0:
                    wait(eng, "SC", f"exp_31_1_{len(SLOTSETS[1])-1}")
                    wait(eng, "SC", "rbe_31_1")
                if fo == 1:
                    wait(eng, "VE", "ao_31_1")
                if fo >= 2:
                    wait(eng, "VE", f"x1_{fo-2}")
                for kt in range(NFO):
                    E(lambda fo=fo, kt=kt: eng.matmul(
                        pb[(fo % 2) * 2][:, :],
                        wobuf[fo % 2][:, kt * 128:(kt + 1) * 128],
                        attnout[:, kt * 1024:kt * 1024 + 512],
                        start=(kt == 0), stop=(kt == NFO - 1)))
                    i2 = E(lambda fo=fo, kt=kt: eng.matmul(
                        pb[(fo % 2) * 2 + 1][:, :],
                        wobuf[fo % 2][:, kt * 128:(kt + 1) * 128],
                        attnout[:, kt * 1024 + 512:kt * 1024 + 1024],
                        start=(kt == 0), stop=(kt == NFO - 1)))
                    if kt == NFO - 1:
                        inc(i2, "PE", f"wo_{fo}", 1)
                wait(eng, "VE", f"x1_{fo}")
                wait(eng, "SC", f"sq2_{fo}")
                E(lambda fo=fo: eng.matmul(
                    pb[4][0:1, :], sb_ones.ap()[:, 0:1],
                    x1bf[:, fo * 1024:fo * 1024 + 512],
                    start=(fo == 0), stop=False, skip_group_check=True))
                E(lambda fo=fo: eng.matmul(
                    pb[4][32:33, :], sb_ones.ap()[:, 0:1],
                    x1bf[:, fo * 1024 + 512:fo * 1024 + 1024],
                    start=False, stop=False, skip_group_check=True))
                E(lambda fo=fo: eng.matmul(
                    pb[4][64:65, :], sb_ones.ap()[:, 0:1],
                    sb_sq.ap()[:, 0:512],
                    start=False, stop=(fo == NFO - 1), skip_group_check=True))
                inc(E(lambda fo=fo: eng.matmul(
                    pb[5][0:1, :], sb_ones.ap()[:, 0:1],
                    sb_sq.ap()[:, 512:1024],
                    start=(fo == 0), stop=(fo == NFO - 1))),
                    "PE", f"st2_{fo}", 1)
            wait(eng, "VE", "rows2")
            E(lambda: eng.matmul(pb[0][:, :], sb_ones.ap()[0:1, 0:128],
                                 sb_rbf.ap()[0:1, 0:512], start=True, stop=True))
            E(lambda: eng.matmul(pb[1][:, :], sb_ones.ap()[0:1, 0:128],
                                 sb_rbf.ap()[0:1, 512:1024],
                                 start=True, stop=True))
            E(lambda: eng.matmul(pb[2][:, :], sb_ones.ap()[0:1, 0:128],
                                 sb_rbf.ap()[0:1, 1024:1536],
                                 start=True, stop=True))
            inc(E(lambda: eng.matmul(pb[3][:, :], sb_ones.ap()[0:1, 0:128],
                                     sb_rbf.ap()[0:1, 1536:2048],
                                     start=True, stop=True)),
                "PE", "bc2", 1)
            if stage < 4:
                return
            for fo in range(NFO):
                wait(eng, "VE", f"y2_{fo}")
                E(lambda fo=fo: eng.matmul(
                    pb[4][0:16, :], sb_c1.ap()[:, fo * 16:(fo + 1) * 16],
                    y2r[fo % 2][:, 0:512],
                    start=(fo == 0), stop=(fo == NFO - 1)))
                i2 = E(lambda fo=fo: eng.matmul(
                    pb[5][0:16, :], sb_c1.ap()[:, fo * 16:(fo + 1) * 16],
                    y2r[fo % 2][:, 512:1024],
                    start=(fo == 0), stop=(fo == NFO - 1)))
                if fo == NFO - 1:
                    inc(i2, "PE", "u", 1)
                for i in range(8):
                    gidx = fo * 8 + i
                    if gidx >= 2:
                        pp = gidx - 2
                        wait(eng, "SC", f"tre_{pp // 8}_{pp % 8}")
                    inc(E(lambda fo=fo, i=i: eng.transpose(
                        pt[i % 2][:, :],
                        y2r[fo % 2][:, i * 128:(i + 1) * 128],
                        sb_id.ap())), "PE", f"tr_{fo}_{i}", 1)
            if stage < 5:
                return
            def pe_int(g2):
                t2, j2 = g2 // NB, g2 % NB
                wait(eng, ("LA", "LB", "LD")[g2 % 3], f"c2_{t2}_{j2}")
                if g2 == 0:
                    wait(eng, "VE", "ue")
                if g2 >= 3:
                    p2, q2 = (g2 - 3) // NB, (g2 - 3) % NB
                    wait(eng, "SC", f"ie_{p2}_{q2}")
                inc(E(lambda t2=t2, j2=j2, g2=g2: eng.matmul(
                    pb[2 + g2 % 3][:, 0:256],
                    sb_u.ap()[0:16, t2 * 128:(t2 + 1) * 128],
                    c2b[g2 % 3][:, :],
                    start=True, stop=True)),
                    "PE", f"int_{t2}_{j2}", 1)

            def pe_ft(tt):
                wait(eng, "VE", f"o_{tt}_{NB-1}")
                for fo in range(NFO):
                    gidx = tt * NFO + fo
                    if gidx >= 2:
                        pp = gidx - 2
                        wait(eng, "VE", f"fa_{pp // NFO}_{pp % NFO}")
                    inc(E(lambda fo=fo: eng.transpose(
                        pt[fo % 2][:, :], acc_bf[0][:, fo * 128:(fo + 1) * 128],
                        sb_id.ap())), "PE", f"ft_{tt}_{fo}", 1)

            for g in range(NTT * NB):
                pe_int(g)
                if g % NB == 3 and g >= NB:
                    pe_ft(g // NB - 1)
            pe_ft(NTT - 1)

        def stream_scalar(eng):
            E, inc, wait = ctx.E, ctx.inc, ctx.wait
            pass
            for tc in range(4):
                wait(eng, "LC", f"b_xc{tc}")
                for kt in range(NFO):
                    if kt >= 2:
                        wait(eng, "PE", f"sx_{tc}_{kt-2}")
                    elif tc >= 1:
                        wait(eng, "PE", f"sx_{tc-1}_{NFO-2+kt}")
                    inc(E(lambda kt=kt, tc=tc: eng.activation(
                        sb_sq.ap()[:, (kt % 2) * 512:(kt % 2) * 512 + 512],
                        xchunk[:, kt * 512:(kt + 1) * 512], AF.Square)),
                        "SC", f"sq_{tc}_{kt}", 1)
                wait(eng, "VE", f"var_{tc}")
                inc(E(lambda: eng.activation(
                    rows_kv[0:1, 1024:1536], rows_kv[0:1, 1536:2048],
                    AF.Sqrt)), "SC", f"srt_{tc}", 1)
                for i in range(4):
                    wait(eng, "PE", f"vt_{tc}_{i}")
                    inc(E(lambda tc=tc, i=i: eng.copy(
                        sb_vt.ap()[:, (tc * 4 + i) * 128:(tc * 4 + i + 1) * 128],
                        pt[i % 2][:, :])), "SC", f"vt_{tc}_{i}", 1)
            if stage < 2:
                if ctx.emit:
                    eng.wait_ge(ctx.sems["SD"], 0)
                return
            wait(eng, "LC", "b_pre")
            for qc in (0, 1):
                slots = SLOTSETS[qc]
                for h in range(NH):
                    ng = n_groups(h)
                    w = 512 // ng
                    for si in range(len(slots)):
                        wait(eng, "PE", f"sc_{h}_{qc}_{si}")
                        if si >= 2:
                            wait(eng, "PE", f"av_{h}_{qc}_{si-2}")
                        psA = pb[si % 2]
                        for g in range(ng):
                            col = COLOF[(h, qc, si, g)]
                            i1 = E(lambda si=si, g=g, w=w, col=col, psA=psA:
                                   eng.activation(
                                       P_tiles[si % 2].ap()[:, g * w:(g + 1) * w],
                                       psA[:, g * w:(g + 1) * w],
                                       AF.Exp,
                                       bias=sb_bias.ap()[:, col:col + 1],
                                       scale=SCALE))
                            if g == ng - 1:
                                inc(i1, "SC", f"exp_{h}_{qc}_{si}", 1)
                    wait(eng, "PE", f"rb_{h}_{qc}")
                    inc(E(lambda: eng.copy(sb_rbc[:, :], pb[4][:, :])),
                        "SC", f"rbe_{h}_{qc}", 1)
            if stage < 3:
                return
            for fo in range(NFO):
                wait(eng, "VE", f"x1_{fo}")
                inc(E(lambda fo=fo: eng.dma_start(out=d_x1s[fo],
                                                  in_=sb_x1o)),
                    "SD", f"x1s_{fo}", 16)
                if fo >= 1:
                    wait(eng, "PE", f"st2_{fo-1}")
                inc(E(lambda fo=fo: eng.activation(
                    sb_sq.ap()[:, :], x1bf[:, fo * 1024:(fo + 1) * 1024],
                    AF.Square)), "SC", f"sq2_{fo}", 1)
            wait(eng, "VE", "var2")
            E(lambda: eng.activation(
                sb_x1o[0:1, 0:512], rows_ln2[0:1, 1024:1536],
                AF.Sqrt))
            inc(E(lambda: eng.activation(
                sb_x1o[0:1, 512:1024], rows_ln2[0:1, 1536:2048],
                AF.Sqrt)), "SC", "srt2", 1)
            wait(eng, "PE", "bc2")
            E(lambda: eng.copy(sb_mbc.ap()[:, 0:512], pb[0][:, :]))
            E(lambda: eng.copy(sb_mbc.ap()[:, 512:1024], pb[1][:, :]))
            E(lambda: eng.copy(sb_mbc.ap()[:, 1024:1536], pb[2][:, :]))
            inc(E(lambda: eng.copy(sb_mbc.ap()[:, 1536:2048], pb[3][:, :])),
                "SC", "bc2e", 1)
            if stage < 4:
                return
            for fo in range(NFO):
                for i in range(8):
                    wait(eng, "PE", f"tr_{fo}_{i}")
                    inc(E(lambda fo=fo, i=i: eng.copy(
                        y2tm[:, i * 4096 + fo * 128: i * 4096 + (fo + 1) * 128],
                        pt[i % 2][:, :])), "SC", f"tre_{fo}_{i}", 1)
            if stage >= 5:
                NG_ALL = NTT * NB

                def sc_ie(g2):
                    t2, j2 = g2 // NB, g2 % NB
                    wait(eng, "PE", f"int_{t2}_{j2}")
                    if g2 >= 3:
                        p2, q2 = (g2 - 3) // NB, (g2 - 3) % NB
                        wait(eng, "VE", f"am_{p2}_{q2}")
                    inc(E(lambda g2=g2: eng.copy(
                        ie_dst[g2 % 3][:, 0:256],
                        pb[2 + g2 % 3][:, 0:256])),
                        "SC", f"ie_{t2}_{j2}", 1)

                sc_ie(0)
                sc_ie(1)
                sc_ie(2)
                for g in range(NG_ALL):
                    tt, j = g // NB, g % NB
                    if g + 3 < NG_ALL:
                        sc_ie(g + 3)
                    wait(eng, "VE", f"pr_{tt}_{j}")
                    inc(E(lambda j=j, g=g: eng.activation(
                        sc_scr[:, :], prod[g % 2][:, :], AF.Copy,
                        accum_out=sb_h.ap()[:, j:j + 1])),
                        "SC", f"ha_{tt}_{j}", 1)
                    wait(eng, "SC", f"ha_{tt}_{j}")
                    inc(E(lambda j=j: eng.activation(
                        sb_hr.ap()[:, j:j + 1], sb_h.ap()[:, j:j + 1],
                        AF.Relu)), "SC", f"hrl_{tt}_{j}", 1)
                    if j == 2 and tt >= 1:
                        wait(eng, "VE", f"fa_{tt-1}_31")
                        inc(E(lambda tt=tt: eng.dma_start(out=d_y[tt - 1],
                                                          in_=otile)),
                            "SD", f"y_{tt-1}", 16)
                wait(eng, "VE", f"fa_{NTT-1}_31")
                inc(E(lambda: eng.dma_start(out=d_y[NTT - 1], in_=otile)),
                    "SD", f"y_{NTT-1}", 16)
                if ctx.emit:
                    eng.wait_ge(ctx.sems["SD"], ctx.ev[("SD", f"y_{NTT-1}")])
            else:
                inc(E(lambda: eng.dma_start(out=d_y[0][:, 0:NCOL],
                                            in_=sb_bias.ap())),
                    "SD", "y_stub", 16)
                if ctx.emit:
                    eng.wait_ge(ctx.sems["SD"], ctx.ev[("SD", "y_stub")])

        def stream_vector(eng):
            E, inc, wait = ctx.E, ctx.inc, ctx.wait
            pass
            for tc in range(4):
                wait(eng, "PE", f"sx_{tc}_{NFO-1}")
                E(lambda: eng.tensor_scalar(rows_kv[0:1, 0:512], pb[0][0:1, :],
                                            1.0 / D, None, ALU.mult))
                E(lambda: eng.tensor_scalar(rows_kv[0:1, 512:1024], pb[1][0:1, :],
                                            1.0 / D, None, ALU.mult))
                E(lambda: eng.tensor_tensor(rows_kv[0:1, 1536:2048],
                                            rows_kv[0:1, 0:512], rows_kv[0:1, 0:512],
                                            ALU.mult))
                E(lambda: eng.tensor_tensor(rows_kv[0:1, 1024:1536],
                                            rows_kv[0:1, 512:1024],
                                            rows_kv[0:1, 1536:2048],
                                            ALU.subtract))
                inc(E(lambda: eng.tensor_scalar(rows_kv[0:1, 1536:2048],
                                                rows_kv[0:1, 1024:1536],
                                                LN_EPS, None, ALU.add)),
                    "VE", f"var_{tc}", 1)
                wait(eng, "SC", f"srt_{tc}")
                E(lambda: eng.tensor_copy(sb_rbf.ap()[0:1, 0:512],
                                          rows_kv[0:1, 0:512]))
                E(lambda: eng.reciprocal(rows_kv[0:1, 0:512],
                                         rows_kv[0:1, 1024:1536]))
                inc(E(lambda: eng.tensor_copy(sb_rbf.ap()[0:1, 512:1024],
                                              rows_kv[0:1, 0:512])),
                    "VE", f"rows_{tc}", 1)
                wait(eng, "PE", f"bc_{tc}")
                E(lambda: eng.tensor_copy(sb_mbc.ap()[:, 0:512], pb[2][:, :]))
                inc(E(lambda: eng.tensor_copy(sb_mbc.ap()[:, 512:1024],
                                              pb[3][:, :])),
                    "VE", f"mbc_{tc}", 1)
                for kt in range(NFO):
                    if tc < 2:
                        dst = y1fm[:, kt * 1024 + tc * 512:
                                   kt * 1024 + tc * 512 + 512]
                    else:
                        dst = sb_ykv[:, (kt % 2) * 512:(kt % 2) * 512 + 512]
                        if kt >= 2:
                            wait(eng, "PE", f"kv_{tc}_{kt-2}")
                    E(lambda kt=kt: eng.tensor_tensor(
                        sb_nrm_kv[:, (kt % 2) * 512:(kt % 2) * 512 + 512],
                        xchunk[:, kt * 512:(kt + 1) * 512],
                        sb_mbc.ap()[:, 0:512], ALU.subtract))
                    inc(E(lambda kt=kt, dst=dst: eng.tensor_tensor(
                        dst,
                        sb_nrm_kv[:, (kt % 2) * 512:(kt % 2) * 512 + 512],
                        sb_mbc.ap()[:, 512:1024], ALU.mult)),
                        "VE", f"y_{tc}_{kt}", 1)
                wait(eng, "PE", f"kv_{tc}_{NFO-1}")
                E(lambda tc=tc: eng.tensor_copy(
                    sb_kf.ap()[:, tc * 512:(tc + 1) * 512], pb[4][:, :]))
                inc(E(lambda: eng.tensor_copy(sb_vfm[:, :], pb[5][:, :])),
                    "VE", f"kve_{tc}", 1)
            if stage < 1:
                return
            for m in range(NFO):
                wait(eng, "PE", f"q_{m}")
                E(lambda m=m: eng.tensor_copy(
                    qT[:, m * 1024:m * 1024 + 512], pb[(m % 2) * 2][:, :]))
                inc(E(lambda m=m: eng.tensor_copy(
                    qT[:, m * 1024 + 512:m * 1024 + 1024],
                    pb[(m % 2) * 2 + 1][:, :])), "VE", f"qe_{m}", 1)
            if stage < 2:
                return
            for qc in (0, 1):
                nsl = len(SLOTSETS[qc])
                for h in range(NH):
                    wait(eng, "PE", f"av_{h}_{qc}_{nsl-1}")
                    E(lambda: eng.reciprocal(sb_rrow.ap()[0:1, :],
                                             pb[3][0:1, :]))
                    inc(E(lambda: eng.tensor_copy(sb_rbf.ap()[0:1, 0:512],
                                                  sb_rrow.ap()[0:1, :])),
                        "VE", f"r_{h}_{qc}", 1)
                    wait(eng, "SC", f"rbe_{h}_{qc}")
                    inc(E(lambda h=h, qc=qc: eng.tensor_tensor(
                        attnout[:, q_off(h, qc):q_off(h, qc) + 512],
                        pb[2][:, :], sb_rbc[:, :], ALU.mult)),
                        "VE", f"ao_{h}_{qc}", 1)
            if stage < 3:
                return
            for fo in range(NFO):
                wait(eng, "PE", f"wo_{fo}")
                wait(eng, "LA" if fo % 2 == 0 else "LB", f"xf_{fo}")
                if fo >= 1:
                    wait(eng, "SD", f"x1s_{fo-1}")
                E(lambda fo=fo: eng.tensor_tensor(
                    sb_x1o[:, 0:512], pb[(fo % 2) * 2][:, :],
                    sb_x1f[:, 0:512], ALU.add))
                E(lambda fo=fo: eng.tensor_tensor(
                    sb_x1o[:, 512:1024], pb[(fo % 2) * 2 + 1][:, :],
                    sb_x1f[:, 512:1024], ALU.add))
                inc(E(lambda fo=fo: eng.tensor_copy(
                    x1bf[:, fo * 1024:(fo + 1) * 1024], sb_x1o[:, :])),
                    "VE", f"x1_{fo}", 1)
            wait(eng, "PE", f"st2_{NFO-1}")
            wait(eng, "SD", f"x1s_{NFO-1}")
            x1fr = sb_x1f[0:1, :]
            x1or = sb_x1o[0:1, :]
            E(lambda: eng.tensor_scalar(rows_ln2[0:1, 0:512], pb[4][0:1, :],
                                        1.0 / D, None, ALU.mult))
            E(lambda: eng.tensor_scalar(rows_ln2[0:1, 512:1024], pb[4][32:33, :],
                                        1.0 / D, None, ALU.mult))
            E(lambda: eng.tensor_scalar(rows_ln2[0:1, 1024:1536], pb[4][64:65, :],
                                        1.0 / D, None, ALU.mult))
            E(lambda: eng.tensor_scalar(rows_ln2[0:1, 1536:2048], pb[5][0:1, :],
                                        1.0 / D, None, ALU.mult))
            E(lambda: eng.tensor_tensor(x1fr[:, 0:512],
                                        rows_ln2[0:1, 0:512], rows_ln2[0:1, 0:512],
                                        ALU.mult))
            E(lambda: eng.tensor_tensor(x1fr[:, 512:1024],
                                        rows_ln2[0:1, 512:1024],
                                        rows_ln2[0:1, 512:1024], ALU.mult))
            E(lambda: eng.tensor_tensor(x1or[:, 0:512],
                                        rows_ln2[0:1, 1024:1536],
                                        x1fr[:, 0:512], ALU.subtract))
            E(lambda: eng.tensor_tensor(x1or[:, 512:1024],
                                        rows_ln2[0:1, 1536:2048],
                                        x1fr[:, 512:1024],
                                        ALU.subtract))
            E(lambda: eng.tensor_scalar(rows_ln2[0:1, 1024:1536],
                                        x1or[:, 0:512],
                                        LN_EPS, None, ALU.add))
            inc(E(lambda: eng.tensor_scalar(rows_ln2[0:1, 1536:2048],
                                            x1or[:, 512:1024],
                                            LN_EPS, None, ALU.add)),
                "VE", "var2", 1)
            wait(eng, "SC", "srt2")
            E(lambda: eng.reciprocal(x1fr[:, 0:512], x1or[:, 0:512]))
            E(lambda: eng.reciprocal(x1fr[:, 512:1024], x1or[:, 512:1024]))
            E(lambda: eng.tensor_copy(sb_rbf.ap()[0:1, 0:1024],
                                      rows_ln2[0:1, 0:1024]))
            inc(E(lambda: eng.tensor_copy(sb_rbf.ap()[0:1, 1024:2048],
                                          x1fr[:, 0:1024])),
                "VE", "rows2", 1)
            if stage < 4:
                return
            wait(eng, "SC", "bc2e")
            for fo in range(NFO):
                if fo >= 2:
                    wait(eng, "PE", f"tr_{fo-2}_7")
                E(lambda fo=fo: eng.tensor_tensor(
                    sb_nrm_t[:, :], x1bf[:, fo * 1024:(fo + 1) * 1024],
                    sb_mbc.ap()[:, 0:1024], ALU.subtract))
                inc(E(lambda fo=fo: eng.tensor_tensor(
                    y2r[fo % 2][:, :], sb_nrm_t[:, :],
                    sb_mbc.ap()[:, 1024:2048], ALU.mult)),
                    "VE", f"y2_{fo}", 1)
            wait(eng, "PE", "u")
            E(lambda: eng.tensor_copy(sb_u.ap()[0:16, 0:512], pb[4][0:16, :]))
            inc(E(lambda: eng.tensor_copy(sb_u.ap()[0:16, 512:1024],
                                          pb[5][0:16, :])), "VE", "ue", 1)
            if stage < 5:
                return
            NG_ALL = NTT * NB

            def ve_argmax(g2):
                t2, j2 = g2 // NB, g2 % NB
                wait(eng, "SC", f"ie_{t2}_{j2}")
                if g2 == 0:
                    wait(eng, "LC", "b_iota")
                inc(E(lambda g2=g2: eng.reduce_max(
                    sb_m8.ap()[:, 0:1], ie_dst[g2 % 3][:, 0:256],
                    axis=mybir.AxisListType.X)),
                    "VE", f"amx_{t2}_{j2}", 1)
                wait(eng, "VE", f"amx_{t2}_{j2}")
                E(lambda g2=g2: eng.tensor_scalar(
                    sb_pt1.ap()[:, 256:512], ie_dst[g2 % 3][:, 0:256],
                    sb_m8.ap()[:, 0:1], None, ALU.is_ge))
                E(lambda: eng.tensor_tensor(
                    sb_sq.ap()[:, 0:256], sb_pt1.ap()[:, 256:512],
                    sb_pt0.ap()[:, 256:512], ALU.mult))
                inc(E(lambda: eng.reduce_max(
                    sb_m8.ap()[:, 1:2], sb_sq.ap()[:, 0:256],
                    axis=mybir.AxisListType.X)),
                    "VE", f"amr_{t2}_{j2}", 1)
                wait(eng, "VE", f"amr_{t2}_{j2}")
                inc(E(lambda g2=g2: eng.tensor_copy(
                    sb_i8.ap()[:, g2 % 4:g2 % 4 + 1],
                    sb_m8.ap()[:, 1:2])),
                    "VE", f"am_{t2}_{j2}", 1)

            def ve_add(p):
                pt_, pj = p // NB, p % NB
                wait(eng, "SC", f"hrl_{pt_}_{pj}")
                wait(eng, "GA", f"g2_{pt_}_{pj}")
                inc(E(lambda pt_=pt_, pj=pj: eng.tensor_scalar(
                    w2sc[:, :], w2buf[pj % 2][:, :],
                    sb_hr.ap()[:, pj:pj + 1], None, ALU.mult)),
                    "VE", f"ws_{pt_}_{pj}", 1)
                if pj == 0:
                    if pt_ >= 1:
                        wait(eng, "PE", f"ft_{pt_-1}_{NFO-1}")
                    inc(E(lambda: eng.tensor_copy(
                        acc_bf[1][:, :], w2sc[:, :])),
                        "VE", f"o_{pt_}_0", 1)
                else:
                    if pj == 1 and pt_ >= 1:
                        wait(eng, "PE", f"ft_{pt_-1}_{NFO-1}")
                    inc(E(lambda pj=pj: eng.tensor_tensor(
                        acc_bf[(pj + 1) % 2][:, :], w2sc[:, :],
                        acc_bf[pj % 2][:, :], ALU.add)),
                        "VE", f"o_{pt_}_{pj}", 1)
                if pj == NB - 1:
                    for fo in range(NFO):
                        wait(eng, "PE", f"ft_{pt_}_{fo}")
                        wait(eng, "LC", f"b_x1c{pt_}")
                        if pt_ >= 1 and fo == 0:
                            wait(eng, "SD", f"y_{pt_-1}")
                        inc(E(lambda pt_=pt_, fo=fo: eng.tensor_tensor(
                            otile[:, fo * 128:(fo + 1) * 128],
                            pt[fo % 2][:, :],
                            x1c[:, fo * 128:(fo + 1) * 128], ALU.add)),
                            "VE", f"fa_{pt_}_{fo}", 1)

            ve_argmax(0)
            ve_argmax(1)
            ve_argmax(2)
            for g in range(NG_ALL):
                tt, j = g // NB, g % NB
                la = g + 3 < NG_ALL
                if la:
                    g3 = g + 3
                    t2, j2 = g3 // NB, g3 % NB
                    wait(eng, "SC", f"ie_{t2}_{j2}")
                    inc(E(lambda g3=g3: eng.reduce_max(
                        sb_m8.ap()[:, 0:1], ie_dst[g3 % 3][:, 0:256],
                        axis=mybir.AxisListType.X)),
                        "VE", f"amx_{t2}_{j2}", 1)
                wait(eng, "GA", f"g_{tt}_{j}")
                if g >= 2:
                    p2, q2 = (g - 2) // NB, (g - 2) % NB
                    wait(eng, "SC", f"ha_{p2}_{q2}")
                inc(E(lambda tt=tt, j=j, g=g: eng.tensor_tensor(
                    prod[g % 2][:, :], w1buf[j % 2][:, :],
                    y2tm[:, tt * 4096:(tt + 1) * 4096], ALU.mult)),
                    "VE", f"pr_{tt}_{j}", 1)
                if la:
                    wait(eng, "VE", f"amx_{t2}_{j2}")
                    E(lambda g3=g3: eng.tensor_scalar(
                        sb_pt1.ap()[:, 256:512], ie_dst[g3 % 3][:, 0:256],
                        sb_m8.ap()[:, 0:1], None, ALU.is_ge))
                    E(lambda: eng.tensor_tensor(
                        sb_sq.ap()[:, 0:256], sb_pt1.ap()[:, 256:512],
                        sb_pt0.ap()[:, 256:512], ALU.mult))
                    inc(E(lambda: eng.reduce_max(
                        sb_m8.ap()[:, 1:2], sb_sq.ap()[:, 0:256],
                        axis=mybir.AxisListType.X)),
                        "VE", f"amr_{t2}_{j2}", 1)
                if g >= 1:
                    ve_add(g - 1)
                if la:
                    wait(eng, "VE", f"amr_{t2}_{j2}")
                    inc(E(lambda g3=g3: eng.tensor_copy(
                        sb_i8.ap()[:, g3 % 4:g3 % 4 + 1],
                        sb_m8.ap()[:, 1:2])),
                        "VE", f"am_{t2}_{j2}", 1)
            ve_add(NG_ALL - 1)

        ctx.emit = False
        stream_sync(None)
        stream_gpsimd(None)
        stream_tensor(None)
        stream_scalar(None)
        stream_vector(None)

        ctx.emit = True

        @block.sync
        def _(eng):
            stream_sync(eng)

        @block.gpsimd
        def _(eng):
            stream_gpsimd(eng)

        @block.tensor
        def _(eng):
            stream_tensor(eng)

        @block.scalar
        def _(eng):
            stream_scalar(eng)

        @block.vector
        def _(eng):
            stream_vector(eng)

    return nc


def _bf(x):
    return np.ascontiguousarray(x).astype(ml_dtypes.bfloat16)


def kernel(L0_attn_hidden_states, L0_attn_Wq, L0_attn_Wkv, L0_attn_Wo,
           L0_ffn_c1_w, L0_ffn_c2_w, L0_ffn_w1, L0_ffn_w2t, _debug=False,
           _stage=99, _fsub=99):
    X = np.asarray(L0_attn_hidden_states, dtype=np.float32)
    Wq = np.asarray(L0_attn_Wq, dtype=np.float32)
    Wkv = np.asarray(L0_attn_Wkv, dtype=np.float32)
    Wo = np.asarray(L0_attn_Wo, dtype=np.float32)
    c1 = np.asarray(L0_ffn_c1_w, dtype=np.float32)
    c2 = np.asarray(L0_ffn_c2_w, dtype=np.float32)
    w1 = np.asarray(L0_ffn_w1, dtype=np.float32)
    w2t = np.asarray(L0_ffn_w2t, dtype=np.float32)

    wq_t = _bf(Wq.reshape(NFO, 128, NFO, 128).transpose(2, 1, 0, 3)
               .reshape(NFO, 128, D))
    wo_t = _bf(Wo.reshape(NFO, 128, NFO, 128).transpose(2, 1, 0, 3)
               .reshape(NFO, 128, D))
    wkv_t = _bf(Wkv.reshape(NFO, 128, 256).transpose(1, 0, 2).reshape(128, 8192))
    c1t = _bf(c1.T.reshape(NFO, 128, 16).transpose(1, 0, 2).reshape(128, 512))
    c2t = _bf(c2.T)
    w1w2 = np.concatenate([_bf(w1), _bf(w2t)], axis=1)
    mask4 = _host_masks()
    u1 = np.triu(np.ones((128, 128))).astype(ml_dtypes.bfloat16)
    ident = np.eye(128).astype(ml_dtypes.bfloat16)
    ones = np.ones((128, 128)).astype(ml_dtypes.bfloat16)
    iota = np.broadcast_to(np.arange(256, dtype=np.float32), (128, 256)) \
        .astype(ml_dtypes.bfloat16)

    nc = build_program(debug=_debug, stage=_stage, fsub=_fsub)
    in_maps = []
    for c in range(8):
        perm, _, _ = _core_geom(c)
        b = c // 2
        xp = X[b][perm]
        in_maps.append({
            "xfmt": _bf(xp.T.reshape(NFO, 128, S)),
            "xf32": np.ascontiguousarray(
                xp[:NTOK].T.reshape(NFO, 128, NTOK)).astype(np.float32),
            "wq_t": wq_t, "wo_t": wo_t, "wkv_t": wkv_t,
            "c1t": c1t, "c2t": c2t, "w1w2": w1w2,
            "biasm": _host_biasmat(c), "mask4": mask4,
            "u1": u1, "ident": ident, "onesb": ones, "iota": iota,
        })

    res = run_bass_kernel_spmd(nc, in_maps, core_ids=list(range(8)))

    out = np.empty((B, S, D), dtype=np.float32)
    for c in range(8):
        perm, _, _ = _core_geom(c)
        b = c // 2
        yc = np.asarray(res.results[c]["y"])
        toks = yc.reshape(NTT, 128, NFO, 128).transpose(0, 3, 2, 1) \
                 .reshape(NTOK, D)
        out[b][perm[:NTOK]] = toks
    if _debug:
        return out, res
    return out


# version 11
# speedup vs baseline: 1.7553x; 1.0116x over previous
import math
import numpy as np
import ml_dtypes

import concourse.bass as bass
import concourse.mybir as mybir
from concourse.bass import IndirectOffsetOnAxis
from concourse.bass_utils import run_bass_kernel_spmd

F32 = mybir.dt.float32
BF16 = mybir.dt.bfloat16
U8 = mybir.dt.uint8
U32 = mybir.dt.uint32
ALU = mybir.AluOpType
AF = mybir.ActivationFunctionType

IDX_ZERO = False
DBG_H = False
GLIMIT = 10**9
PLAIN_GATHER = False

B, S, D, NH, HD = 4, 2048, 4096, 32, 128
INTER, SPR, NB, RANK = 14336, 256, 56, 16
SCALE = HD ** -0.5
NTOK = 1024
NKT = 16
NFO = 32
NTT = 8
LN_EPS = 1e-5
MASK_NEG = -50000.0
KILL = -1e9

_start = 2.0 ** (-(2.0 ** -(math.log2(NH) - 3)))
SLOPES = [float(np.float32(_start ** (i + 1))) for i in range(NH)]

SLOTSETS = {0: [0, 1, 2, 3, 8, 9, 10, 11], 1: list(range(16))}
MASKED = {0: {0: 0, 1: 1, 2: 2, 3: 3}, 1: {4: 0, 5: 1, 6: 2, 7: 3}}


def n_groups(h):
    sh = SLOPES[h]
    if sh * 256.0 <= 66.0:
        return 1
    if sh * 128.0 <= 66.0:
        return 2
    return 4


def _core_geom(c):
    half = c % 2
    if half == 0:
        perm = np.concatenate([np.arange(0, 512), np.arange(1536, 2048),
                               np.arange(512, 1536)])
        q0s = (0, 1536)
    else:
        perm = np.concatenate([np.arange(512, 1536), np.arange(0, 512),
                               np.arange(1536, 2048)])
        q0s = (512, 1024)
    tile_orig = [int(perm[s * 128]) for s in range(NKT)]
    return perm, q0s, tile_orig


_BIAS_COLS = []
COLOF = {}
for _h in range(NH):
    for _qc in (0, 1):
        for _si in range(len(SLOTSETS[_qc])):
            for _g in range(n_groups(_h)):
                COLOF[(_h, _qc, _si, _g)] = len(_BIAS_COLS)
                _BIAS_COLS.append((_h, _qc, _si, _g))
NCOL = len(_BIAS_COLS)


def _host_biasmat(c):
    _, q0s, torig = _core_geom(c)
    bm = np.zeros((128, NCOL), dtype=np.float64)
    p = np.arange(128, dtype=np.float64)
    for col, (h, qc, si, g) in enumerate(_BIAS_COLS):
        s = SLOTSETS[qc][si]
        k0, q0 = torig[s], q0s[qc]
        if k0 > q0 + 511:
            bm[:, col] = KILL
        else:
            w = 512 // n_groups(h)
            qt = q0 + w * g + w // 2
            bm[:, col] = SLOPES[h] * (k0 + p - qt)
    return bm.astype(np.float32)


def _host_masks():
    m = np.zeros((4, 128, 512), dtype=np.float32)
    c = np.arange(128)[:, None]
    j = np.arange(512)[None, :]
    for d in range(4):
        m[d] = np.where(c > j - 128 * d, MASK_NEG, 0.0)
    return m.astype(ml_dtypes.bfloat16)


class Ctx:
    SEMS = ("LD", "LA", "LB", "LC", "SD", "GA", "PE", "VE", "SC")

    def __init__(self):
        self.emit = False
        self.cnt = {s: 0 for s in self.SEMS}
        self.ev = {}
        self.sems = {}

    def inc(self, inst, sem, name, n):
        if self.emit:
            assert inst is not None, (sem, name)
            inst.then_inc(self.sems[sem], n)
        else:
            self.cnt[sem] += n
            self.ev[(sem, name)] = self.cnt[sem]

    def wait(self, eng, sem, name):
        if self.emit:
            eng.wait_ge(self.sems[sem], self.ev[(sem, name)])

    def E(self, thunk):
        return thunk() if self.emit else None

    def mark(self, sem, name):
        if not self.emit:
            self.ev[(sem, name)] = self.cnt[sem]


def build_program(debug=False, stage=99, fsub=99):
    nc = bass.Bass("TRN2", target_bir_lowering=False)

    d_xfmt = nc.dram_tensor("xfmt", [NFO, 128, S], BF16, kind="ExternalInput").ap()
    d_xf32 = nc.dram_tensor("xf32", [NFO, 128, NTOK], F32, kind="ExternalInput").ap()
    d_wq = nc.dram_tensor("wq_t", [NFO, 128, D], BF16, kind="ExternalInput").ap()
    d_wo = nc.dram_tensor("wo_t", [NFO, 128, D], BF16, kind="ExternalInput").ap()
    d_wkv = nc.dram_tensor("wkv_t", [128, 8192], BF16, kind="ExternalInput").ap()
    d_c1 = nc.dram_tensor("c1t", [128, 512], BF16, kind="ExternalInput").ap()
    d_c2 = nc.dram_tensor("c2t", [16, INTER], BF16, kind="ExternalInput").ap()
    d_w12 = nc.dram_tensor("w1w2", [INTER, 8192], BF16, kind="ExternalInput").ap()
    d_bias = nc.dram_tensor("biasm", [128, NCOL], F32, kind="ExternalInput").ap()
    d_mask = nc.dram_tensor("mask4", [4, 128, 512], BF16, kind="ExternalInput").ap()
    d_u1 = nc.dram_tensor("u1", [128, 128], BF16, kind="ExternalInput").ap()
    d_id = nc.dram_tensor("ident", [128, 128], BF16, kind="ExternalInput").ap()
    d_ones = nc.dram_tensor("onesb", [128, 128], BF16, kind="ExternalInput").ap()
    d_iota = nc.dram_tensor("iota", [128, 256], BF16, kind="ExternalInput").ap()
    d_x1s = nc.dram_tensor("x1s", [NFO, 128, NTOK], F32, kind="Internal").ap()
    d_y = nc.dram_tensor("y", [NTT, 128, D], F32, kind="ExternalOutput").ap()
    dbg = {}
    if debug:
        for nm, shp, dt in [("dbg_y1", [128, NFO * NTOK], BF16),
                            ("dbg_q", [128, NH * NTOK], BF16),
                            ("dbg_kf", [128, NKT * 128], BF16),
                            ("dbg_vt", [128, NKT * 128], BF16),
                            ("dbg_ao", [128, NH * NTOK], BF16),
                            ("dbg_x1", [128, NFO * NTOK], BF16),
                            ("dbg_u", [16, NTOK], BF16),
                            ("dbg_y2", [128, NTT * D], BF16),
                            ("dbg_h", [NTT, 128, 56], F32)]:
            dbg[nm] = nc.dram_tensor(nm, shp, dt, kind="ExternalOutput").ap()

    ctx = Ctx()

    from contextlib import ExitStack
    with ExitStack() as _st:
        def _sb(name, shape, dt):
            return _st.enter_context(nc.sbuf_tensor(name, shape, dt))

        def _ps(name, shape, dt):
            return _st.enter_context(nc.psum_tensor(name, shape, dt))

        arA = _sb("arA", [128, 65536], U8)
        arB = _sb("arB", [128, 32768], BF16)
        arC = _sb("arC", [128, 49152], U8)
        sb_bias = _sb("sb_bias", [128, NCOL], F32)
        sb_u1 = _sb("sb_u1", [128, 128], BF16)
        sb_id = _sb("sb_id", [128, 128], BF16)
        sb_ones = _sb("sb_ones", [128, 128], BF16)
        sb_kf = _sb("sb_kf", [128, NKT * 128], BF16)
        sb_vt = _sb("sb_vt", [128, NKT * 128], BF16)
        sb_c1 = _sb("sb_c1", [128, 512], BF16)
        sb_pt0 = _sb("sb_pt0", [128, 512], BF16)
        sb_pt1 = _sb("sb_pt1", [128, 512], BF16)
        sb_mbc = _sb("sb_mbc", [128, 2048], BF16)
        sb_sq = _sb("sb_sq", [128, 1024], BF16)
        sb_rrow = _sb("sb_rrow", [1, 512], F32)
        sb_rbf = _sb("sb_rbf", [1, 2048], BF16)
        sb_u = _sb("sb_u", [16, 1024], BF16)
        sb_c2b0 = _sb("sb_c2b0", [16, 256], BF16)
        sb_c2b1 = _sb("sb_c2b1", [16, 256], BF16)
        sb_m8 = _sb("sb_m8", [128, 8], F32)
        sb_i8 = _sb("sb_i8", [128, 16], U32)
        sb_h = _sb("sb_h", [128, 64], F32)
        sb_hr = _sb("sb_hr", [128, 64], F32)
        pb0 = _ps("pb0", [128, 512], F32)
        pb1 = _ps("pb1", [128, 512], F32)
        pb2 = _ps("pb2", [128, 512], F32)
        pb3 = _ps("pb3", [128, 512], F32)
        pb4 = _ps("pb4", [128, 512], F32)
        pb5 = _ps("pb5", [128, 512], F32)
        pt0 = _ps("pt0", [128, 128], BF16)
        pt1 = _ps("pt1", [128, 128], BF16)
        sLD = _st.enter_context(nc.semaphore("sLD"))
        sLA = _st.enter_context(nc.semaphore("sLA"))
        sLB = _st.enter_context(nc.semaphore("sLB"))
        sLC = _st.enter_context(nc.semaphore("sLC"))
        sSD = _st.enter_context(nc.semaphore("sSD"))
        sGA = _st.enter_context(nc.semaphore("sGA"))
        sPE = _st.enter_context(nc.semaphore("sPE"))
        sVE = _st.enter_context(nc.semaphore("sVE"))
        sSC = _st.enter_context(nc.semaphore("sSC"))
        block = _st.enter_context(nc.Block())
        ctx.sems = {"LD": sLD, "LA": sLA, "LB": sLB, "LC": sLC,
                    "SD": sSD, "GA": sGA,
                    "PE": sPE, "VE": sVE, "SC": sSC}
        pb = [pb0, pb1, pb2, pb3, pb4, pb5]
        pt = [pt0, pt1]

        vA = arA.ap().bitcast(BF16)
        y1fm = vA
        attnout = vA
        y2tm = vA
        vB = arB.ap()
        qT = vB
        x1bf = vB
        wbuf = [vB[:, 0:8192], vB[:, 8192:16384]]
        x1c = arB.ap().bitcast(F32)[:, 8192:12288]
        otile = arB.ap().bitcast(F32)[:, 12288:16384]
        vC16 = arC.ap().bitcast(BF16)
        xchunk = vC16[:, 0:16384]
        wkv_sb = vC16[:, 16384:24576]
        wqbuf = [vC16[:, 0:4096], vC16[:, 4096:8192]]
        wobuf = wqbuf
        acc_bf = [vC16[:, 0:4096], vC16[:, 4096:8192]]
        w1buf = [vB[:, 0:4096], vB[:, 4096:8192]]
        w2buf = [vB[:, 8192:12288], vB[:, 12288:16384]]
        w2sc = vC16[:, 16384:20480]
        sc_scr = vC16[:, 12288:16384]
        prod = [vC16[:, 20480:24576], vC16[:, 8192:12288]]
        sb_ykv = vB[:, 0:1024]
        sb_nrm_kv = vB[:, 1024:2048]
        sb_vfm = vB[:, 2048:2560]
        vBf32 = arB.ap().bitcast(F32)
        rows_kv = vBf32[0:1, 1280:3328]
        vCf32 = arC.ap().bitcast(F32)
        sb_x1o = vCf32[:, 4096:5120]
        sb_x1f = vCf32[:, 5120:6144]
        rows_ln2 = vCf32[0:1, 6144:8192]
        y2r = [vC16[:, 8192:9216], vC16[:, 9216:10240]]
        sb_nrm_t = vC16[:, 10240:11264]
        sb_mask = vC16[:, 14336:16384]
        sb_rbc = vC16[:, 6144:6656]

        P_tiles = [sb_pt0, sb_pt1]
        ie_dst = [sb_pt0.ap()[:, 0:256], sb_pt1.ap()[:, 0:256],
                  sb_sq.ap()[:, 512:768]]
        P3 = [sb_pt0.ap(), sb_pt1.ap(), sb_sq.ap()[:, 0:512]]
        c2b = [sb_c2b0.ap(), sb_c2b1.ap(),
               sb_sq.ap()[0:16, 768:1024]]

        def q_off(h, qc):
            return h * 1024 + qc * 512

        def stream_sync(eng):
            E, inc, wait = ctx.E, ctx.inc, ctx.wait
            for nm, dst, src in [("c_bias", sb_bias.ap(), d_bias),
                                 ("c_u1", sb_u1.ap(), d_u1),
                                 ("c_id", sb_id.ap(), d_id),
                                 ("c_ones", sb_ones.ap(), d_ones),
                                 ("c_c1", sb_c1.ap(), d_c1),
                                 ("c_wkv", wkv_sb, d_wkv)]:
                inc(E(lambda dst=dst, src=src: eng.dma_start(out=dst, in_=src)),
                    "LC", nm, 16)
            ctx.mark("LC", "b_pre")
            wait(eng, "LC", "b_pre")
            for tc in range(4):
                for kt in range(NFO):
                    if tc >= 1:
                        wait(eng, "VE", f"y_{tc-1}_{kt}")
                    inc(E(lambda tc=tc, kt=kt: eng.dma_start(
                        out=xchunk[:, kt * 512:(kt + 1) * 512],
                        in_=d_xfmt[kt][:, tc * 512:(tc + 1) * 512])),
                        "LC", f"xc_{tc}_{kt}", 16)
                ctx.mark("LC", f"b_xc{tc}")
                wait(eng, "LC", f"b_xc{tc}")
            if stage < 1:
                return
            wait(eng, "VE", "y_3_31")
            for dd in range(4):
                inc(E(lambda dd=dd: eng.dma_start(
                    out=sb_mask[:, dd * 512:(dd + 1) * 512],
                    in_=d_mask[dd])), "LC", f"c_mask{dd}", 16)
            ctx.mark("LC", "b_mask")
            wait(eng, "LC", "b_mask")
            for m in range(NFO):
                if m >= 2:
                    wait(eng, "PE", f"q_{m-2}")
                inc(E(lambda m=m: eng.dma_start(out=wqbuf[m % 2],
                                                in_=d_wq[m])),
                    "LA" if m % 2 == 0 else "LB", f"wq_{m}", 16)
            if stage < 3:
                return
            for fo in range(NFO):
                if fo == 0:
                    wait(eng, "VE", "ao_31_1")
                if fo >= 2:
                    wait(eng, "PE", f"wo_{fo-2}")
                inc(E(lambda fo=fo: eng.dma_start(out=wobuf[fo % 2],
                                                  in_=d_wo[fo])),
                    "LA" if fo % 2 == 0 else "LB", f"wwo_{fo}", 16)
                if fo >= 1:
                    wait(eng, "VE", f"x1_{fo-1}")
                inc(E(lambda fo=fo: eng.dma_start(out=sb_x1f,
                                                  in_=d_xf32[fo])),
                    "LA" if fo % 2 == 0 else "LB", f"xf_{fo}", 16)
            if stage < 5:
                return
            wait(eng, "PE", f"av_31_1_{len(SLOTSETS[1])-1}")
            inc(E(lambda: eng.dma_start(out=sb_pt0.ap()[:, 256:512],
                                        in_=d_iota)), "LC", "iota", 16)
            ctx.mark("LC", "b_iota")
            wait(eng, "LC", "b_iota")
            def sy_c2(g2):
                t2, j2 = g2 // NB, g2 % NB
                if g2 >= 3:
                    p2, q2 = (g2 - 3) // NB, (g2 - 3) % NB
                    wait(eng, "PE", f"int_{p2}_{q2}")
                elif g2 == 2:
                    wait(eng, "PE", f"st2_{NFO-1}")
                inc(E(lambda t2=t2, j2=j2, g2=g2: eng.dma_start(
                    out=c2b[g2 % 3],
                    in_=d_c2[:, j2 * 256:(j2 + 1) * 256])),
                    ("LA", "LB", "LD")[g2 % 3], f"c2_{t2}_{j2}", 16)

            def sy_x1c(tt):
                if tt == 0:
                    wait(eng, "VE", "ue")
                    wait(eng, "SD", f"x1s_{NFO-1}")
                else:
                    wait(eng, "VE", f"fa_{tt-1}_31")
                for fo in range(NFO):
                    inc(E(lambda tt=tt, fo=fo: eng.dma_start(
                        out=x1c[:, fo * 128:(fo + 1) * 128],
                        in_=d_x1s[fo][:, tt * 128:(tt + 1) * 128])),
                        "LC", f"x1c_{tt}_{fo}", 16)
                ctx.mark("LC", f"b_x1c{tt}")
                wait(eng, "LC", f"b_x1c{tt}")

            for g in range(NTT * NB):
                sy_c2(g)
                if g % NB == 3:
                    sy_x1c(g // NB)
            if NB <= 3:
                raise AssertionError

        def stream_gpsimd(eng):
            E, inc, wait = ctx.E, ctx.inc, ctx.wait
            if debug:
                wait(eng, "VE", "y_1_31")
                inc(E(lambda: eng.dma_start(out=dbg["dbg_y1"], in_=y1fm)),
                    "GA", "dbg_y1", 16)
                wait(eng, "VE", "kve_3")
                wait(eng, "SC", "vt_3_3")
                inc(E(lambda: eng.dma_start(out=dbg["dbg_kf"], in_=sb_kf.ap())),
                    "GA", "dbg_kf", 16)
                inc(E(lambda: eng.dma_start(out=dbg["dbg_vt"], in_=sb_vt.ap())),
                    "GA", "dbg_vt", 16)
                wait(eng, "VE", "qe_31")
                inc(E(lambda: eng.dma_start(out=dbg["dbg_q"], in_=qT)),
                    "GA", "dbg_q", 16)
                wait(eng, "VE", "ao_31_1")
                inc(E(lambda: eng.dma_start(out=dbg["dbg_ao"], in_=attnout)),
                    "GA", "dbg_ao", 16)
                wait(eng, "VE", "x1_31")
                inc(E(lambda: eng.dma_start(out=dbg["dbg_x1"], in_=x1bf)),
                    "GA", "dbg_x1", 16)
                wait(eng, "VE", "ue")
                inc(E(lambda: eng.dma_start(out=dbg["dbg_u"], in_=sb_u.ap())),
                    "GA", "dbg_u", 16)
                wait(eng, "SC", "tre_31_7")
                inc(E(lambda: eng.dma_start(out=dbg["dbg_y2"], in_=y2tm)),
                    "GA", "dbg_y2", 16)
            if stage < 5:
                return
            NG_ALL = NTT * NB
            for g in range(NG_ALL):
                tt, j = g // NB, g % NB
                wait(eng, "VE", f"am_{tt}_{j}")
                if g >= 2:
                    pt_, pj = (g - 2) // NB, (g - 2) % NB
                    wait(eng, "VE", f"pr_{pt_}_{pj}")
                inc(E(lambda tt=tt, j=j, g=g: eng.indirect_dma_start(
                    w1buf[j % 2],
                    None,
                    d_w12,
                    IndirectOffsetOnAxis(
                        ap=sb_i8.ap()[:, g % 4:g % 4 + 1], axis=0),
                    element_offset=j * 256 * 8192)),
                    "GA", f"g_{tt}_{j}", 16)
                if g >= 2:
                    wait(eng, "VE", f"ws_{pt_}_{pj}")
                inc(E(lambda tt=tt, j=j, g=g: eng.indirect_dma_start(
                    w2buf[j % 2],
                    None,
                    d_w12,
                    IndirectOffsetOnAxis(
                        ap=sb_i8.ap()[:, g % 4:g % 4 + 1], axis=0),
                    element_offset=j * 256 * 8192 + 4096)),
                    "GA", f"g2_{tt}_{j}", 16)

        def stream_tensor(eng):
            E, inc, wait = ctx.E, ctx.inc, ctx.wait
            wait(eng, "LC", "b_pre")
            for tc in range(4):
                wait(eng, "LC", f"b_xc{tc}")
                for kt in range(NFO):
                    E(lambda tc=tc, kt=kt: eng.matmul(
                        pb[0][0:1, :], sb_ones.ap()[:, 0:1],
                        xchunk[:, kt * 512:(kt + 1) * 512],
                        start=(kt == 0), stop=(kt == NFO - 1)))
                    wait(eng, "SC", f"sq_{tc}_{kt}")
                    inc(E(lambda kt=kt: eng.matmul(
                        pb[1][0:1, :], sb_ones.ap()[:, 0:1],
                        sb_sq.ap()[:, (kt % 2) * 512:(kt % 2) * 512 + 512],
                        start=(kt == 0), stop=(kt == NFO - 1))),
                        "PE", f"sx_{tc}_{kt}", 1)
                wait(eng, "VE", f"rows_{tc}")
                E(lambda: eng.matmul(pb[2][:, :], sb_ones.ap()[0:1, 0:128],
                                     sb_rbf.ap()[0:1, 0:512],
                                     start=True, stop=True))
                inc(E(lambda: eng.matmul(pb[3][:, :], sb_ones.ap()[0:1, 0:128],
                                         sb_rbf.ap()[0:1, 512:1024],
                                         start=True, stop=True)),
                    "PE", f"bc_{tc}", 1)
                for kt in range(NFO):
                    wait(eng, "VE", f"y_{tc}_{kt}")
                    if tc < 2:
                        dst = y1fm[:, kt * 1024 + tc * 512:
                                   kt * 1024 + tc * 512 + 512]
                    else:
                        dst = sb_ykv[:, (kt % 2) * 512:(kt % 2) * 512 + 512]
                    E(lambda kt=kt, dst=dst: eng.matmul(
                        pb[4][:, :], wkv_sb[:, kt * 256:kt * 256 + 128], dst,
                        start=(kt == 0), stop=(kt == NFO - 1)))
                    inc(E(lambda kt=kt, dst=dst: eng.matmul(
                        pb[5][:, :], wkv_sb[:, kt * 256 + 128:kt * 256 + 256],
                        dst, start=(kt == 0), stop=(kt == NFO - 1))),
                        "PE", f"kv_{tc}_{kt}", 1)
                wait(eng, "VE", f"kve_{tc}")
                for i in range(4):
                    if i >= 2:
                        wait(eng, "SC", f"vt_{tc}_{i-2}")
                    elif tc >= 1:
                        wait(eng, "SC", f"vt_{tc-1}_{2+i}")
                    inc(E(lambda i=i: eng.transpose(
                        pt[i % 2][:, :], sb_vfm[:, i * 128:(i + 1) * 128],
                        sb_id.ap())), "PE", f"vt_{tc}_{i}", 1)
            if stage < 1:
                return
            for m in range(NFO):
                wait(eng, "LA" if m % 2 == 0 else "LB", f"wq_{m}")
                if m >= 2:
                    wait(eng, "VE", f"qe_{m-2}")
                for kt in range(NFO):
                    E(lambda m=m, kt=kt: eng.matmul(
                        pb[(m % 2) * 2][:, :],
                        wqbuf[m % 2][:, kt * 128:(kt + 1) * 128],
                        y1fm[:, kt * 1024:kt * 1024 + 512],
                        start=(kt == 0), stop=(kt == NFO - 1)))
                    i2 = E(lambda m=m, kt=kt: eng.matmul(
                        pb[(m % 2) * 2 + 1][:, :],
                        wqbuf[m % 2][:, kt * 128:(kt + 1) * 128],
                        y1fm[:, kt * 1024 + 512:kt * 1024 + 1024],
                        start=(kt == 0), stop=(kt == NFO - 1)))
                    if kt == NFO - 1:
                        inc(i2, "PE", f"q_{m}", 1)
            if stage < 2:
                return
            wait(eng, "LC", "b_mask")
            PSA3 = [pb[0], pb[1], pb[5]]
            last_exp = {}
            for qc in (0, 1):
                slots = SLOTSETS[qc]
                nsl = len(slots)
                for h in range(NH):
                    wait(eng, "VE", f"qe_{NFO-1}" if (qc == 0 and h == 0)
                         else f"qe_{h}")
                    if not (qc == 0 and h == 0):
                        ph, pqc = (h - 1, qc) if h > 0 else (NH - 1, 0)
                        wait(eng, "VE", f"ao_{ph}_{pqc}")
                    for si in range(nsl + 2):
                        if si < nsl:
                            s = slots[si]
                            prv = last_exp.get(si % 3)
                            if prv is not None:
                                wait(eng, "SC", f"exp_{prv}")
                            last_exp[si % 3] = f"{h}_{qc}_{si}"
                            mk = MASKED[qc].get(s)
                            psA = PSA3[si % 3]
                            i1 = E(lambda s=s, h=h, qc=qc, psA=psA, mk=mk:
                                   eng.matmul(
                                       psA[:, :],
                                       sb_kf.ap()[:, s * 128:(s + 1) * 128],
                                       qT[:, q_off(h, qc):q_off(h, qc) + 512],
                                       start=True, stop=(mk is None)))
                            if mk is not None:
                                i1 = E(lambda mk=mk, psA=psA: eng.matmul(
                                    psA[:, :], sb_u1.ap(),
                                    sb_mask[:, mk * 512:(mk + 1) * 512],
                                    start=False, stop=True))
                            inc(i1, "PE", f"sc_{h}_{qc}_{si}", 1)
                        if si >= 2:
                            pi = si - 2
                            s_prev = slots[pi]
                            wait(eng, "SC", f"exp_{h}_{qc}_{pi}")
                            E(lambda s_prev=s_prev, pi=pi, nsl=nsl: eng.matmul(
                                pb[2][:, :],
                                sb_vt.ap()[:, s_prev * 128:(s_prev + 1) * 128],
                                P3[pi % 3][:, :],
                                start=(pi == 0), stop=(pi == nsl - 1)))
                            inc(E(lambda pi=pi, nsl=nsl: eng.matmul(
                                pb[3][0:1, :], sb_ones.ap()[:, 0:1],
                                P3[pi % 3][:, :],
                                start=(pi == 0), stop=(pi == nsl - 1))),
                                "PE", f"av_{h}_{qc}_{pi}", 1)
                    wait(eng, "VE", f"r_{h}_{qc}")
                    if not (qc == 0 and h == 0):
                        ph, pqc = (h - 1, qc) if h > 0 else (NH - 1, 0)
                        wait(eng, "SC", f"rbe_{ph}_{pqc}")
                    inc(E(lambda: eng.matmul(pb[4][:, :],
                                             sb_ones.ap()[0:1, 0:128],
                                             sb_rbf.ap()[0:1, 0:512],
                                             start=True, stop=True)),
                        "PE", f"rb_{h}_{qc}", 1)
            if stage < 3:
                return
            for fo in range(NFO):
                wait(eng, "LA" if fo % 2 == 0 else "LB", f"xf_{fo}")
                if fo == 0:
                    wait(eng, "SC", f"exp_31_1_{len(SLOTSETS[1])-1}")
                    wait(eng, "SC", "rbe_31_1")
                if fo == 1:
                    wait(eng, "VE", "ao_31_1")
                if fo >= 2:
                    wait(eng, "VE", f"x1_{fo-2}")
                for kt in range(NFO):
                    E(lambda fo=fo, kt=kt: eng.matmul(
                        pb[(fo % 2) * 2][:, :],
                        wobuf[fo % 2][:, kt * 128:(kt + 1) * 128],
                        attnout[:, kt * 1024:kt * 1024 + 512],
                        start=(kt == 0), stop=(kt == NFO - 1)))
                    i2 = E(lambda fo=fo, kt=kt: eng.matmul(
                        pb[(fo % 2) * 2 + 1][:, :],
                        wobuf[fo % 2][:, kt * 128:(kt + 1) * 128],
                        attnout[:, kt * 1024 + 512:kt * 1024 + 1024],
                        start=(kt == 0), stop=(kt == NFO - 1)))
                    if kt == NFO - 1:
                        inc(i2, "PE", f"wo_{fo}", 1)
                wait(eng, "VE", f"x1_{fo}")
                wait(eng, "SC", f"sq2_{fo}")
                E(lambda fo=fo: eng.matmul(
                    pb[4][0:1, :], sb_ones.ap()[:, 0:1],
                    x1bf[:, fo * 1024:fo * 1024 + 512],
                    start=(fo == 0), stop=False, skip_group_check=True))
                E(lambda fo=fo: eng.matmul(
                    pb[4][32:33, :], sb_ones.ap()[:, 0:1],
                    x1bf[:, fo * 1024 + 512:fo * 1024 + 1024],
                    start=False, stop=False, skip_group_check=True))
                E(lambda fo=fo: eng.matmul(
                    pb[4][64:65, :], sb_ones.ap()[:, 0:1],
                    sb_sq.ap()[:, 0:512],
                    start=False, stop=(fo == NFO - 1), skip_group_check=True))
                inc(E(lambda fo=fo: eng.matmul(
                    pb[5][0:1, :], sb_ones.ap()[:, 0:1],
                    sb_sq.ap()[:, 512:1024],
                    start=(fo == 0), stop=(fo == NFO - 1))),
                    "PE", f"st2_{fo}", 1)
            wait(eng, "VE", "rows2")
            E(lambda: eng.matmul(pb[0][:, :], sb_ones.ap()[0:1, 0:128],
                                 sb_rbf.ap()[0:1, 0:512], start=True, stop=True))
            E(lambda: eng.matmul(pb[1][:, :], sb_ones.ap()[0:1, 0:128],
                                 sb_rbf.ap()[0:1, 512:1024],
                                 start=True, stop=True))
            E(lambda: eng.matmul(pb[2][:, :], sb_ones.ap()[0:1, 0:128],
                                 sb_rbf.ap()[0:1, 1024:1536],
                                 start=True, stop=True))
            inc(E(lambda: eng.matmul(pb[3][:, :], sb_ones.ap()[0:1, 0:128],
                                     sb_rbf.ap()[0:1, 1536:2048],
                                     start=True, stop=True)),
                "PE", "bc2", 1)
            if stage < 4:
                return
            for fo in range(NFO):
                wait(eng, "VE", f"y2_{fo}")
                E(lambda fo=fo: eng.matmul(
                    pb[4][0:16, :], sb_c1.ap()[:, fo * 16:(fo + 1) * 16],
                    y2r[fo % 2][:, 0:512],
                    start=(fo == 0), stop=(fo == NFO - 1)))
                i2 = E(lambda fo=fo: eng.matmul(
                    pb[5][0:16, :], sb_c1.ap()[:, fo * 16:(fo + 1) * 16],
                    y2r[fo % 2][:, 512:1024],
                    start=(fo == 0), stop=(fo == NFO - 1)))
                if fo == NFO - 1:
                    inc(i2, "PE", "u", 1)
                for i in range(8):
                    gidx = fo * 8 + i
                    if gidx >= 2:
                        pp = gidx - 2
                        wait(eng, "SC", f"tre_{pp // 8}_{pp % 8}")
                    inc(E(lambda fo=fo, i=i: eng.transpose(
                        pt[i % 2][:, :],
                        y2r[fo % 2][:, i * 128:(i + 1) * 128],
                        sb_id.ap())), "PE", f"tr_{fo}_{i}", 1)
            if stage < 5:
                return
            def pe_int(g2):
                t2, j2 = g2 // NB, g2 % NB
                wait(eng, ("LA", "LB", "LD")[g2 % 3], f"c2_{t2}_{j2}")
                if g2 == 0:
                    wait(eng, "VE", "ue")
                if g2 >= 3:
                    p2, q2 = (g2 - 3) // NB, (g2 - 3) % NB
                    wait(eng, "SC", f"ie_{p2}_{q2}")
                inc(E(lambda t2=t2, j2=j2, g2=g2: eng.matmul(
                    pb[2 + g2 % 3][:, 0:256],
                    sb_u.ap()[0:16, t2 * 128:(t2 + 1) * 128],
                    c2b[g2 % 3][:, :],
                    start=True, stop=True)),
                    "PE", f"int_{t2}_{j2}", 1)

            def pe_ft(tt):
                wait(eng, "VE", f"o_{tt}_{NB-1}")
                for fo in range(NFO):
                    gidx = tt * NFO + fo
                    if gidx >= 2:
                        pp = gidx - 2
                        wait(eng, "VE", f"fa_{pp // NFO}_{pp % NFO}")
                    inc(E(lambda fo=fo: eng.transpose(
                        pt[fo % 2][:, :], acc_bf[0][:, fo * 128:(fo + 1) * 128],
                        sb_id.ap())), "PE", f"ft_{tt}_{fo}", 1)

            for g in range(NTT * NB):
                pe_int(g)
                if g % NB == 3 and g >= NB:
                    pe_ft(g // NB - 1)
            pe_ft(NTT - 1)

        def stream_scalar(eng):
            E, inc, wait = ctx.E, ctx.inc, ctx.wait
            pass
            for tc in range(4):
                wait(eng, "LC", f"b_xc{tc}")
                for kt in range(NFO):
                    if kt >= 2:
                        wait(eng, "PE", f"sx_{tc}_{kt-2}")
                    elif tc >= 1:
                        wait(eng, "PE", f"sx_{tc-1}_{NFO-2+kt}")
                    inc(E(lambda kt=kt, tc=tc: eng.activation(
                        sb_sq.ap()[:, (kt % 2) * 512:(kt % 2) * 512 + 512],
                        xchunk[:, kt * 512:(kt + 1) * 512], AF.Square)),
                        "SC", f"sq_{tc}_{kt}", 1)
                wait(eng, "VE", f"var_{tc}")
                inc(E(lambda: eng.activation(
                    rows_kv[0:1, 1024:1536], rows_kv[0:1, 1536:2048],
                    AF.Sqrt)), "SC", f"srt_{tc}", 1)
                for i in range(4):
                    wait(eng, "PE", f"vt_{tc}_{i}")
                    inc(E(lambda tc=tc, i=i: eng.copy(
                        sb_vt.ap()[:, (tc * 4 + i) * 128:(tc * 4 + i + 1) * 128],
                        pt[i % 2][:, :])), "SC", f"vt_{tc}_{i}", 1)
            if stage < 2:
                if ctx.emit:
                    eng.wait_ge(ctx.sems["SD"], 0)
                return
            wait(eng, "LC", "b_pre")
            PSA3 = [pb[0], pb[1], pb[5]]
            last_pt = {}
            for qc in (0, 1):
                slots = SLOTSETS[qc]
                for h in range(NH):
                    ng = n_groups(h)
                    w = 512 // ng
                    for si in range(len(slots)):
                        wait(eng, "PE", f"sc_{h}_{qc}_{si}")
                        prv = last_pt.get(si % 3)
                        if prv is not None:
                            wait(eng, "PE", f"av_{prv}")
                        last_pt[si % 3] = f"{h}_{qc}_{si}"
                        psA = PSA3[si % 3]
                        for g in range(ng):
                            col = COLOF[(h, qc, si, g)]
                            i1 = E(lambda si=si, g=g, w=w, col=col, psA=psA:
                                   eng.activation(
                                       P3[si % 3][:, g * w:(g + 1) * w],
                                       psA[:, g * w:(g + 1) * w],
                                       AF.Exp,
                                       bias=sb_bias.ap()[:, col:col + 1],
                                       scale=SCALE))
                            if g == ng - 1:
                                inc(i1, "SC", f"exp_{h}_{qc}_{si}", 1)
                    wait(eng, "PE", f"rb_{h}_{qc}")
                    inc(E(lambda: eng.copy(sb_rbc[:, :], pb[4][:, :])),
                        "SC", f"rbe_{h}_{qc}", 1)
            if stage < 3:
                return
            for fo in range(NFO):
                wait(eng, "VE", f"x1_{fo}")
                inc(E(lambda fo=fo: eng.dma_start(out=d_x1s[fo],
                                                  in_=sb_x1o)),
                    "SD", f"x1s_{fo}", 16)
                if fo >= 1:
                    wait(eng, "PE", f"st2_{fo-1}")
                inc(E(lambda fo=fo: eng.activation(
                    sb_sq.ap()[:, :], x1bf[:, fo * 1024:(fo + 1) * 1024],
                    AF.Square)), "SC", f"sq2_{fo}", 1)
            wait(eng, "VE", "var2")
            E(lambda: eng.activation(
                sb_x1o[0:1, 0:512], rows_ln2[0:1, 1024:1536],
                AF.Sqrt))
            inc(E(lambda: eng.activation(
                sb_x1o[0:1, 512:1024], rows_ln2[0:1, 1536:2048],
                AF.Sqrt)), "SC", "srt2", 1)
            wait(eng, "PE", "bc2")
            E(lambda: eng.copy(sb_mbc.ap()[:, 0:512], pb[0][:, :]))
            E(lambda: eng.copy(sb_mbc.ap()[:, 512:1024], pb[1][:, :]))
            E(lambda: eng.copy(sb_mbc.ap()[:, 1024:1536], pb[2][:, :]))
            inc(E(lambda: eng.copy(sb_mbc.ap()[:, 1536:2048], pb[3][:, :])),
                "SC", "bc2e", 1)
            if stage < 4:
                return
            for fo in range(NFO):
                for i in range(8):
                    wait(eng, "PE", f"tr_{fo}_{i}")
                    inc(E(lambda fo=fo, i=i: eng.copy(
                        y2tm[:, i * 4096 + fo * 128: i * 4096 + (fo + 1) * 128],
                        pt[i % 2][:, :])), "SC", f"tre_{fo}_{i}", 1)
            if stage >= 5:
                NG_ALL = NTT * NB

                def sc_ie(g2):
                    t2, j2 = g2 // NB, g2 % NB
                    wait(eng, "PE", f"int_{t2}_{j2}")
                    if g2 >= 3:
                        p2, q2 = (g2 - 3) // NB, (g2 - 3) % NB
                        wait(eng, "VE", f"am_{p2}_{q2}")
                    inc(E(lambda g2=g2: eng.copy(
                        ie_dst[g2 % 3][:, 0:256],
                        pb[2 + g2 % 3][:, 0:256])),
                        "SC", f"ie_{t2}_{j2}", 1)

                sc_ie(0)
                sc_ie(1)
                sc_ie(2)
                for g in range(NG_ALL):
                    tt, j = g // NB, g % NB
                    if g + 3 < NG_ALL:
                        sc_ie(g + 3)
                    wait(eng, "VE", f"pr_{tt}_{j}")
                    inc(E(lambda j=j, g=g: eng.activation(
                        sc_scr[:, :], prod[g % 2][:, :], AF.Copy,
                        accum_out=sb_h.ap()[:, j:j + 1])),
                        "SC", f"ha_{tt}_{j}", 1)
                    wait(eng, "SC", f"ha_{tt}_{j}")
                    inc(E(lambda j=j: eng.activation(
                        sb_hr.ap()[:, j:j + 1], sb_h.ap()[:, j:j + 1],
                        AF.Relu)), "SC", f"hrl_{tt}_{j}", 1)
                    if j == 2 and tt >= 1:
                        wait(eng, "VE", f"fa_{tt-1}_31")
                        inc(E(lambda tt=tt: eng.dma_start(out=d_y[tt - 1],
                                                          in_=otile)),
                            "SD", f"y_{tt-1}", 16)
                wait(eng, "VE", f"fa_{NTT-1}_31")
                inc(E(lambda: eng.dma_start(out=d_y[NTT - 1], in_=otile)),
                    "SD", f"y_{NTT-1}", 16)
                if ctx.emit:
                    eng.wait_ge(ctx.sems["SD"], ctx.ev[("SD", f"y_{NTT-1}")])
            else:
                inc(E(lambda: eng.dma_start(out=d_y[0][:, 0:NCOL],
                                            in_=sb_bias.ap())),
                    "SD", "y_stub", 16)
                if ctx.emit:
                    eng.wait_ge(ctx.sems["SD"], ctx.ev[("SD", "y_stub")])

        def stream_vector(eng):
            E, inc, wait = ctx.E, ctx.inc, ctx.wait
            pass
            for tc in range(4):
                wait(eng, "PE", f"sx_{tc}_{NFO-1}")
                E(lambda: eng.tensor_scalar(rows_kv[0:1, 0:512], pb[0][0:1, :],
                                            1.0 / D, None, ALU.mult))
                E(lambda: eng.tensor_scalar(rows_kv[0:1, 512:1024], pb[1][0:1, :],
                                            1.0 / D, None, ALU.mult))
                E(lambda: eng.tensor_tensor(rows_kv[0:1, 1536:2048],
                                            rows_kv[0:1, 0:512], rows_kv[0:1, 0:512],
                                            ALU.mult))
                E(lambda: eng.tensor_tensor(rows_kv[0:1, 1024:1536],
                                            rows_kv[0:1, 512:1024],
                                            rows_kv[0:1, 1536:2048],
                                            ALU.subtract))
                inc(E(lambda: eng.tensor_scalar(rows_kv[0:1, 1536:2048],
                                                rows_kv[0:1, 1024:1536],
                                                LN_EPS, None, ALU.add)),
                    "VE", f"var_{tc}", 1)
                wait(eng, "SC", f"srt_{tc}")
                E(lambda: eng.tensor_copy(sb_rbf.ap()[0:1, 0:512],
                                          rows_kv[0:1, 0:512]))
                E(lambda: eng.reciprocal(rows_kv[0:1, 0:512],
                                         rows_kv[0:1, 1024:1536]))
                inc(E(lambda: eng.tensor_copy(sb_rbf.ap()[0:1, 512:1024],
                                              rows_kv[0:1, 0:512])),
                    "VE", f"rows_{tc}", 1)
                wait(eng, "PE", f"bc_{tc}")
                E(lambda: eng.tensor_copy(sb_mbc.ap()[:, 0:512], pb[2][:, :]))
                inc(E(lambda: eng.tensor_copy(sb_mbc.ap()[:, 512:1024],
                                              pb[3][:, :])),
                    "VE", f"mbc_{tc}", 1)
                for kt in range(NFO):
                    if tc < 2:
                        dst = y1fm[:, kt * 1024 + tc * 512:
                                   kt * 1024 + tc * 512 + 512]
                    else:
                        dst = sb_ykv[:, (kt % 2) * 512:(kt % 2) * 512 + 512]
                        if kt >= 2:
                            wait(eng, "PE", f"kv_{tc}_{kt-2}")
                    E(lambda kt=kt: eng.tensor_tensor(
                        sb_nrm_kv[:, (kt % 2) * 512:(kt % 2) * 512 + 512],
                        xchunk[:, kt * 512:(kt + 1) * 512],
                        sb_mbc.ap()[:, 0:512], ALU.subtract))
                    inc(E(lambda kt=kt, dst=dst: eng.tensor_tensor(
                        dst,
                        sb_nrm_kv[:, (kt % 2) * 512:(kt % 2) * 512 + 512],
                        sb_mbc.ap()[:, 512:1024], ALU.mult)),
                        "VE", f"y_{tc}_{kt}", 1)
                wait(eng, "PE", f"kv_{tc}_{NFO-1}")
                E(lambda tc=tc: eng.tensor_copy(
                    sb_kf.ap()[:, tc * 512:(tc + 1) * 512], pb[4][:, :]))
                inc(E(lambda: eng.tensor_copy(sb_vfm[:, :], pb[5][:, :])),
                    "VE", f"kve_{tc}", 1)
            if stage < 1:
                return
            for m in range(NFO):
                wait(eng, "PE", f"q_{m}")
                E(lambda m=m: eng.tensor_copy(
                    qT[:, m * 1024:m * 1024 + 512], pb[(m % 2) * 2][:, :]))
                inc(E(lambda m=m: eng.tensor_copy(
                    qT[:, m * 1024 + 512:m * 1024 + 1024],
                    pb[(m % 2) * 2 + 1][:, :])), "VE", f"qe_{m}", 1)
            if stage < 2:
                return
            for qc in (0, 1):
                nsl = len(SLOTSETS[qc])
                for h in range(NH):
                    wait(eng, "PE", f"av_{h}_{qc}_{nsl-1}")
                    E(lambda: eng.reciprocal(sb_rrow.ap()[0:1, :],
                                             pb[3][0:1, :]))
                    inc(E(lambda: eng.tensor_copy(sb_rbf.ap()[0:1, 0:512],
                                                  sb_rrow.ap()[0:1, :])),
                        "VE", f"r_{h}_{qc}", 1)
                    wait(eng, "SC", f"rbe_{h}_{qc}")
                    inc(E(lambda h=h, qc=qc: eng.tensor_tensor(
                        attnout[:, q_off(h, qc):q_off(h, qc) + 512],
                        pb[2][:, :], sb_rbc[:, :], ALU.mult)),
                        "VE", f"ao_{h}_{qc}", 1)
            if stage < 3:
                return
            for fo in range(NFO):
                wait(eng, "PE", f"wo_{fo}")
                wait(eng, "LA" if fo % 2 == 0 else "LB", f"xf_{fo}")
                if fo >= 1:
                    wait(eng, "SD", f"x1s_{fo-1}")
                E(lambda fo=fo: eng.tensor_tensor(
                    sb_x1o[:, 0:512], pb[(fo % 2) * 2][:, :],
                    sb_x1f[:, 0:512], ALU.add))
                E(lambda fo=fo: eng.tensor_tensor(
                    sb_x1o[:, 512:1024], pb[(fo % 2) * 2 + 1][:, :],
                    sb_x1f[:, 512:1024], ALU.add))
                inc(E(lambda fo=fo: eng.tensor_copy(
                    x1bf[:, fo * 1024:(fo + 1) * 1024], sb_x1o[:, :])),
                    "VE", f"x1_{fo}", 1)
            wait(eng, "PE", f"st2_{NFO-1}")
            wait(eng, "SD", f"x1s_{NFO-1}")
            x1fr = sb_x1f[0:1, :]
            x1or = sb_x1o[0:1, :]
            E(lambda: eng.tensor_scalar(rows_ln2[0:1, 0:512], pb[4][0:1, :],
                                        1.0 / D, None, ALU.mult))
            E(lambda: eng.tensor_scalar(rows_ln2[0:1, 512:1024], pb[4][32:33, :],
                                        1.0 / D, None, ALU.mult))
            E(lambda: eng.tensor_scalar(rows_ln2[0:1, 1024:1536], pb[4][64:65, :],
                                        1.0 / D, None, ALU.mult))
            E(lambda: eng.tensor_scalar(rows_ln2[0:1, 1536:2048], pb[5][0:1, :],
                                        1.0 / D, None, ALU.mult))
            E(lambda: eng.tensor_tensor(x1fr[:, 0:512],
                                        rows_ln2[0:1, 0:512], rows_ln2[0:1, 0:512],
                                        ALU.mult))
            E(lambda: eng.tensor_tensor(x1fr[:, 512:1024],
                                        rows_ln2[0:1, 512:1024],
                                        rows_ln2[0:1, 512:1024], ALU.mult))
            E(lambda: eng.tensor_tensor(x1or[:, 0:512],
                                        rows_ln2[0:1, 1024:1536],
                                        x1fr[:, 0:512], ALU.subtract))
            E(lambda: eng.tensor_tensor(x1or[:, 512:1024],
                                        rows_ln2[0:1, 1536:2048],
                                        x1fr[:, 512:1024],
                                        ALU.subtract))
            E(lambda: eng.tensor_scalar(rows_ln2[0:1, 1024:1536],
                                        x1or[:, 0:512],
                                        LN_EPS, None, ALU.add))
            inc(E(lambda: eng.tensor_scalar(rows_ln2[0:1, 1536:2048],
                                            x1or[:, 512:1024],
                                            LN_EPS, None, ALU.add)),
                "VE", "var2", 1)
            wait(eng, "SC", "srt2")
            E(lambda: eng.reciprocal(x1fr[:, 0:512], x1or[:, 0:512]))
            E(lambda: eng.reciprocal(x1fr[:, 512:1024], x1or[:, 512:1024]))
            E(lambda: eng.tensor_copy(sb_rbf.ap()[0:1, 0:1024],
                                      rows_ln2[0:1, 0:1024]))
            inc(E(lambda: eng.tensor_copy(sb_rbf.ap()[0:1, 1024:2048],
                                          x1fr[:, 0:1024])),
                "VE", "rows2", 1)
            if stage < 4:
                return
            wait(eng, "SC", "bc2e")
            for fo in range(NFO):
                if fo >= 2:
                    wait(eng, "PE", f"tr_{fo-2}_7")
                E(lambda fo=fo: eng.tensor_tensor(
                    sb_nrm_t[:, :], x1bf[:, fo * 1024:(fo + 1) * 1024],
                    sb_mbc.ap()[:, 0:1024], ALU.subtract))
                inc(E(lambda fo=fo: eng.tensor_tensor(
                    y2r[fo % 2][:, :], sb_nrm_t[:, :],
                    sb_mbc.ap()[:, 1024:2048], ALU.mult)),
                    "VE", f"y2_{fo}", 1)
            wait(eng, "PE", "u")
            E(lambda: eng.tensor_copy(sb_u.ap()[0:16, 0:512], pb[4][0:16, :]))
            inc(E(lambda: eng.tensor_copy(sb_u.ap()[0:16, 512:1024],
                                          pb[5][0:16, :])), "VE", "ue", 1)
            if stage < 5:
                return
            NG_ALL = NTT * NB

            def ve_argmax(g2):
                t2, j2 = g2 // NB, g2 % NB
                wait(eng, "SC", f"ie_{t2}_{j2}")
                if g2 == 0:
                    wait(eng, "LC", "b_iota")
                inc(E(lambda g2=g2: eng.reduce_max(
                    sb_m8.ap()[:, 0:1], ie_dst[g2 % 3][:, 0:256],
                    axis=mybir.AxisListType.X)),
                    "VE", f"amx_{t2}_{j2}", 1)
                wait(eng, "VE", f"amx_{t2}_{j2}")
                E(lambda g2=g2: eng.tensor_scalar(
                    sb_pt1.ap()[:, 256:512], ie_dst[g2 % 3][:, 0:256],
                    sb_m8.ap()[:, 0:1], None, ALU.is_ge))
                E(lambda: eng.tensor_tensor(
                    sb_sq.ap()[:, 0:256], sb_pt1.ap()[:, 256:512],
                    sb_pt0.ap()[:, 256:512], ALU.mult))
                inc(E(lambda: eng.reduce_max(
                    sb_m8.ap()[:, 1:2], sb_sq.ap()[:, 0:256],
                    axis=mybir.AxisListType.X)),
                    "VE", f"amr_{t2}_{j2}", 1)
                wait(eng, "VE", f"amr_{t2}_{j2}")
                inc(E(lambda g2=g2: eng.tensor_copy(
                    sb_i8.ap()[:, g2 % 4:g2 % 4 + 1],
                    sb_m8.ap()[:, 1:2])),
                    "VE", f"am_{t2}_{j2}", 1)

            def ve_add(p):
                pt_, pj = p // NB, p % NB
                wait(eng, "SC", f"hrl_{pt_}_{pj}")
                wait(eng, "GA", f"g2_{pt_}_{pj}")
                inc(E(lambda pt_=pt_, pj=pj: eng.tensor_scalar(
                    w2sc[:, :], w2buf[pj % 2][:, :],
                    sb_hr.ap()[:, pj:pj + 1], None, ALU.mult)),
                    "VE", f"ws_{pt_}_{pj}", 1)
                if pj == 0:
                    if pt_ >= 1:
                        wait(eng, "PE", f"ft_{pt_-1}_{NFO-1}")
                    inc(E(lambda: eng.tensor_copy(
                        acc_bf[1][:, :], w2sc[:, :])),
                        "VE", f"o_{pt_}_0", 1)
                else:
                    if pj == 1 and pt_ >= 1:
                        wait(eng, "PE", f"ft_{pt_-1}_{NFO-1}")
                    inc(E(lambda pj=pj: eng.tensor_tensor(
                        acc_bf[(pj + 1) % 2][:, :], w2sc[:, :],
                        acc_bf[pj % 2][:, :], ALU.add)),
                        "VE", f"o_{pt_}_{pj}", 1)
                if pj == NB - 1:
                    for fo in range(NFO):
                        wait(eng, "PE", f"ft_{pt_}_{fo}")
                        wait(eng, "LC", f"b_x1c{pt_}")
                        if pt_ >= 1 and fo == 0:
                            wait(eng, "SD", f"y_{pt_-1}")
                        inc(E(lambda pt_=pt_, fo=fo: eng.tensor_tensor(
                            otile[:, fo * 128:(fo + 1) * 128],
                            pt[fo % 2][:, :],
                            x1c[:, fo * 128:(fo + 1) * 128], ALU.add)),
                            "VE", f"fa_{pt_}_{fo}", 1)

            ve_argmax(0)
            ve_argmax(1)
            ve_argmax(2)
            for g in range(NG_ALL):
                tt, j = g // NB, g % NB
                la = g + 3 < NG_ALL
                if la:
                    g3 = g + 3
                    t2, j2 = g3 // NB, g3 % NB
                    wait(eng, "SC", f"ie_{t2}_{j2}")
                    inc(E(lambda g3=g3: eng.reduce_max(
                        sb_m8.ap()[:, 0:1], ie_dst[g3 % 3][:, 0:256],
                        axis=mybir.AxisListType.X)),
                        "VE", f"amx_{t2}_{j2}", 1)
                wait(eng, "GA", f"g_{tt}_{j}")
                if g >= 2:
                    p2, q2 = (g - 2) // NB, (g - 2) % NB
                    wait(eng, "SC", f"ha_{p2}_{q2}")
                inc(E(lambda tt=tt, j=j, g=g: eng.tensor_tensor(
                    prod[g % 2][:, :], w1buf[j % 2][:, :],
                    y2tm[:, tt * 4096:(tt + 1) * 4096], ALU.mult)),
                    "VE", f"pr_{tt}_{j}", 1)
                if la:
                    wait(eng, "VE", f"amx_{t2}_{j2}")
                    E(lambda g3=g3: eng.tensor_scalar(
                        sb_pt1.ap()[:, 256:512], ie_dst[g3 % 3][:, 0:256],
                        sb_m8.ap()[:, 0:1], None, ALU.is_ge))
                    E(lambda: eng.tensor_tensor(
                        sb_sq.ap()[:, 0:256], sb_pt1.ap()[:, 256:512],
                        sb_pt0.ap()[:, 256:512], ALU.mult))
                    inc(E(lambda: eng.reduce_max(
                        sb_m8.ap()[:, 1:2], sb_sq.ap()[:, 0:256],
                        axis=mybir.AxisListType.X)),
                        "VE", f"amr_{t2}_{j2}", 1)
                if g >= 1:
                    ve_add(g - 1)
                if la:
                    wait(eng, "VE", f"amr_{t2}_{j2}")
                    inc(E(lambda g3=g3: eng.tensor_copy(
                        sb_i8.ap()[:, g3 % 4:g3 % 4 + 1],
                        sb_m8.ap()[:, 1:2])),
                        "VE", f"am_{t2}_{j2}", 1)
            ve_add(NG_ALL - 1)

        ctx.emit = False
        stream_sync(None)
        stream_gpsimd(None)
        stream_tensor(None)
        stream_scalar(None)
        stream_vector(None)

        ctx.emit = True

        @block.sync
        def _(eng):
            stream_sync(eng)

        @block.gpsimd
        def _(eng):
            stream_gpsimd(eng)

        @block.tensor
        def _(eng):
            stream_tensor(eng)

        @block.scalar
        def _(eng):
            stream_scalar(eng)

        @block.vector
        def _(eng):
            stream_vector(eng)

    return nc


def _bf(x):
    return np.ascontiguousarray(x).astype(ml_dtypes.bfloat16)


def kernel(L0_attn_hidden_states, L0_attn_Wq, L0_attn_Wkv, L0_attn_Wo,
           L0_ffn_c1_w, L0_ffn_c2_w, L0_ffn_w1, L0_ffn_w2t, _debug=False,
           _stage=99, _fsub=99):
    X = np.asarray(L0_attn_hidden_states, dtype=np.float32)
    Wq = np.asarray(L0_attn_Wq, dtype=np.float32)
    Wkv = np.asarray(L0_attn_Wkv, dtype=np.float32)
    Wo = np.asarray(L0_attn_Wo, dtype=np.float32)
    c1 = np.asarray(L0_ffn_c1_w, dtype=np.float32)
    c2 = np.asarray(L0_ffn_c2_w, dtype=np.float32)
    w1 = np.asarray(L0_ffn_w1, dtype=np.float32)
    w2t = np.asarray(L0_ffn_w2t, dtype=np.float32)

    wq_t = _bf(Wq.reshape(NFO, 128, NFO, 128).transpose(2, 1, 0, 3)
               .reshape(NFO, 128, D))
    wo_t = _bf(Wo.reshape(NFO, 128, NFO, 128).transpose(2, 1, 0, 3)
               .reshape(NFO, 128, D))
    wkv_t = _bf(Wkv.reshape(NFO, 128, 256).transpose(1, 0, 2).reshape(128, 8192))
    c1t = _bf(c1.T.reshape(NFO, 128, 16).transpose(1, 0, 2).reshape(128, 512))
    c2t = _bf(c2.T)
    w1w2 = np.concatenate([_bf(w1), _bf(w2t)], axis=1)
    mask4 = _host_masks()
    u1 = np.triu(np.ones((128, 128))).astype(ml_dtypes.bfloat16)
    ident = np.eye(128).astype(ml_dtypes.bfloat16)
    ones = np.ones((128, 128)).astype(ml_dtypes.bfloat16)
    iota = np.broadcast_to(np.arange(256, dtype=np.float32), (128, 256)) \
        .astype(ml_dtypes.bfloat16)

    nc = build_program(debug=_debug, stage=_stage, fsub=_fsub)
    in_maps = []
    for c in range(8):
        perm, _, _ = _core_geom(c)
        b = c // 2
        xp = X[b][perm]
        in_maps.append({
            "xfmt": _bf(xp.T.reshape(NFO, 128, S)),
            "xf32": np.ascontiguousarray(
                xp[:NTOK].T.reshape(NFO, 128, NTOK)).astype(np.float32),
            "wq_t": wq_t, "wo_t": wo_t, "wkv_t": wkv_t,
            "c1t": c1t, "c2t": c2t, "w1w2": w1w2,
            "biasm": _host_biasmat(c), "mask4": mask4,
            "u1": u1, "ident": ident, "onesb": ones, "iota": iota,
        })

    res = run_bass_kernel_spmd(nc, in_maps, core_ids=list(range(8)))

    out = np.empty((B, S, D), dtype=np.float32)
    for c in range(8):
        perm, _, _ = _core_geom(c)
        b = c // 2
        yc = np.asarray(res.results[c]["y"])
        toks = yc.reshape(NTT, 128, NFO, 128).transpose(0, 3, 2, 1) \
                 .reshape(NTOK, D)
        out[b][perm[:NTOK]] = toks
    if _debug:
        return out, res
    return out


# version 12 (final)
# speedup vs baseline: 1.7817x; 1.0150x over previous
import math
import numpy as np
import ml_dtypes

import concourse.bass as bass
import concourse.mybir as mybir
from concourse.bass import IndirectOffsetOnAxis
from concourse.bass_utils import run_bass_kernel_spmd

F32 = mybir.dt.float32
BF16 = mybir.dt.bfloat16
U8 = mybir.dt.uint8
U32 = mybir.dt.uint32
ALU = mybir.AluOpType
AF = mybir.ActivationFunctionType

IDX_ZERO = False
DBG_H = False
GLIMIT = 10**9
PLAIN_GATHER = False

B, S, D, NH, HD = 4, 2048, 4096, 32, 128
INTER, SPR, NB, RANK = 14336, 256, 56, 16
SCALE = HD ** -0.5
NTOK = 1024
NKT = 16
NFO = 32
NTT = 8
LN_EPS = 1e-5
MASK_NEG = -50000.0
KILL = -1e9

_start = 2.0 ** (-(2.0 ** -(math.log2(NH) - 3)))
SLOPES = [float(np.float32(_start ** (i + 1))) for i in range(NH)]

SLOTSETS = {0: [0, 1, 2, 3, 8, 9, 10, 11], 1: list(range(16))}
MASKED = {0: {0: 0, 1: 1, 2: 2, 3: 3}, 1: {4: 0, 5: 1, 6: 2, 7: 3}}


def n_groups(h):
    sh = SLOPES[h]
    if sh * 256.0 <= 66.0:
        return 1
    if sh * 128.0 <= 66.0:
        return 2
    return 4


def _core_geom(c):
    half = c % 2
    if half == 0:
        perm = np.concatenate([np.arange(0, 512), np.arange(1536, 2048),
                               np.arange(512, 1536)])
        q0s = (0, 1536)
    else:
        perm = np.concatenate([np.arange(512, 1536), np.arange(0, 512),
                               np.arange(1536, 2048)])
        q0s = (512, 1024)
    tile_orig = [int(perm[s * 128]) for s in range(NKT)]
    return perm, q0s, tile_orig


_BIAS_COLS = []
COLOF = {}
for _h in range(NH):
    for _qc in (0, 1):
        for _si in range(len(SLOTSETS[_qc])):
            for _g in range(n_groups(_h)):
                COLOF[(_h, _qc, _si, _g)] = len(_BIAS_COLS)
                _BIAS_COLS.append((_h, _qc, _si, _g))
NCOL = len(_BIAS_COLS)


def _host_biasmat(c):
    _, q0s, torig = _core_geom(c)
    bm = np.zeros((128, NCOL), dtype=np.float64)
    p = np.arange(128, dtype=np.float64)
    for col, (h, qc, si, g) in enumerate(_BIAS_COLS):
        s = SLOTSETS[qc][si]
        k0, q0 = torig[s], q0s[qc]
        if k0 > q0 + 511:
            bm[:, col] = KILL
        else:
            w = 512 // n_groups(h)
            qt = q0 + w * g + w // 2
            bm[:, col] = SLOPES[h] * (k0 + p - qt)
    return bm.astype(np.float32)


def _host_masks():
    m = np.zeros((4, 128, 512), dtype=np.float32)
    c = np.arange(128)[:, None]
    j = np.arange(512)[None, :]
    for d in range(4):
        m[d] = np.where(c > j - 128 * d, MASK_NEG, 0.0)
    return m.astype(ml_dtypes.bfloat16)


class Ctx:
    SEMS = ("LD", "LA", "LB", "LC", "SD", "GA", "PE", "VE", "SC")

    def __init__(self):
        self.emit = False
        self.cnt = {s: 0 for s in self.SEMS}
        self.ev = {}
        self.sems = {}

    def inc(self, inst, sem, name, n):
        if self.emit:
            assert inst is not None, (sem, name)
            inst.then_inc(self.sems[sem], n)
        else:
            self.cnt[sem] += n
            self.ev[(sem, name)] = self.cnt[sem]

    def wait(self, eng, sem, name):
        if self.emit:
            eng.wait_ge(self.sems[sem], self.ev[(sem, name)])

    def E(self, thunk):
        return thunk() if self.emit else None

    def mark(self, sem, name):
        if not self.emit:
            self.ev[(sem, name)] = self.cnt[sem]


def build_program(debug=False, stage=99, fsub=99):
    nc = bass.Bass("TRN2", target_bir_lowering=False)

    d_xfmt = nc.dram_tensor("xfmt", [NFO, 128, S], BF16, kind="ExternalInput").ap()
    d_xf32 = nc.dram_tensor("xf32", [NFO, 128, NTOK], F32, kind="ExternalInput").ap()
    d_wq = nc.dram_tensor("wq_t", [NFO, 128, D], BF16, kind="ExternalInput").ap()
    d_wo = nc.dram_tensor("wo_t", [NFO, 128, D], BF16, kind="ExternalInput").ap()
    d_wkv = nc.dram_tensor("wkv_t", [128, 8192], BF16, kind="ExternalInput").ap()
    d_c1 = nc.dram_tensor("c1t", [128, 512], BF16, kind="ExternalInput").ap()
    d_c2 = nc.dram_tensor("c2t", [16, INTER], BF16, kind="ExternalInput").ap()
    d_w12 = nc.dram_tensor("w1w2", [INTER, 8192], BF16, kind="ExternalInput").ap()
    d_bias = nc.dram_tensor("biasm", [128, NCOL], F32, kind="ExternalInput").ap()
    d_mask = nc.dram_tensor("mask4", [4, 128, 512], BF16, kind="ExternalInput").ap()
    d_u1 = nc.dram_tensor("u1", [128, 128], BF16, kind="ExternalInput").ap()
    d_id = nc.dram_tensor("ident", [128, 128], BF16, kind="ExternalInput").ap()
    d_ones = nc.dram_tensor("onesb", [128, 128], BF16, kind="ExternalInput").ap()
    d_iota = nc.dram_tensor("iota", [128, 256], BF16, kind="ExternalInput").ap()
    d_x1s = nc.dram_tensor("x1s", [NFO, 128, NTOK], F32, kind="Internal").ap()
    d_y = nc.dram_tensor("y", [NTT, 128, D], F32, kind="ExternalOutput").ap()
    dbg = {}
    if debug:
        for nm, shp, dt in [("dbg_y1", [128, NFO * NTOK], BF16),
                            ("dbg_q", [128, NH * NTOK], BF16),
                            ("dbg_kf", [128, NKT * 128], BF16),
                            ("dbg_vt", [128, NKT * 128], BF16),
                            ("dbg_ao", [128, NH * NTOK], BF16),
                            ("dbg_x1", [128, NFO * NTOK], BF16),
                            ("dbg_u", [16, NTOK], BF16),
                            ("dbg_y2", [128, NTT * D], BF16),
                            ("dbg_h", [NTT, 128, 56], F32)]:
            dbg[nm] = nc.dram_tensor(nm, shp, dt, kind="ExternalOutput").ap()

    ctx = Ctx()

    from contextlib import ExitStack
    with ExitStack() as _st:
        def _sb(name, shape, dt):
            return _st.enter_context(nc.sbuf_tensor(name, shape, dt))

        def _ps(name, shape, dt):
            return _st.enter_context(nc.psum_tensor(name, shape, dt))

        arA = _sb("arA", [128, 65536], U8)
        arB = _sb("arB", [128, 32768], BF16)
        arC = _sb("arC", [128, 49152], U8)
        sb_bias = _sb("sb_bias", [128, NCOL], F32)
        sb_u1 = _sb("sb_u1", [128, 128], BF16)
        sb_id = _sb("sb_id", [128, 128], BF16)
        sb_ones = _sb("sb_ones", [128, 128], BF16)
        sb_kf = _sb("sb_kf", [128, NKT * 128], BF16)
        sb_vt = _sb("sb_vt", [128, NKT * 128], BF16)
        sb_c1 = _sb("sb_c1", [128, 512], BF16)
        sb_pt0 = _sb("sb_pt0", [128, 512], BF16)
        sb_pt1 = _sb("sb_pt1", [128, 512], BF16)
        sb_mbc = _sb("sb_mbc", [128, 2048], BF16)
        sb_sq = _sb("sb_sq", [128, 1024], BF16)
        sb_rrow = _sb("sb_rrow", [1, 512], F32)
        sb_rbf = _sb("sb_rbf", [1, 2048], BF16)
        sb_u = _sb("sb_u", [16, 1024], BF16)
        sb_c2b0 = _sb("sb_c2b0", [16, 256], BF16)
        sb_c2b1 = _sb("sb_c2b1", [16, 256], BF16)
        sb_m8 = _sb("sb_m8", [128, 8], F32)
        sb_i8 = _sb("sb_i8", [128, 16], U32)
        sb_h = _sb("sb_h", [128, 64], F32)
        sb_hr = _sb("sb_hr", [128, 64], F32)
        pb0 = _ps("pb0", [128, 512], F32)
        pb1 = _ps("pb1", [128, 512], F32)
        pb2 = _ps("pb2", [128, 512], F32)
        pb3 = _ps("pb3", [128, 512], F32)
        pb4 = _ps("pb4", [128, 512], F32)
        pb5 = _ps("pb5", [128, 512], F32)
        pt0 = _ps("pt0", [128, 128], BF16)
        pt1 = _ps("pt1", [128, 128], BF16)
        sLD = _st.enter_context(nc.semaphore("sLD"))
        sLA = _st.enter_context(nc.semaphore("sLA"))
        sLB = _st.enter_context(nc.semaphore("sLB"))
        sLC = _st.enter_context(nc.semaphore("sLC"))
        sSD = _st.enter_context(nc.semaphore("sSD"))
        sGA = _st.enter_context(nc.semaphore("sGA"))
        sPE = _st.enter_context(nc.semaphore("sPE"))
        sVE = _st.enter_context(nc.semaphore("sVE"))
        sSC = _st.enter_context(nc.semaphore("sSC"))
        block = _st.enter_context(nc.Block())
        ctx.sems = {"LD": sLD, "LA": sLA, "LB": sLB, "LC": sLC,
                    "SD": sSD, "GA": sGA,
                    "PE": sPE, "VE": sVE, "SC": sSC}
        pb = [pb0, pb1, pb2, pb3, pb4, pb5]
        pt = [pt0, pt1]

        vA = arA.ap().bitcast(BF16)
        y1fm = vA
        attnout = vA
        y2tm = vA
        vB = arB.ap()
        qT = vB
        x1bf = vB
        wbuf = [vB[:, 0:8192], vB[:, 8192:16384]]
        x1c = arB.ap().bitcast(F32)[:, 8192:12288]
        otile = arB.ap().bitcast(F32)[:, 12288:16384]
        vC16 = arC.ap().bitcast(BF16)
        xchunk = vC16[:, 0:16384]
        wkv_sb = vC16[:, 16384:24576]
        wqbuf = [vC16[:, 0:4096], vC16[:, 4096:8192]]
        wobuf = wqbuf
        acc_bf = [vC16[:, 0:4096], vC16[:, 4096:8192]]
        w1buf = [vB[:, 0:4096], vB[:, 4096:8192]]
        w2buf = [vB[:, 8192:12288], vB[:, 12288:16384]]
        w2sc = vC16[:, 16384:20480]
        sc_scr = vC16[:, 12288:16384]
        prod = [vC16[:, 20480:24576], vC16[:, 8192:12288]]
        sb_ykv = vB[:, 0:1024]
        sb_nrm_kv = vB[:, 1024:2048]
        sb_vfm = vB[:, 2048:2560]
        vBf32 = arB.ap().bitcast(F32)
        rows_kv = vBf32[0:1, 1280:3328]
        vCf32 = arC.ap().bitcast(F32)
        sb_x1o = vCf32[:, 4096:5120]
        sb_x1f = vCf32[:, 5120:6144]
        rows_ln2 = vCf32[0:1, 6144:8192]
        y2r = [vC16[:, 8192:9216], vC16[:, 9216:10240]]
        sb_nrm_t = vC16[:, 10240:11264]
        sb_mask = vC16[:, 14336:16384]
        sb_rbc = vC16[:, 6144:6656]

        P_tiles = [sb_pt0, sb_pt1]
        ie_dst = [sb_pt0.ap()[:, 0:256], sb_pt1.ap()[:, 0:256],
                  sb_sq.ap()[:, 512:768]]
        P3 = [sb_pt0.ap(), sb_pt1.ap(), sb_sq.ap()[:, 0:512]]
        c2b = [sb_c2b0.ap(), sb_c2b1.ap(),
               sb_sq.ap()[0:16, 768:1024]]

        def q_off(h, qc):
            return h * 1024 + qc * 512

        def stream_sync(eng):
            E, inc, wait = ctx.E, ctx.inc, ctx.wait
            for nm, dst, src in [("c_bias", sb_bias.ap(), d_bias),
                                 ("c_u1", sb_u1.ap(), d_u1),
                                 ("c_id", sb_id.ap(), d_id),
                                 ("c_ones", sb_ones.ap(), d_ones),
                                 ("c_c1", sb_c1.ap(), d_c1),
                                 ("c_wkv", wkv_sb, d_wkv)]:
                inc(E(lambda dst=dst, src=src: eng.dma_start(out=dst, in_=src)),
                    "LC", nm, 16)
            ctx.mark("LC", "b_pre")
            wait(eng, "LC", "b_pre")
            for tc in range(4):
                for kt in range(NFO):
                    if tc >= 1:
                        wait(eng, "VE", f"y_{tc-1}_{kt}")
                    inc(E(lambda tc=tc, kt=kt: eng.dma_start(
                        out=xchunk[:, kt * 512:(kt + 1) * 512],
                        in_=d_xfmt[kt][:, tc * 512:(tc + 1) * 512])),
                        "LC", f"xc_{tc}_{kt}", 16)
                ctx.mark("LC", f"b_xc{tc}")
                wait(eng, "LC", f"b_xc{tc}")
            if stage < 1:
                return
            wait(eng, "VE", "y_3_31")
            for dd in range(4):
                inc(E(lambda dd=dd: eng.dma_start(
                    out=sb_mask[:, dd * 512:(dd + 1) * 512],
                    in_=d_mask[dd])), "LC", f"c_mask{dd}", 16)
            ctx.mark("LC", "b_mask")
            wait(eng, "LC", "b_mask")
            for m in range(NFO):
                if m >= 2:
                    wait(eng, "PE", f"q_{m-2}")
                inc(E(lambda m=m: eng.dma_start(out=wqbuf[m % 2],
                                                in_=d_wq[m])),
                    "LA" if m % 2 == 0 else "LB", f"wq_{m}", 16)
            if stage < 3:
                return
            for fo in range(NFO):
                if fo == 0:
                    wait(eng, "VE", "ao_31_1")
                if fo >= 2:
                    wait(eng, "PE", f"wo_{fo-2}")
                inc(E(lambda fo=fo: eng.dma_start(out=wobuf[fo % 2],
                                                  in_=d_wo[fo])),
                    "LA" if fo % 2 == 0 else "LB", f"wwo_{fo}", 16)
                if fo >= 1:
                    wait(eng, "VE", f"x1_{fo-1}")
                inc(E(lambda fo=fo: eng.dma_start(out=sb_x1f,
                                                  in_=d_xf32[fo])),
                    "LA" if fo % 2 == 0 else "LB", f"xf_{fo}", 16)
            if stage < 5:
                return
            wait(eng, "PE", f"av_31_1_{len(SLOTSETS[1])-1}")
            inc(E(lambda: eng.dma_start(out=sb_pt0.ap()[:, 256:512],
                                        in_=d_iota)), "LC", "iota", 16)
            ctx.mark("LC", "b_iota")
            wait(eng, "LC", "b_iota")
            def sy_c2(g2):
                t2, j2 = g2 // NB, g2 % NB
                if g2 >= 3:
                    p2, q2 = (g2 - 3) // NB, (g2 - 3) % NB
                    wait(eng, "PE", f"int_{p2}_{q2}")
                elif g2 == 2:
                    wait(eng, "PE", f"st2_{NFO-1}")
                inc(E(lambda t2=t2, j2=j2, g2=g2: eng.dma_start(
                    out=c2b[g2 % 3],
                    in_=d_c2[:, j2 * 256:(j2 + 1) * 256])),
                    ("LA", "LB", "LD")[g2 % 3], f"c2_{t2}_{j2}", 16)

            def sy_x1c(tt):
                if tt == 0:
                    wait(eng, "VE", "ue")
                    wait(eng, "SD", f"x1s_{NFO-1}")
                else:
                    wait(eng, "VE", f"fa_{tt-1}_31")
                for fo in range(NFO):
                    inc(E(lambda tt=tt, fo=fo: eng.dma_start(
                        out=x1c[:, fo * 128:(fo + 1) * 128],
                        in_=d_x1s[fo][:, tt * 128:(tt + 1) * 128])),
                        "LC", f"x1c_{tt}_{fo}", 16)
                ctx.mark("LC", f"b_x1c{tt}")
                wait(eng, "LC", f"b_x1c{tt}")

            for g in range(NTT * NB):
                sy_c2(g)
                if g % NB == 3:
                    sy_x1c(g // NB)
            if NB <= 3:
                raise AssertionError

        def stream_gpsimd(eng):
            E, inc, wait = ctx.E, ctx.inc, ctx.wait
            if debug:
                wait(eng, "VE", "y_1_31")
                inc(E(lambda: eng.dma_start(out=dbg["dbg_y1"], in_=y1fm)),
                    "GA", "dbg_y1", 16)
                wait(eng, "VE", "kve_3")
                wait(eng, "SC", "vt_3_3")
                inc(E(lambda: eng.dma_start(out=dbg["dbg_kf"], in_=sb_kf.ap())),
                    "GA", "dbg_kf", 16)
                inc(E(lambda: eng.dma_start(out=dbg["dbg_vt"], in_=sb_vt.ap())),
                    "GA", "dbg_vt", 16)
                wait(eng, "VE", "qe_31")
                inc(E(lambda: eng.dma_start(out=dbg["dbg_q"], in_=qT)),
                    "GA", "dbg_q", 16)
                wait(eng, "VE", "ao_31_1")
                inc(E(lambda: eng.dma_start(out=dbg["dbg_ao"], in_=attnout)),
                    "GA", "dbg_ao", 16)
                wait(eng, "VE", "x1_31")
                inc(E(lambda: eng.dma_start(out=dbg["dbg_x1"], in_=x1bf)),
                    "GA", "dbg_x1", 16)
                wait(eng, "VE", "ue")
                inc(E(lambda: eng.dma_start(out=dbg["dbg_u"], in_=sb_u.ap())),
                    "GA", "dbg_u", 16)
                wait(eng, "SC", "tre_31_7")
                inc(E(lambda: eng.dma_start(out=dbg["dbg_y2"], in_=y2tm)),
                    "GA", "dbg_y2", 16)
            if stage < 5:
                return
            NG_ALL = NTT * NB
            for g in range(NG_ALL):
                tt, j = g // NB, g % NB
                wait(eng, "VE", f"am_{tt}_{j}")
                if g >= 2:
                    pt_, pj = (g - 2) // NB, (g - 2) % NB
                    wait(eng, "VE", f"pr_{pt_}_{pj}")
                inc(E(lambda tt=tt, j=j, g=g: eng.indirect_dma_start(
                    w1buf[j % 2],
                    None,
                    d_w12,
                    IndirectOffsetOnAxis(
                        ap=sb_i8.ap()[:, g % 4:g % 4 + 1], axis=0),
                    element_offset=j * 256 * 8192)),
                    "GA", f"g_{tt}_{j}", 16)
                if g >= 2:
                    wait(eng, "VE", f"ws_{pt_}_{pj}")
                inc(E(lambda tt=tt, j=j, g=g: eng.indirect_dma_start(
                    w2buf[j % 2],
                    None,
                    d_w12,
                    IndirectOffsetOnAxis(
                        ap=sb_i8.ap()[:, g % 4:g % 4 + 1], axis=0),
                    element_offset=j * 256 * 8192 + 4096)),
                    "GA", f"g2_{tt}_{j}", 16)

        def stream_tensor(eng):
            E, inc, wait = ctx.E, ctx.inc, ctx.wait
            wait(eng, "LC", "b_pre")
            for tc in range(4):
                wait(eng, "LC", f"b_xc{tc}")
                for kt in range(NFO):
                    E(lambda tc=tc, kt=kt: eng.matmul(
                        pb[0][0:1, :], sb_ones.ap()[:, 0:1],
                        xchunk[:, kt * 512:(kt + 1) * 512],
                        start=(kt == 0), stop=(kt == NFO - 1)))
                    wait(eng, "SC", f"sq_{tc}_{kt}")
                    inc(E(lambda kt=kt: eng.matmul(
                        pb[1][0:1, :], sb_ones.ap()[:, 0:1],
                        sb_sq.ap()[:, (kt % 2) * 512:(kt % 2) * 512 + 512],
                        start=(kt == 0), stop=(kt == NFO - 1))),
                        "PE", f"sx_{tc}_{kt}", 1)
                wait(eng, "VE", f"rows_{tc}")
                E(lambda: eng.matmul(pb[2][:, :], sb_ones.ap()[0:1, 0:128],
                                     sb_rbf.ap()[0:1, 0:512],
                                     start=True, stop=True))
                inc(E(lambda: eng.matmul(pb[3][:, :], sb_ones.ap()[0:1, 0:128],
                                         sb_rbf.ap()[0:1, 512:1024],
                                         start=True, stop=True)),
                    "PE", f"bc_{tc}", 1)
                for kt in range(NFO):
                    wait(eng, "VE", f"y_{tc}_{kt}")
                    if tc < 2:
                        dst = y1fm[:, kt * 1024 + tc * 512:
                                   kt * 1024 + tc * 512 + 512]
                    else:
                        dst = sb_ykv[:, (kt % 2) * 512:(kt % 2) * 512 + 512]
                    E(lambda kt=kt, dst=dst: eng.matmul(
                        pb[4][:, :], wkv_sb[:, kt * 256:kt * 256 + 128], dst,
                        start=(kt == 0), stop=(kt == NFO - 1)))
                    inc(E(lambda kt=kt, dst=dst: eng.matmul(
                        pb[5][:, :], wkv_sb[:, kt * 256 + 128:kt * 256 + 256],
                        dst, start=(kt == 0), stop=(kt == NFO - 1))),
                        "PE", f"kv_{tc}_{kt}", 1)
                wait(eng, "VE", f"kve_{tc}")
                for i in range(4):
                    if i >= 2:
                        wait(eng, "SC", f"vt_{tc}_{i-2}")
                    elif tc >= 1:
                        wait(eng, "SC", f"vt_{tc-1}_{2+i}")
                    inc(E(lambda i=i: eng.transpose(
                        pt[i % 2][:, :], sb_vfm[:, i * 128:(i + 1) * 128],
                        sb_id.ap())), "PE", f"vt_{tc}_{i}", 1)
            if stage < 1:
                return
            for m in range(NFO):
                wait(eng, "LA" if m % 2 == 0 else "LB", f"wq_{m}")
                if m >= 2:
                    wait(eng, "VE", f"qe_{m-2}")
                for kt in range(NFO):
                    E(lambda m=m, kt=kt: eng.matmul(
                        pb[(m % 2) * 2][:, :],
                        wqbuf[m % 2][:, kt * 128:(kt + 1) * 128],
                        y1fm[:, kt * 1024:kt * 1024 + 512],
                        start=(kt == 0), stop=(kt == NFO - 1)))
                    i2 = E(lambda m=m, kt=kt: eng.matmul(
                        pb[(m % 2) * 2 + 1][:, :],
                        wqbuf[m % 2][:, kt * 128:(kt + 1) * 128],
                        y1fm[:, kt * 1024 + 512:kt * 1024 + 1024],
                        start=(kt == 0), stop=(kt == NFO - 1)))
                    if kt == NFO - 1:
                        inc(i2, "PE", f"q_{m}", 1)
            if stage < 2:
                return
            wait(eng, "LC", "b_mask")
            PSA3 = [pb[0], pb[1], pb[5]]
            last_exp = {}
            for qc in (0, 1):
                slots = SLOTSETS[qc]
                nsl = len(slots)
                for h in range(NH):
                    wait(eng, "VE", f"qe_{NFO-1}" if (qc == 0 and h == 0)
                         else f"qe_{h}")
                    for si in range(nsl + 2):
                        if si == 2 and not (qc == 0 and h == 0):
                            ph, pqc = (h - 1, qc) if h > 0 else (NH - 1, 0)
                            wait(eng, "VE", f"ao_{ph}_{pqc}")
                        if si < nsl:
                            s = slots[si]
                            prv = last_exp.get(si % 3)
                            if prv is not None:
                                wait(eng, "SC", f"exp_{prv}")
                            last_exp[si % 3] = f"{h}_{qc}_{si}"
                            mk = MASKED[qc].get(s)
                            psA = PSA3[si % 3]
                            i1 = E(lambda s=s, h=h, qc=qc, psA=psA, mk=mk:
                                   eng.matmul(
                                       psA[:, :],
                                       sb_kf.ap()[:, s * 128:(s + 1) * 128],
                                       qT[:, q_off(h, qc):q_off(h, qc) + 512],
                                       start=True, stop=(mk is None)))
                            if mk is not None:
                                i1 = E(lambda mk=mk, psA=psA: eng.matmul(
                                    psA[:, :], sb_u1.ap(),
                                    sb_mask[:, mk * 512:(mk + 1) * 512],
                                    start=False, stop=True))
                            inc(i1, "PE", f"sc_{h}_{qc}_{si}", 1)
                        if si >= 2:
                            pi = si - 2
                            s_prev = slots[pi]
                            wait(eng, "SC", f"exp_{h}_{qc}_{pi}")
                            E(lambda s_prev=s_prev, pi=pi, nsl=nsl: eng.matmul(
                                pb[2][:, :],
                                sb_vt.ap()[:, s_prev * 128:(s_prev + 1) * 128],
                                P3[pi % 3][:, :],
                                start=(pi == 0), stop=(pi == nsl - 1)))
                            inc(E(lambda pi=pi, nsl=nsl: eng.matmul(
                                pb[3][0:1, :], sb_ones.ap()[:, 0:1],
                                P3[pi % 3][:, :],
                                start=(pi == 0), stop=(pi == nsl - 1))),
                                "PE", f"av_{h}_{qc}_{pi}", 1)
                    wait(eng, "VE", f"r_{h}_{qc}")
                    if not (qc == 0 and h == 0):
                        ph, pqc = (h - 1, qc) if h > 0 else (NH - 1, 0)
                        wait(eng, "SC", f"rbe_{ph}_{pqc}")
                    inc(E(lambda: eng.matmul(pb[4][:, :],
                                             sb_ones.ap()[0:1, 0:128],
                                             sb_rbf.ap()[0:1, 0:512],
                                             start=True, stop=True)),
                        "PE", f"rb_{h}_{qc}", 1)
            if stage < 3:
                return
            for fo in range(NFO):
                wait(eng, "LA" if fo % 2 == 0 else "LB", f"xf_{fo}")
                if fo == 0:
                    wait(eng, "SC", f"exp_31_1_{len(SLOTSETS[1])-1}")
                    wait(eng, "SC", "rbe_31_1")
                if fo == 1:
                    wait(eng, "VE", "ao_31_1")
                if fo >= 2:
                    wait(eng, "VE", f"x1_{fo-2}")
                for kt in range(NFO):
                    E(lambda fo=fo, kt=kt: eng.matmul(
                        pb[(fo % 2) * 2][:, :],
                        wobuf[fo % 2][:, kt * 128:(kt + 1) * 128],
                        attnout[:, kt * 1024:kt * 1024 + 512],
                        start=(kt == 0), stop=(kt == NFO - 1)))
                    i2 = E(lambda fo=fo, kt=kt: eng.matmul(
                        pb[(fo % 2) * 2 + 1][:, :],
                        wobuf[fo % 2][:, kt * 128:(kt + 1) * 128],
                        attnout[:, kt * 1024 + 512:kt * 1024 + 1024],
                        start=(kt == 0), stop=(kt == NFO - 1)))
                    if kt == NFO - 1:
                        inc(i2, "PE", f"wo_{fo}", 1)
                wait(eng, "VE", f"x1_{fo}")
                wait(eng, "SC", f"sq2_{fo}")
                E(lambda fo=fo: eng.matmul(
                    pb[4][0:1, :], sb_ones.ap()[:, 0:1],
                    x1bf[:, fo * 1024:fo * 1024 + 512],
                    start=(fo == 0), stop=False, skip_group_check=True))
                E(lambda fo=fo: eng.matmul(
                    pb[4][32:33, :], sb_ones.ap()[:, 0:1],
                    x1bf[:, fo * 1024 + 512:fo * 1024 + 1024],
                    start=False, stop=False, skip_group_check=True))
                E(lambda fo=fo: eng.matmul(
                    pb[4][64:65, :], sb_ones.ap()[:, 0:1],
                    sb_sq.ap()[:, 0:512],
                    start=False, stop=(fo == NFO - 1), skip_group_check=True))
                inc(E(lambda fo=fo: eng.matmul(
                    pb[5][0:1, :], sb_ones.ap()[:, 0:1],
                    sb_sq.ap()[:, 512:1024],
                    start=(fo == 0), stop=(fo == NFO - 1))),
                    "PE", f"st2_{fo}", 1)
            wait(eng, "VE", "rows2")
            E(lambda: eng.matmul(pb[0][:, :], sb_ones.ap()[0:1, 0:128],
                                 sb_rbf.ap()[0:1, 0:512], start=True, stop=True))
            E(lambda: eng.matmul(pb[1][:, :], sb_ones.ap()[0:1, 0:128],
                                 sb_rbf.ap()[0:1, 512:1024],
                                 start=True, stop=True))
            E(lambda: eng.matmul(pb[2][:, :], sb_ones.ap()[0:1, 0:128],
                                 sb_rbf.ap()[0:1, 1024:1536],
                                 start=True, stop=True))
            inc(E(lambda: eng.matmul(pb[3][:, :], sb_ones.ap()[0:1, 0:128],
                                     sb_rbf.ap()[0:1, 1536:2048],
                                     start=True, stop=True)),
                "PE", "bc2", 1)
            if stage < 4:
                return
            for fo in range(NFO):
                wait(eng, "VE", f"y2_{fo}")
                E(lambda fo=fo: eng.matmul(
                    pb[4][0:16, :], sb_c1.ap()[:, fo * 16:(fo + 1) * 16],
                    y2r[fo % 2][:, 0:512],
                    start=(fo == 0), stop=(fo == NFO - 1)))
                i2 = E(lambda fo=fo: eng.matmul(
                    pb[5][0:16, :], sb_c1.ap()[:, fo * 16:(fo + 1) * 16],
                    y2r[fo % 2][:, 512:1024],
                    start=(fo == 0), stop=(fo == NFO - 1)))
                if fo == NFO - 1:
                    inc(i2, "PE", "u", 1)
                for i in range(8):
                    gidx = fo * 8 + i
                    if gidx >= 2:
                        pp = gidx - 2
                        wait(eng, "SC", f"tre_{pp // 8}_{pp % 8}")
                    inc(E(lambda fo=fo, i=i: eng.transpose(
                        pt[i % 2][:, :],
                        y2r[fo % 2][:, i * 128:(i + 1) * 128],
                        sb_id.ap())), "PE", f"tr_{fo}_{i}", 1)
            if stage < 5:
                return
            def pe_int(g2):
                t2, j2 = g2 // NB, g2 % NB
                wait(eng, ("LA", "LB", "LD")[g2 % 3], f"c2_{t2}_{j2}")
                if g2 == 0:
                    wait(eng, "VE", "ue")
                if g2 >= 3:
                    p2, q2 = (g2 - 3) // NB, (g2 - 3) % NB
                    wait(eng, "SC", f"ie_{p2}_{q2}")
                inc(E(lambda t2=t2, j2=j2, g2=g2: eng.matmul(
                    pb[2 + g2 % 3][:, 0:256],
                    sb_u.ap()[0:16, t2 * 128:(t2 + 1) * 128],
                    c2b[g2 % 3][:, :],
                    start=True, stop=True)),
                    "PE", f"int_{t2}_{j2}", 1)

            def pe_ft(tt):
                wait(eng, "VE", f"o_{tt}_{NB-1}")
                for fo in range(NFO):
                    gidx = tt * NFO + fo
                    if gidx >= 2:
                        pp = gidx - 2
                        wait(eng, "VE", f"fa_{pp // NFO}_{pp % NFO}")
                    inc(E(lambda fo=fo: eng.transpose(
                        pt[fo % 2][:, :], acc_bf[0][:, fo * 128:(fo + 1) * 128],
                        sb_id.ap())), "PE", f"ft_{tt}_{fo}", 1)

            for g in range(NTT * NB):
                pe_int(g)
                if g % NB == 3 and g >= NB:
                    pe_ft(g // NB - 1)
            pe_ft(NTT - 1)

        def stream_scalar(eng):
            E, inc, wait = ctx.E, ctx.inc, ctx.wait
            pass
            for tc in range(4):
                wait(eng, "LC", f"b_xc{tc}")
                for kt in range(NFO):
                    if kt >= 2:
                        wait(eng, "PE", f"sx_{tc}_{kt-2}")
                    elif tc >= 1:
                        wait(eng, "PE", f"sx_{tc-1}_{NFO-2+kt}")
                    inc(E(lambda kt=kt, tc=tc: eng.activation(
                        sb_sq.ap()[:, (kt % 2) * 512:(kt % 2) * 512 + 512],
                        xchunk[:, kt * 512:(kt + 1) * 512], AF.Square)),
                        "SC", f"sq_{tc}_{kt}", 1)
                wait(eng, "VE", f"var_{tc}")
                inc(E(lambda: eng.activation(
                    rows_kv[0:1, 1024:1536], rows_kv[0:1, 1536:2048],
                    AF.Sqrt)), "SC", f"srt_{tc}", 1)
                for i in range(4):
                    wait(eng, "PE", f"vt_{tc}_{i}")
                    inc(E(lambda tc=tc, i=i: eng.copy(
                        sb_vt.ap()[:, (tc * 4 + i) * 128:(tc * 4 + i + 1) * 128],
                        pt[i % 2][:, :])), "SC", f"vt_{tc}_{i}", 1)
            if stage < 2:
                if ctx.emit:
                    eng.wait_ge(ctx.sems["SD"], 0)
                return
            wait(eng, "LC", "b_pre")
            PSA3 = [pb[0], pb[1], pb[5]]
            last_pt = {}
            for qc in (0, 1):
                slots = SLOTSETS[qc]
                for h in range(NH):
                    ng = n_groups(h)
                    w = 512 // ng
                    for si in range(len(slots)):
                        wait(eng, "PE", f"sc_{h}_{qc}_{si}")
                        prv = last_pt.get(si % 3)
                        if prv is not None:
                            wait(eng, "PE", f"av_{prv}")
                        last_pt[si % 3] = f"{h}_{qc}_{si}"
                        psA = PSA3[si % 3]
                        for g in range(ng):
                            col = COLOF[(h, qc, si, g)]
                            i1 = E(lambda si=si, g=g, w=w, col=col, psA=psA:
                                   eng.activation(
                                       P3[si % 3][:, g * w:(g + 1) * w],
                                       psA[:, g * w:(g + 1) * w],
                                       AF.Exp,
                                       bias=sb_bias.ap()[:, col:col + 1],
                                       scale=SCALE))
                            if g == ng - 1:
                                inc(i1, "SC", f"exp_{h}_{qc}_{si}", 1)
                    wait(eng, "PE", f"rb_{h}_{qc}")
                    inc(E(lambda: eng.copy(sb_rbc[:, :], pb[4][:, :])),
                        "SC", f"rbe_{h}_{qc}", 1)
            if stage < 3:
                return
            for fo in range(NFO):
                wait(eng, "VE", f"x1_{fo}")
                inc(E(lambda fo=fo: eng.dma_start(out=d_x1s[fo],
                                                  in_=sb_x1o)),
                    "SD", f"x1s_{fo}", 16)
                if fo >= 1:
                    wait(eng, "PE", f"st2_{fo-1}")
                inc(E(lambda fo=fo: eng.activation(
                    sb_sq.ap()[:, :], x1bf[:, fo * 1024:(fo + 1) * 1024],
                    AF.Square)), "SC", f"sq2_{fo}", 1)
            wait(eng, "VE", "var2")
            E(lambda: eng.activation(
                sb_x1o[0:1, 0:512], rows_ln2[0:1, 1024:1536],
                AF.Sqrt))
            inc(E(lambda: eng.activation(
                sb_x1o[0:1, 512:1024], rows_ln2[0:1, 1536:2048],
                AF.Sqrt)), "SC", "srt2", 1)
            wait(eng, "PE", "bc2")
            E(lambda: eng.copy(sb_mbc.ap()[:, 0:512], pb[0][:, :]))
            E(lambda: eng.copy(sb_mbc.ap()[:, 512:1024], pb[1][:, :]))
            E(lambda: eng.copy(sb_mbc.ap()[:, 1024:1536], pb[2][:, :]))
            inc(E(lambda: eng.copy(sb_mbc.ap()[:, 1536:2048], pb[3][:, :])),
                "SC", "bc2e", 1)
            if stage < 4:
                return
            for fo in range(NFO):
                for i in range(8):
                    wait(eng, "PE", f"tr_{fo}_{i}")
                    inc(E(lambda fo=fo, i=i: eng.copy(
                        y2tm[:, i * 4096 + fo * 128: i * 4096 + (fo + 1) * 128],
                        pt[i % 2][:, :])), "SC", f"tre_{fo}_{i}", 1)
            if stage >= 5:
                NG_ALL = NTT * NB

                def sc_ie(g2):
                    t2, j2 = g2 // NB, g2 % NB
                    wait(eng, "PE", f"int_{t2}_{j2}")
                    if g2 >= 3:
                        p2, q2 = (g2 - 3) // NB, (g2 - 3) % NB
                        wait(eng, "VE", f"am_{p2}_{q2}")
                    inc(E(lambda g2=g2: eng.copy(
                        ie_dst[g2 % 3][:, 0:256],
                        pb[2 + g2 % 3][:, 0:256])),
                        "SC", f"ie_{t2}_{j2}", 1)

                sc_ie(0)
                sc_ie(1)
                sc_ie(2)
                for g in range(NG_ALL):
                    tt, j = g // NB, g % NB
                    if g + 3 < NG_ALL:
                        sc_ie(g + 3)
                    wait(eng, "VE", f"pr_{tt}_{j}")
                    inc(E(lambda j=j, g=g: eng.activation(
                        sc_scr[:, :], prod[g % 2][:, :], AF.Copy,
                        accum_out=sb_h.ap()[:, j:j + 1])),
                        "SC", f"ha_{tt}_{j}", 1)
                    wait(eng, "SC", f"ha_{tt}_{j}")
                    inc(E(lambda j=j: eng.activation(
                        sb_hr.ap()[:, j:j + 1], sb_h.ap()[:, j:j + 1],
                        AF.Relu)), "SC", f"hrl_{tt}_{j}", 1)
                    if j == 2 and tt >= 1:
                        wait(eng, "VE", f"fa_{tt-1}_31")
                        inc(E(lambda tt=tt: eng.dma_start(out=d_y[tt - 1],
                                                          in_=otile)),
                            "SD", f"y_{tt-1}", 16)
                wait(eng, "VE", f"fa_{NTT-1}_31")
                inc(E(lambda: eng.dma_start(out=d_y[NTT - 1], in_=otile)),
                    "SD", f"y_{NTT-1}", 16)
                if ctx.emit:
                    eng.wait_ge(ctx.sems["SD"], ctx.ev[("SD", f"y_{NTT-1}")])
            else:
                inc(E(lambda: eng.dma_start(out=d_y[0][:, 0:NCOL],
                                            in_=sb_bias.ap())),
                    "SD", "y_stub", 16)
                if ctx.emit:
                    eng.wait_ge(ctx.sems["SD"], ctx.ev[("SD", "y_stub")])

        def stream_vector(eng):
            E, inc, wait = ctx.E, ctx.inc, ctx.wait
            pass
            for tc in range(4):
                wait(eng, "PE", f"sx_{tc}_{NFO-1}")
                E(lambda: eng.tensor_scalar(rows_kv[0:1, 0:512], pb[0][0:1, :],
                                            1.0 / D, None, ALU.mult))
                E(lambda: eng.tensor_scalar(rows_kv[0:1, 512:1024], pb[1][0:1, :],
                                            1.0 / D, None, ALU.mult))
                E(lambda: eng.tensor_tensor(rows_kv[0:1, 1536:2048],
                                            rows_kv[0:1, 0:512], rows_kv[0:1, 0:512],
                                            ALU.mult))
                E(lambda: eng.tensor_tensor(rows_kv[0:1, 1024:1536],
                                            rows_kv[0:1, 512:1024],
                                            rows_kv[0:1, 1536:2048],
                                            ALU.subtract))
                inc(E(lambda: eng.tensor_scalar(rows_kv[0:1, 1536:2048],
                                                rows_kv[0:1, 1024:1536],
                                                LN_EPS, None, ALU.add)),
                    "VE", f"var_{tc}", 1)
                wait(eng, "SC", f"srt_{tc}")
                E(lambda: eng.tensor_copy(sb_rbf.ap()[0:1, 0:512],
                                          rows_kv[0:1, 0:512]))
                E(lambda: eng.reciprocal(rows_kv[0:1, 0:512],
                                         rows_kv[0:1, 1024:1536]))
                inc(E(lambda: eng.tensor_copy(sb_rbf.ap()[0:1, 512:1024],
                                              rows_kv[0:1, 0:512])),
                    "VE", f"rows_{tc}", 1)
                wait(eng, "PE", f"bc_{tc}")
                E(lambda: eng.tensor_copy(sb_mbc.ap()[:, 0:512], pb[2][:, :]))
                inc(E(lambda: eng.tensor_copy(sb_mbc.ap()[:, 512:1024],
                                              pb[3][:, :])),
                    "VE", f"mbc_{tc}", 1)
                for kt in range(NFO):
                    if tc < 2:
                        dst = y1fm[:, kt * 1024 + tc * 512:
                                   kt * 1024 + tc * 512 + 512]
                    else:
                        dst = sb_ykv[:, (kt % 2) * 512:(kt % 2) * 512 + 512]
                        if kt >= 2:
                            wait(eng, "PE", f"kv_{tc}_{kt-2}")
                    E(lambda kt=kt: eng.tensor_tensor(
                        sb_nrm_kv[:, (kt % 2) * 512:(kt % 2) * 512 + 512],
                        xchunk[:, kt * 512:(kt + 1) * 512],
                        sb_mbc.ap()[:, 0:512], ALU.subtract))
                    inc(E(lambda kt=kt, dst=dst: eng.tensor_tensor(
                        dst,
                        sb_nrm_kv[:, (kt % 2) * 512:(kt % 2) * 512 + 512],
                        sb_mbc.ap()[:, 512:1024], ALU.mult)),
                        "VE", f"y_{tc}_{kt}", 1)
                wait(eng, "PE", f"kv_{tc}_{NFO-1}")
                E(lambda tc=tc: eng.tensor_copy(
                    sb_kf.ap()[:, tc * 512:(tc + 1) * 512], pb[4][:, :]))
                inc(E(lambda: eng.tensor_copy(sb_vfm[:, :], pb[5][:, :])),
                    "VE", f"kve_{tc}", 1)
            if stage < 1:
                return
            for m in range(NFO):
                wait(eng, "PE", f"q_{m}")
                E(lambda m=m: eng.tensor_copy(
                    qT[:, m * 1024:m * 1024 + 512], pb[(m % 2) * 2][:, :]))
                inc(E(lambda m=m: eng.tensor_copy(
                    qT[:, m * 1024 + 512:m * 1024 + 1024],
                    pb[(m % 2) * 2 + 1][:, :])), "VE", f"qe_{m}", 1)
            if stage < 2:
                return
            for qc in (0, 1):
                nsl = len(SLOTSETS[qc])
                for h in range(NH):
                    wait(eng, "PE", f"av_{h}_{qc}_{nsl-1}")
                    E(lambda: eng.reciprocal(sb_rrow.ap()[0:1, :],
                                             pb[3][0:1, :]))
                    inc(E(lambda: eng.tensor_copy(sb_rbf.ap()[0:1, 0:512],
                                                  sb_rrow.ap()[0:1, :])),
                        "VE", f"r_{h}_{qc}", 1)
                    wait(eng, "SC", f"rbe_{h}_{qc}")
                    inc(E(lambda h=h, qc=qc: eng.tensor_tensor(
                        attnout[:, q_off(h, qc):q_off(h, qc) + 512],
                        pb[2][:, :], sb_rbc[:, :], ALU.mult)),
                        "VE", f"ao_{h}_{qc}", 1)
            if stage < 3:
                return
            for fo in range(NFO):
                wait(eng, "PE", f"wo_{fo}")
                wait(eng, "LA" if fo % 2 == 0 else "LB", f"xf_{fo}")
                if fo >= 1:
                    wait(eng, "SD", f"x1s_{fo-1}")
                E(lambda fo=fo: eng.tensor_tensor(
                    sb_x1o[:, 0:512], pb[(fo % 2) * 2][:, :],
                    sb_x1f[:, 0:512], ALU.add))
                E(lambda fo=fo: eng.tensor_tensor(
                    sb_x1o[:, 512:1024], pb[(fo % 2) * 2 + 1][:, :],
                    sb_x1f[:, 512:1024], ALU.add))
                inc(E(lambda fo=fo: eng.tensor_copy(
                    x1bf[:, fo * 1024:(fo + 1) * 1024], sb_x1o[:, :])),
                    "VE", f"x1_{fo}", 1)
            wait(eng, "PE", f"st2_{NFO-1}")
            wait(eng, "SD", f"x1s_{NFO-1}")
            x1fr = sb_x1f[0:1, :]
            x1or = sb_x1o[0:1, :]
            E(lambda: eng.tensor_scalar(rows_ln2[0:1, 0:512], pb[4][0:1, :],
                                        1.0 / D, None, ALU.mult))
            E(lambda: eng.tensor_scalar(rows_ln2[0:1, 512:1024], pb[4][32:33, :],
                                        1.0 / D, None, ALU.mult))
            E(lambda: eng.tensor_scalar(rows_ln2[0:1, 1024:1536], pb[4][64:65, :],
                                        1.0 / D, None, ALU.mult))
            E(lambda: eng.tensor_scalar(rows_ln2[0:1, 1536:2048], pb[5][0:1, :],
                                        1.0 / D, None, ALU.mult))
            E(lambda: eng.tensor_tensor(x1fr[:, 0:512],
                                        rows_ln2[0:1, 0:512], rows_ln2[0:1, 0:512],
                                        ALU.mult))
            E(lambda: eng.tensor_tensor(x1fr[:, 512:1024],
                                        rows_ln2[0:1, 512:1024],
                                        rows_ln2[0:1, 512:1024], ALU.mult))
            E(lambda: eng.tensor_tensor(x1or[:, 0:512],
                                        rows_ln2[0:1, 1024:1536],
                                        x1fr[:, 0:512], ALU.subtract))
            E(lambda: eng.tensor_tensor(x1or[:, 512:1024],
                                        rows_ln2[0:1, 1536:2048],
                                        x1fr[:, 512:1024],
                                        ALU.subtract))
            E(lambda: eng.tensor_scalar(rows_ln2[0:1, 1024:1536],
                                        x1or[:, 0:512],
                                        LN_EPS, None, ALU.add))
            inc(E(lambda: eng.tensor_scalar(rows_ln2[0:1, 1536:2048],
                                            x1or[:, 512:1024],
                                            LN_EPS, None, ALU.add)),
                "VE", "var2", 1)
            wait(eng, "SC", "srt2")
            E(lambda: eng.reciprocal(x1fr[:, 0:512], x1or[:, 0:512]))
            E(lambda: eng.reciprocal(x1fr[:, 512:1024], x1or[:, 512:1024]))
            E(lambda: eng.tensor_copy(sb_rbf.ap()[0:1, 0:1024],
                                      rows_ln2[0:1, 0:1024]))
            inc(E(lambda: eng.tensor_copy(sb_rbf.ap()[0:1, 1024:2048],
                                          x1fr[:, 0:1024])),
                "VE", "rows2", 1)
            if stage < 4:
                return
            wait(eng, "SC", "bc2e")
            for fo in range(NFO):
                if fo >= 2:
                    wait(eng, "PE", f"tr_{fo-2}_7")
                E(lambda fo=fo: eng.tensor_tensor(
                    sb_nrm_t[:, :], x1bf[:, fo * 1024:(fo + 1) * 1024],
                    sb_mbc.ap()[:, 0:1024], ALU.subtract))
                inc(E(lambda fo=fo: eng.tensor_tensor(
                    y2r[fo % 2][:, :], sb_nrm_t[:, :],
                    sb_mbc.ap()[:, 1024:2048], ALU.mult)),
                    "VE", f"y2_{fo}", 1)
            wait(eng, "PE", "u")
            E(lambda: eng.tensor_copy(sb_u.ap()[0:16, 0:512], pb[4][0:16, :]))
            inc(E(lambda: eng.tensor_copy(sb_u.ap()[0:16, 512:1024],
                                          pb[5][0:16, :])), "VE", "ue", 1)
            if stage < 5:
                return
            NG_ALL = NTT * NB

            def ve_argmax(g2):
                t2, j2 = g2 // NB, g2 % NB
                wait(eng, "SC", f"ie_{t2}_{j2}")
                if g2 == 0:
                    wait(eng, "LC", "b_iota")
                inc(E(lambda g2=g2: eng.reduce_max(
                    sb_m8.ap()[:, 0:1], ie_dst[g2 % 3][:, 0:256],
                    axis=mybir.AxisListType.X)),
                    "VE", f"amx_{t2}_{j2}", 1)
                wait(eng, "VE", f"amx_{t2}_{j2}")
                E(lambda g2=g2: eng.tensor_scalar(
                    sb_pt1.ap()[:, 256:512], ie_dst[g2 % 3][:, 0:256],
                    sb_m8.ap()[:, 0:1], None, ALU.is_ge))
                E(lambda: eng.tensor_tensor(
                    sb_sq.ap()[:, 0:256], sb_pt1.ap()[:, 256:512],
                    sb_pt0.ap()[:, 256:512], ALU.mult))
                inc(E(lambda: eng.reduce_max(
                    sb_m8.ap()[:, 1:2], sb_sq.ap()[:, 0:256],
                    axis=mybir.AxisListType.X)),
                    "VE", f"amr_{t2}_{j2}", 1)
                wait(eng, "VE", f"amr_{t2}_{j2}")
                inc(E(lambda g2=g2: eng.tensor_copy(
                    sb_i8.ap()[:, g2 % 4:g2 % 4 + 1],
                    sb_m8.ap()[:, 1:2])),
                    "VE", f"am_{t2}_{j2}", 1)

            def ve_add(p):
                pt_, pj = p // NB, p % NB
                wait(eng, "SC", f"hrl_{pt_}_{pj}")
                wait(eng, "GA", f"g2_{pt_}_{pj}")
                inc(E(lambda pt_=pt_, pj=pj: eng.tensor_scalar(
                    w2sc[:, :], w2buf[pj % 2][:, :],
                    sb_hr.ap()[:, pj:pj + 1], None, ALU.mult)),
                    "VE", f"ws_{pt_}_{pj}", 1)
                if pj == 0:
                    if pt_ >= 1:
                        wait(eng, "PE", f"ft_{pt_-1}_{NFO-1}")
                    inc(E(lambda: eng.tensor_copy(
                        acc_bf[1][:, :], w2sc[:, :])),
                        "VE", f"o_{pt_}_0", 1)
                else:
                    if pj == 1 and pt_ >= 1:
                        wait(eng, "PE", f"ft_{pt_-1}_{NFO-1}")
                    inc(E(lambda pj=pj: eng.tensor_tensor(
                        acc_bf[(pj + 1) % 2][:, :], w2sc[:, :],
                        acc_bf[pj % 2][:, :], ALU.add)),
                        "VE", f"o_{pt_}_{pj}", 1)
                if pj == NB - 1:
                    for fo in range(NFO):
                        wait(eng, "PE", f"ft_{pt_}_{fo}")
                        wait(eng, "LC", f"b_x1c{pt_}")
                        if pt_ >= 1 and fo == 0:
                            wait(eng, "SD", f"y_{pt_-1}")
                        inc(E(lambda pt_=pt_, fo=fo: eng.tensor_tensor(
                            otile[:, fo * 128:(fo + 1) * 128],
                            pt[fo % 2][:, :],
                            x1c[:, fo * 128:(fo + 1) * 128], ALU.add)),
                            "VE", f"fa_{pt_}_{fo}", 1)

            ve_argmax(0)
            ve_argmax(1)
            ve_argmax(2)
            for g in range(NG_ALL):
                tt, j = g // NB, g % NB
                la = g + 3 < NG_ALL
                if la:
                    g3 = g + 3
                    t2, j2 = g3 // NB, g3 % NB
                    wait(eng, "SC", f"ie_{t2}_{j2}")
                    inc(E(lambda g3=g3: eng.reduce_max(
                        sb_m8.ap()[:, 0:1], ie_dst[g3 % 3][:, 0:256],
                        axis=mybir.AxisListType.X)),
                        "VE", f"amx_{t2}_{j2}", 1)
                wait(eng, "GA", f"g_{tt}_{j}")
                if g >= 2:
                    p2, q2 = (g - 2) // NB, (g - 2) % NB
                    wait(eng, "SC", f"ha_{p2}_{q2}")
                inc(E(lambda tt=tt, j=j, g=g: eng.tensor_tensor(
                    prod[g % 2][:, :], w1buf[j % 2][:, :],
                    y2tm[:, tt * 4096:(tt + 1) * 4096], ALU.mult)),
                    "VE", f"pr_{tt}_{j}", 1)
                if la:
                    wait(eng, "VE", f"amx_{t2}_{j2}")
                    E(lambda g3=g3: eng.tensor_scalar(
                        sb_pt1.ap()[:, 256:512], ie_dst[g3 % 3][:, 0:256],
                        sb_m8.ap()[:, 0:1], None, ALU.is_ge))
                    E(lambda: eng.tensor_tensor(
                        sb_sq.ap()[:, 0:256], sb_pt1.ap()[:, 256:512],
                        sb_pt0.ap()[:, 256:512], ALU.mult))
                    inc(E(lambda: eng.reduce_max(
                        sb_m8.ap()[:, 1:2], sb_sq.ap()[:, 0:256],
                        axis=mybir.AxisListType.X)),
                        "VE", f"amr_{t2}_{j2}", 1)
                if g >= 1:
                    ve_add(g - 1)
                if la:
                    wait(eng, "VE", f"amr_{t2}_{j2}")
                    inc(E(lambda g3=g3: eng.tensor_copy(
                        sb_i8.ap()[:, g3 % 4:g3 % 4 + 1],
                        sb_m8.ap()[:, 1:2])),
                        "VE", f"am_{t2}_{j2}", 1)
            ve_add(NG_ALL - 1)

        ctx.emit = False
        stream_sync(None)
        stream_gpsimd(None)
        stream_tensor(None)
        stream_scalar(None)
        stream_vector(None)

        ctx.emit = True

        @block.sync
        def _(eng):
            stream_sync(eng)

        @block.gpsimd
        def _(eng):
            stream_gpsimd(eng)

        @block.tensor
        def _(eng):
            stream_tensor(eng)

        @block.scalar
        def _(eng):
            stream_scalar(eng)

        @block.vector
        def _(eng):
            stream_vector(eng)

    return nc


def _bf(x):
    return np.ascontiguousarray(x).astype(ml_dtypes.bfloat16)


def kernel(L0_attn_hidden_states, L0_attn_Wq, L0_attn_Wkv, L0_attn_Wo,
           L0_ffn_c1_w, L0_ffn_c2_w, L0_ffn_w1, L0_ffn_w2t, _debug=False,
           _stage=99, _fsub=99):
    X = np.asarray(L0_attn_hidden_states, dtype=np.float32)
    Wq = np.asarray(L0_attn_Wq, dtype=np.float32)
    Wkv = np.asarray(L0_attn_Wkv, dtype=np.float32)
    Wo = np.asarray(L0_attn_Wo, dtype=np.float32)
    c1 = np.asarray(L0_ffn_c1_w, dtype=np.float32)
    c2 = np.asarray(L0_ffn_c2_w, dtype=np.float32)
    w1 = np.asarray(L0_ffn_w1, dtype=np.float32)
    w2t = np.asarray(L0_ffn_w2t, dtype=np.float32)

    wq_t = _bf(Wq.reshape(NFO, 128, NFO, 128).transpose(2, 1, 0, 3)
               .reshape(NFO, 128, D))
    wo_t = _bf(Wo.reshape(NFO, 128, NFO, 128).transpose(2, 1, 0, 3)
               .reshape(NFO, 128, D))
    wkv_t = _bf(Wkv.reshape(NFO, 128, 256).transpose(1, 0, 2).reshape(128, 8192))
    c1t = _bf(c1.T.reshape(NFO, 128, 16).transpose(1, 0, 2).reshape(128, 512))
    c2t = _bf(c2.T)
    w1w2 = np.concatenate([_bf(w1), _bf(w2t)], axis=1)
    mask4 = _host_masks()
    u1 = np.triu(np.ones((128, 128))).astype(ml_dtypes.bfloat16)
    ident = np.eye(128).astype(ml_dtypes.bfloat16)
    ones = np.ones((128, 128)).astype(ml_dtypes.bfloat16)
    iota = np.broadcast_to(np.arange(256, dtype=np.float32), (128, 256)) \
        .astype(ml_dtypes.bfloat16)

    nc = build_program(debug=_debug, stage=_stage, fsub=_fsub)
    in_maps = []
    for c in range(8):
        perm, _, _ = _core_geom(c)
        b = c // 2
        xp = X[b][perm]
        in_maps.append({
            "xfmt": _bf(xp.T.reshape(NFO, 128, S)),
            "xf32": np.ascontiguousarray(
                xp[:NTOK].T.reshape(NFO, 128, NTOK)).astype(np.float32),
            "wq_t": wq_t, "wo_t": wo_t, "wkv_t": wkv_t,
            "c1t": c1t, "c2t": c2t, "w1w2": w1w2,
            "biasm": _host_biasmat(c), "mask4": mask4,
            "u1": u1, "ident": ident, "onesb": ones, "iota": iota,
        })

    res = run_bass_kernel_spmd(nc, in_maps, core_ids=list(range(8)))

    out = np.empty((B, S, D), dtype=np.float32)
    for c in range(8):
        perm, _, _ = _core_geom(c)
        b = c // 2
        yc = np.asarray(res.results[c]["y"])
        toks = yc.reshape(NTT, 128, NFO, 128).transpose(0, 3, 2, 1) \
                 .reshape(NTOK, D)
        out[b][perm[:NTOK]] = toks
    if _debug:
        return out, res
    return out
